# version 02
# baseline (speedup 1.0000x reference)
import numpy as np
import ml_dtypes
import concourse.bass as bass
import concourse.mybir as mybir
from concourse.bass_utils import run_bass_kernel_spmd

F32 = mybir.dt.float32
BF16 = mybir.dt.bfloat16
AF = mybir.ActivationFunctionType
ALU = mybir.AluOpType
AX = mybir.AxisListType

BF = ml_dtypes.bfloat16
NCORES = 8
T = 2048
D = 1024
NCH = D // 128

ENGINES = ("t", "v", "s", "p", "y")


class Buf:
    __slots__ = ("ap", "name", "wtok", "rtoks", "_addr", "_size")

    def __init__(self, ap, name=""):
        self.ap = ap
        self.name = name
        self.wtok = None
        self.rtoks = []
        self._addr = None
        self._size = None


class Prog:
    def __init__(self, name="prog"):
        self.nc = bass.Bass("TRN2", target_bir_lowering=False, debug=False)
        self.name = name
        self.ops = {e: [] for e in ENGINES}
        self._ctxs = []

    def emit(self, eng, fn, reads=(), writes=(), kind="c", after=()):
        waits = set()

        def add(tok):
            if tok is None:
                return
            peng, pkind, _ = tok
            if peng == eng and pkind == "d" and kind == "d":
                return
            waits.add(tok)

        for b in reads:
            add(b.wtok)
        for b in writes:
            add(b.wtok)
            for t in b.rtoks:
                add(t)
        for t in after:
            add(t)
        seq = len(self.ops[eng])
        tok = (eng, kind, seq)
        self.ops[eng].append({"fn": fn, "kind": kind, "waits": waits, "inc": False})
        for b in writes:
            b.wtok = tok
            b.rtoks = []
        wset = set(id(b) for b in writes)
        for b in reads:
            if id(b) not in wset:
                b.rtoks.append(tok)
        return tok

    def dma(self, queue, dst_ap, src_ap, reads=(), writes=(), after=()):
        return self.emit(queue, lambda E: E.dma_start(dst_ap, src_ap),
                         reads=reads, writes=writes, kind="d", after=after)

    NDMASEM = 8
    NCSEM = 4

    def build(self, final_waits=()):
        nc = self.nc
        needed = set()
        for e in ENGINES:
            for op in self.ops[e]:
                needed.update(op["waits"])
        needed.update(final_waits)
        for (eng, kind, seq) in needed:
            self.ops[eng][seq]["inc"] = True
        for e in ENGINES:
            for op in self.ops[e]:
                if op["kind"] == "d":
                    op["inc"] = True
        semval = {}
        semslot = {}
        for e in ENGINES:
            cslot_count = [0] * self.NCSEM
            cix = 0
            dslot_count = [0] * self.NDMASEM
            dix = 0
            for i, op in enumerate(self.ops[e]):
                if op["inc"]:
                    if op["kind"] == "d":
                        s = dix % self.NDMASEM
                        dix += 1
                        dslot_count[s] += 16
                        semval[(e, "d", i)] = dslot_count[s]
                        semslot[(e, "d", i)] = s
                    else:
                        s = cix % self.NCSEM
                        cix += 1
                        cslot_count[s] += 1
                        semval[(e, "c", i)] = cslot_count[s]
                        semslot[(e, "c", i)] = s
        sems = {}
        for e in ENGINES:
            for s in range(self.NCSEM):
                sems[(e, "c", s)] = self._enter(nc.semaphore(f"sem_{e}c{s}"))
            for s in range(self.NDMASEM):
                sems[(e, "d", s)] = self._enter(nc.semaphore(f"sem_{e}d{s}"))
        block = self._enter(nc.Block())
        water = {}

        def sem_of(tok):
            pe, pk, _ = tok
            return (pe, pk, semslot.get(tok, 0))

        def run_engine(e, E):
            for i, op in enumerate(self.ops[e]):
                for tok in sorted(op["waits"]):
                    v = semval[tok]
                    sk = sem_of(tok)
                    key = (e,) + sk
                    if water.get(key, 0) >= v:
                        continue
                    water[key] = v
                    E.wait_ge(sems[sk], v)
                if op["kind"] == "d":
                    tok = (e, "d", i)
                    sk = sem_of(tok)
                    v = semval[tok] - 16
                    key = (e,) + sk
                    if v > 0 and water.get(key, 0) < v:
                        water[key] = v
                        E.wait_ge(sems[sk], v)
                inst = op["fn"](E)
                if op["inc"]:
                    k = op["kind"]
                    sk = sem_of((e, k, i))
                    inst.then_inc(sems[sk], 16 if k == "d" else 1)
            if e == "y":
                for tok in sorted(final_waits):
                    v = semval[tok]
                    sk = sem_of(tok)
                    key = (e,) + sk
                    if water.get(key, 0) < v:
                        water[key] = v
                        E.wait_ge(sems[sk], v)

        @block.tensor
        def _(E):
            run_engine("t", E)

        @block.vector
        def _(E):
            run_engine("v", E)

        @block.scalar
        def _(E):
            run_engine("s", E)

        @block.gpsimd
        def _(E):
            run_engine("p", E)

        @block.sync
        def _(E):
            run_engine("y", E)
        return nc

    def _enter(self, cm):
        v = cm.__enter__()
        self._ctxs.append(cm)
        return v

    def make_arena(self, words=52224):
        self.arena_t = self._enter(self.nc.sbuf_tensor("arena", [128, words], F32))
        self.freelist = [(0, words)]
        self.region_hist = []
        self.scopes = []
        self.banks = [Buf(self._enter(self.nc.psum_tensor(f"bank{i}", [128, 512], F32))[:, :],
                          f"bank{i}") for i in range(8)]
        return self

    def alloc(self, cols, dtype=F32, name=""):
        w = cols if dtype == F32 else (cols + 1) // 2
        w = (w + 3) & ~3
        for i, (a, sz) in enumerate(self.freelist):
            if sz >= w:
                if sz == w:
                    self.freelist.pop(i)
                else:
                    self.freelist[i] = (a + w, sz - w)
                region = self.arena_t[:, a:a + w]
                ap = region[:, :cols] if dtype == F32 else region.bitcast(BF16)[:, :cols]
                b = Buf(ap, name)
                b._addr, b._size = a, w
                keep = []
                for (ha, hs, toks) in self.region_hist:
                    if ha < a + w and a < ha + hs:
                        b.rtoks.extend(toks)
                        if ha < a:
                            keep.append((ha, a - ha, toks))
                        if ha + hs > a + w:
                            keep.append((a + w, ha + hs - (a + w), toks))
                    else:
                        keep.append((ha, hs, toks))
                self.region_hist = keep
                if self.scopes:
                    self.scopes[-1].append(b)
                return b
        raise AssertionError(f"arena OOM: {name} needs {w} words; free={self.freelist}")

    def push_scope(self):
        self.scopes.append([])

    def pop_scope(self):
        for b in self.scopes.pop():
            if b._addr is not None:
                self.free(b)

    def free(self, *bufs):
        for b in bufs:
            assert b._addr is not None, f"double free {b.name}"
            toks = list(b.rtoks) + ([b.wtok] if b.wtok else [])
            self.region_hist.append((b._addr, b._size, toks))
            self.freelist.append((b._addr, b._size))
            b._addr = None
            self.freelist.sort()
            merged = []
            for a, sz in self.freelist:
                if merged and merged[-1][0] + merged[-1][1] == a:
                    merged[-1] = (merged[-1][0], merged[-1][1] + sz)
                else:
                    merged.append((a, sz))
            self.freelist = [tuple(m) for m in merged]

    def dram(self, name, shape, dtype=F32, kind="ExternalInput"):
        return self.nc.dram_tensor(name, list(shape), dtype, kind=kind).ap()



def act(P, out_buf, out_ap, in_bufs, in_ap, func, bias=None, scale=None,
        accum_ap=None, accum_buf=None, extra_reads=()):
    kw = {}
    if bias is not None:
        kw["bias"] = bias
    if scale is not None:
        kw["scale"] = scale
    if accum_ap is not None:
        kw["accum_out"] = accum_ap
    wr = [out_buf] + ([accum_buf] if accum_buf is not None else [])
    return P.emit("s", lambda E: E.activation(out_ap, in_ap, func, **kw),
                  reads=list(in_bufs) + list(extra_reads), writes=wr)


def load_weight(P, dram_ap, nch_in, dout, name, queue="y", slot=None, stage=None,
                cast_eng="v", col0=0):
    if slot is None:
        wb = P.alloc(nch_in * dout, BF16, name)
        w3 = wb.ap.rearrange("p (c d) -> p c d", c=nch_in)
    else:
        wb = slot
        flat = slot.ap
        if len(flat.shape) == 3:
            flat = flat.rearrange("p c d -> p (c d)")
        assert flat.shape[1] >= nch_in * dout, f"wslot too small for {name}"
        w3 = flat[:, :nch_in * dout].rearrange("p (c d) -> p c d", c=nch_in)
    own_stage = stage is None
    if own_stage:
        stage = [P.alloc(dout, F32, name + "_stg0"), P.alloc(dout, F32, name + "_stg1")]
    for c in range(nch_in):
        sg = stage[c % 2]
        P.dma(queue, sg.ap[:, :dout], dram_ap[c * 128:(c + 1) * 128, col0:col0 + dout],
              writes=[sg])
        P.emit(cast_eng, lambda E, c=c, sg=sg: E.tensor_copy(w3[:, c, :], sg.ap[:, :dout]),
               reads=[sg], writes=[wb])
    if own_stage:
        P.free(*stage)
    wb.ap = w3
    return wb


def load_vec(P, dram_ap, name, n=NCH):
    v = P.alloc(n, F32, name)
    P.dma("p", v.ap, dram_ap, writes=[v])
    return v


class VecPack:
    NCOLS = 256

    def __init__(self, ncols=None):
        self.NCOLS = ncols or VecPack.NCOLS
        self.host = np.zeros((128, self.NCOLS), np.float32)
        self.off = 0
        self.slices = {}

    def add_cols(self, name, arr):
        arr = np.asarray(arr, np.float32)
        assert arr.shape[0] == 128
        k = arr.shape[1]
        self.host[:, self.off:self.off + k] = arr
        self.slices[name] = ("c", self.off, k)
        self.off += k
        return self

    def add_row(self, name, arr):
        arr = np.asarray(arr, np.float32).reshape(-1)
        k = len(arr)
        self.host[0, self.off:self.off + k] = arr
        self.slices[name] = ("r", self.off, k)
        self.off += k
        return self

    def add_block(self, name, arr):
        arr = np.asarray(arr, np.float32)
        p, k = arr.shape
        self.host[0:p, self.off:self.off + k] = arr
        self.slices[name] = (("b", p), self.off, k)
        self.off += k
        return self

    def load(self, P, dram_ap):
        buf = P.alloc(self.NCOLS, F32, "vecpack")
        P.dma("y", buf.ap, dram_ap, writes=[buf])
        out = {}
        for name, (kind, off, k) in self.slices.items():
            if kind == "r":
                ap = buf.ap[0:1, off:off + k]
            elif isinstance(kind, tuple) and kind[0] == "b":
                ap = buf.ap[0:kind[1], off:off + k]
            else:
                ap = buf.ap[:, off:off + k]
            out[name] = _SharedView(ap, name, buf)
        return out


class _SharedView:
    __slots__ = ("ap", "name", "parent")

    def __init__(self, ap, name, parent):
        self.ap = ap
        self.name = name
        self.parent = parent

    @property
    def wtok(self):
        return self.parent.wtok

    @property
    def rtoks(self):
        return self.parent.rtoks


def load_row(P, dram_ap, cols, name, dtype=F32, queue="p"):
    v = P.alloc(cols, dtype, name)
    row = v.ap[0:1, :]
    P.dma(queue, row, dram_ap, writes=[v])
    v.ap = row
    return v


def proj(P, wb, rhs_fn, rhs_bufs, nch_in, dout_chunks, evac, blkw=512, nblk=None,
         bank_off=0, nbanks=None):
    nblk = nblk if nblk is not None else (T // blkw)
    banks = P.banks[bank_off:(bank_off + nbanks if nbanks else len(P.banks))]
    for dc in range(dout_chunks):
        for blk in range(nblk):
            bank = banks[(dc * nblk + blk) % len(banks)]
            for c in range(nch_in):
                P.emit("t", lambda E, c=c, dc=dc, blk=blk, bank=bank: E.matmul(
                    bank.ap[:, :blkw], wb.ap[:, c, dc * 128:(dc + 1) * 128],
                    rhs_fn(c, blk), start=(c == 0), stop=(c == nch_in - 1)),
                    reads=[wb] + list(rhs_bufs), writes=[bank])
            evac(dc, blk, bank)


def global_ln(P, x, xn, c_ones_col, c_ones_row):
    x3 = x.ap
    xn3 = xn.ap
    P.emit("v", lambda E: E.memset(xn3[:, :, 0:1], 0.0), writes=[xn])
    sq = P.alloc(T, F32, "ln_sq")
    rows1 = P.alloc(2 * T, F32, "ln_rows")
    rows2 = rows1
    A_bc = P.alloc(T, F32, "ln_Abc")
    B_bc = P.alloc(T, F32, "ln_Bbc")
    bsum = P.banks[:4]
    bsq = P.banks[4:8]
    for c in range(NCH):
        act(P, sq, sq.ap, [x], x3[:, c, :], AF.Square)
        for blk in range(4):
            sl = slice(blk * 512, (blk + 1) * 512)
            P.emit("t", lambda E, c=c, blk=blk, sl=sl: E.matmul(
                bsum[blk].ap[0:1, :], c_ones_col.ap, x3[:, c, sl],
                start=(c == 0), stop=(c == NCH - 1)), reads=[x, c_ones_col], writes=[bsum[blk]])
            P.emit("t", lambda E, c=c, blk=blk, sl=sl: E.matmul(
                bsq[blk].ap[0:1, :], c_ones_col.ap, sq.ap[:, sl],
                start=(c == 0), stop=(c == NCH - 1)), reads=[sq, c_ones_col], writes=[bsq[blk]])
    mean = rows1.ap[0:1, 0 * T:1 * T]
    msq = rows1.ap[0:1, 1 * T:2 * T]
    sd = msq
    A = msq
    for blk in range(4):
        sl = slice(blk * 512, (blk + 1) * 512)
        P.emit("v", lambda E, blk=blk, sl=sl: E.tensor_scalar(
            mean[:, sl], bsum[blk].ap[0:1, :], 1.0 / D, None, ALU.mult, ALU.bypass),
            reads=[bsum[blk]], writes=[rows1])
        P.emit("v", lambda E, blk=blk, sl=sl: E.tensor_scalar(
            msq[:, sl], bsq[blk].ap[0:1, :], 1.0 / D, 1e-5, ALU.mult, ALU.add),
            reads=[bsq[blk]], writes=[rows1])
    m2row = sq.ap[0:1, 0:T]
    P.emit("v", lambda E: E.tensor_tensor(m2row, mean, mean, ALU.mult), reads=[rows1], writes=[sq])
    P.emit("v", lambda E: E.tensor_tensor(msq, msq, m2row, ALU.subtract),
           reads=[rows1, sq], writes=[rows1])
    P.emit("s", lambda E: E.activation(sd, msq, AF.Sqrt), reads=[rows1], writes=[rows1])
    P.emit("v", lambda E: E.reciprocal(A, sd), reads=[rows1], writes=[rows1])
    P.emit("v", lambda E: E.scalar_tensor_tensor(mean, mean, -1.0, A, ALU.mult, ALU.mult),
           reads=[rows1], writes=[rows1])
    for (src_buf, srow, dst) in ((rows2, A, A_bc), (rows1, mean, B_bc)):
        for blk in range(4):
            sl = slice(blk * 512, (blk + 1) * 512)
            bank = P.banks[blk]
            P.emit("t", lambda E, srow=srow, sl=sl, bank=bank: E.matmul(
                bank.ap[:, :], c_ones_row.ap, srow[:, sl], start=True, stop=True),
                reads=[src_buf, c_ones_row], writes=[bank])
            P.emit("v", lambda E, dst=dst, sl=sl, bank=bank: E.tensor_copy(
                dst.ap[:, sl], bank.ap), reads=[bank], writes=[dst])
    for c in range(NCH):
        P.emit("v", lambda E, c=c: E.tensor_tensor(sq.ap, x3[:, c, :], A_bc.ap, ALU.mult),
               reads=[x, A_bc], writes=[sq])
        P.emit("v", lambda E, c=c: E.tensor_tensor(xn3[:, c, 1:T + 1], sq.ap, B_bc.ap, ALU.add),
               reads=[sq, B_bc], writes=[xn])
    P.free(sq, rows1, A_bc, B_bc)


def alloc_xn(P):
    xn = P.alloc(NCH * (T + 1) + 1, BF16, "xn")
    xn.ap = xn.ap[:, :NCH * (T + 1)].rearrange("p (c t) -> p c t", c=NCH)
    return xn


def shared_token_mix(P, xn, mu_cols_names):
    outs = []
    for mu, name in mu_cols_names:
        b = P.alloc(NCH * T, BF16, name)
        b.ap = b.ap.rearrange("p (c t) -> p c t", c=NCH)
        outs.append(b)
    d = P.alloc(T, F32, "mix_d")
    for c in range(NCH):
        cur = xn.ap[:, c, 1:T + 1]
        prev = xn.ap[:, c, 0:T]
        P.emit("v", lambda E, cur=cur, prev=prev: E.tensor_tensor(d.ap, cur, prev, ALU.subtract),
               reads=[xn], writes=[d])
        for (mu, name), ob in zip(mu_cols_names, outs):
            P.emit("v", lambda E, c=c, prev=prev, mu=mu, ob=ob: E.scalar_tensor_tensor(
                ob.ap[:, c, :], d.ap, mu.ap[:, c:c + 1], prev, ALU.mult, ALU.add),
                reads=[d, xn, mu], writes=[ob])
    P.free(d)
    return outs


def fused_update_evac(P, x_d, y_d, g3, xst, yst, sgn=1.0):
    tmp = P.alloc(512, F32, "upd_tmp")
    state = {"dc": -1}

    def evac(dc, blk, bank):
        if state["dc"] != dc:
            state["dc"] = dc
            xb = xst[dc % 2]
            P.dma("y", xb.ap, x_d[dc * 128:(dc + 1) * 128, :], writes=[xb])
        xb = xst[dc % 2]
        yb = yst[dc % 2]
        sl = slice(blk * 512, (blk + 1) * 512)
        P.emit("v", lambda E: E.tensor_tensor(tmp.ap, bank.ap, g3.ap[:, dc, sl], ALU.mult),
               reads=[bank, g3], writes=[tmp])
        P.emit("v", lambda E: E.tensor_tensor(yb.ap[:, sl], tmp.ap, xb.ap[:, sl], ALU.add),
               reads=[tmp, xb], writes=[yb])
        if blk == 3:
            P.dma("y", y_d[dc * 128:(dc + 1) * 128, :], yb.ap, reads=[yb])
    return evac



def l1_pack(ins=None):
    vp = VecPack()
    z8 = np.zeros((128, NCH), np.float32)
    get = (lambda k: _col8(k)) if ins is not None else (lambda k: z8)
    if ins is not None:
        vp.add_cols("v_td", _col8(ins["L1_attn_time_decay"]))
        vp.add_cols("v_tf", _col8(ins["L1_attn_time_first"]))
        vp.add_cols("v_mk", _col8(ins["L1_attn_mix_k"].reshape(-1)))
        vp.add_cols("v_mv", _col8(ins["L1_attn_mix_v"].reshape(-1)))
        vp.add_cols("v_mr", _col8(ins["L1_attn_mix_r"].reshape(-1)))
        vp.add_cols("v_mur", _col8(ins["L1_ffn_mu_r"]))
        vp.add_cols("v_muk", _col8(ins["L1_ffn_mu_k"]))
    else:
        for n in ["v_td", "v_tf", "v_mk", "v_mv", "v_mr", "v_mur", "v_muk"]:
            vp.add_cols(n, z8)
    vp.add_cols("ones_col", np.ones((128, 1), np.float32))
    vp.add_row("ones_row", np.ones(128, np.float32))
    return vp


def build_l1(P=None, x_d=None, y_d=None, pfx="", dump=None):
    if P is None:
        P = Prog("l1").make_arena()
        x_d = P.dram("x_in", [D, T])
        y_d = P.dram("y_out", [D, T], kind="ExternalOutput")
    P.push_scope()
    W = {n: P.dram(pfx + n, [D, D]) for n in ["wkT", "wvT", "wrT", "woT", "frT", "fkT", "fvT"]}
    vecs_d = P.dram(pfx + "vecs", [128, VecPack.NCOLS])
    vec = l1_pack().load(P, vecs_d)
    c_ones_col = vec["ones_col"]
    c_ones_row = vec["ones_row"]

    def dump_vp(tag):
        parent = vec["v_td"].parent
        outb = P.alloc(64, F32, "dumpout")
        P.emit("v", lambda E: E.tensor_copy(outb.ap, parent.ap[:, 0:64]), reads=[parent], writes=[outb])
        P.dma("y", y_d[0:128, 0:64], outb.ap, reads=[outb])
        return P, [("y", "d", len(P.ops["y"]) - 1)]

    def dump_out(sb, guard=0):
        outb = P.alloc(T, F32, "dumpout")
        for c in range(NCH):
            sl = sb.ap[:, c, guard:T + guard] if guard else sb.ap[:, c, :]
            P.emit("v", lambda E, sl=sl: E.tensor_copy(outb.ap, sl), reads=[sb], writes=[outb])
            P.dma("y", y_d[c * 128:(c + 1) * 128, :], outb.ap, reads=[outb])
        return P, [("y", "d", len(P.ops["y"]) - 1)]


    xn = alloc_xn(P)
    g3 = P.alloc(NCH * T, BF16, "g")
    g3.ap = g3.ap.rearrange("p (c t) -> p c t", c=NCH)
    wslot = P.alloc(NCH * D, BF16, "wslot")

    x = P.alloc(NCH * T, F32, "x")
    x.ap = x.ap.rearrange("p (c t) -> p c t", c=NCH)
    P.dma("y", x.ap, x_d.rearrange("(c p) t -> p c t", p=128), writes=[x])
    global_ln(P, x, xn, c_ones_col, c_ones_row)
    P.free(x)

    xr, xk = shared_token_mix(P, xn, [(vec["v_mur"], "xr"), (vec["v_muk"], "xk")])
    wbuf = load_weight(P, W["frT"], NCH, D, "w_fr", slot=wslot)
    rs = P.alloc(NCH * T, BF16, "rs")
    rs.ap = rs.ap.rearrange("p (c t) -> p c t", c=NCH)

    def evac_sigmoid(dc, blk, bank):
        sl = slice(blk * 512, (blk + 1) * 512)
        act(P, rs, rs.ap[:, dc, sl], [bank], bank.ap, AF.Sigmoid)
    proj(P, wbuf, lambda c, blk: xr.ap[:, c, blk * 512:(blk + 1) * 512], [xr],
         NCH, NCH, evac_sigmoid)
    P.free(xr)

    wbuf = load_weight(P, W["fkT"], NCH, D, "w_fk", slot=wslot)
    ka = P.alloc(NCH * T, BF16, "ka")
    ka.ap = ka.ap.rearrange("p (c t) -> p c t", c=NCH)
    relu_t = P.alloc(512, F32, "relu_t")

    def evac_relu_sq(dc, blk, bank):
        sl = slice(blk * 512, (blk + 1) * 512)
        act(P, relu_t, relu_t.ap, [bank], bank.ap, AF.Relu)
        P.emit("v", lambda E: E.tensor_tensor(ka.ap[:, dc, sl], relu_t.ap, relu_t.ap, ALU.mult),
               reads=[relu_t], writes=[ka])
    proj(P, wbuf, lambda c, blk: xk.ap[:, c, blk * 512:(blk + 1) * 512], [xk],
         NCH, NCH, evac_relu_sq)
    P.free(xk, relu_t)

    wbuf = load_weight(P, W["fvT"], NCH, D, "w_fv", slot=wslot)
    wv_t = P.alloc(512, F32, "wv_t")

    def evac_g(dc, blk, bank):
        sl = slice(blk * 512, (blk + 1) * 512)
        P.emit("v", lambda E: E.tensor_tensor(wv_t.ap, bank.ap, rs.ap[:, dc, sl], ALU.mult),
               reads=[bank, rs], writes=[wv_t])
        act(P, g3, g3.ap[:, dc, sl], [wv_t], wv_t.ap, AF.Silu)
    proj(P, wbuf, lambda c, blk: ka.ap[:, c, blk * 512:(blk + 1) * 512], [ka],
         NCH, NCH, evac_g)
    P.free(ka, rs, wv_t)

    if dump == "g":
        return dump_out(g3)
    kin, vin, rin = shared_token_mix(P, xn, [(vec["v_mk"], "kin"), (vec["v_mv"], "vin"),
                                             (vec["v_mr"], "rin")])
    if dump == "kin":
        return dump_out(kin)
    P.free(xn)
    wbuf = load_weight(P, W["wrT"], NCH, D, "w_r", slot=wslot)
    rs2 = P.alloc(NCH * T, BF16, "rs2")
    rs2.ap = rs2.ap.rearrange("p (c t) -> p c t", c=NCH)

    def evac_sig2(dc, blk, bank):
        sl = slice(blk * 512, (blk + 1) * 512)
        act(P, rs2, rs2.ap[:, dc, sl], [bank], bank.ap, AF.Sigmoid)
    proj(P, wbuf, lambda c, blk: rin.ap[:, c, blk * 512:(blk + 1) * 512], [rin],
         NCH, NCH, evac_sig2)
    if dump == "rsig":
        return dump_out(rs2)
    P.free(rin)
    wbuf = load_weight(P, W["wvT"], NCH, D, "w_v", slot=wslot)
    vsb = P.alloc(NCH * T, BF16, "vsb")
    vsb.ap = vsb.ap.rearrange("p (c t) -> p c t", c=NCH)

    def evac_copy(dc, blk, bank):
        sl = slice(blk * 512, (blk + 1) * 512)
        P.emit("v", lambda E: E.tensor_copy(vsb.ap[:, dc, sl], bank.ap), reads=[bank],
               writes=[vsb])
    proj(P, wbuf, lambda c, blk: vin.ap[:, c, blk * 512:(blk + 1) * 512], [vin],
         NCH, NCH, evac_copy)
    if dump == "vsb":
        return dump_out(vsb)
    P.free(vin)

    a_col = P.alloc(NCH, F32, "a_col")
    eu_col = P.alloc(NCH, F32, "eu_col")
    act(P, a_col, a_col.ap, [vec["v_td"]], vec["v_td"].ap, AF.Exp)
    act(P, a_col, a_col.ap, [a_col], a_col.ap, AF.Exp, scale=-1.0)
    act(P, eu_col, eu_col.ap, [vec["v_tf"]], vec["v_tf"].ap, AF.Exp)

    wbuf = load_weight(P, W["wkT"], NCH, D, "w_k", slot=wslot)
    ebuf = P.alloc(T, F32, "ebuf")
    enbuf = P.alloc(T, F32, "enbuf")
    num = P.alloc(T + 4, F32, "num")
    den = P.alloc(T + 4, F32, "den")
    P.emit("v", lambda E: E.memset(num.ap[:, 0:1], 0.0), writes=[num])
    P.emit("v", lambda E: E.memset(den.ap[:, 0:1], 0.0), writes=[den])

    ndc = NCH
    if dump and dump.startswith("ndc"):
        ndc = int(dump[3:])
    for dc in range(ndc):
        for blk in range(4):
            bank = P.banks[blk]
            sl = slice(blk * 512, (blk + 1) * 512)
            for c in range(NCH):
                P.emit("t", lambda E, c=c, dc=dc, bank=bank, sl=sl: E.matmul(
                    bank.ap, wbuf.ap[:, c, dc * 128:(dc + 1) * 128], kin.ap[:, c, sl],
                    start=(c == 0), stop=(c == NCH - 1)), reads=[wbuf, kin], writes=[bank])
            act(P, ebuf, ebuf.ap[:, sl], [bank], bank.ap, AF.Exp)
        P.emit("v", lambda E, dc=dc: E.tensor_tensor(enbuf.ap, ebuf.ap, vsb.ap[:, dc, :],
                                                     ALU.mult), reads=[ebuf, vsb], writes=[enbuf])
        a_b = a_col.ap[:, dc:dc + 1].broadcast_to([128, T])
        P.emit("v", lambda E, a_b=a_b: E.tensor_tensor_scan(
            num.ap[:, 1:T + 1], a_b, enbuf.ap, 0.0, ALU.mult, ALU.add),
            reads=[a_col, enbuf], writes=[num])
        P.emit("v", lambda E, a_b=a_b: E.tensor_tensor_scan(
            den.ap[:, 1:T + 1], a_b, ebuf.ap, 0.0, ALU.mult, ALU.add),
            reads=[a_col, ebuf], writes=[den])
        P.emit("v", lambda E, dc=dc: E.scalar_tensor_tensor(
            enbuf.ap, enbuf.ap, eu_col.ap[:, dc:dc + 1], num.ap[:, 0:T], ALU.mult, ALU.add),
            reads=[enbuf, eu_col, num], writes=[enbuf])
        P.emit("v", lambda E, dc=dc: E.scalar_tensor_tensor(
            ebuf.ap, ebuf.ap, eu_col.ap[:, dc:dc + 1], den.ap[:, 0:T], ALU.mult, ALU.add),
            reads=[ebuf, eu_col, den], writes=[ebuf])
        P.emit("v", lambda E: E.reciprocal(ebuf.ap, ebuf.ap), reads=[ebuf], writes=[ebuf])
        P.emit("v", lambda E: E.tensor_tensor(enbuf.ap, enbuf.ap, ebuf.ap, ALU.mult),
               reads=[enbuf, ebuf], writes=[enbuf])
        P.emit("v", lambda E, dc=dc: E.tensor_tensor(rs2.ap[:, dc, :], rs2.ap[:, dc, :],
                                                     enbuf.ap, ALU.mult),
               reads=[rs2, enbuf], writes=[rs2])
    P.free(ebuf, enbuf, num, den, kin, vsb)

    if dump == "rwkv" or (dump and dump.startswith("ndc")):
        return dump_out(rs2)
    wbuf = load_weight(P, W["woT"], NCH, D, "w_o", slot=wslot)
    xst = [P.alloc(T, F32, f"xst{i}") for i in range(2)]
    yst = [P.alloc(T, F32, f"yst{i}") for i in range(2)]
    evac = fused_update_evac(P, x_d, y_d, g3, xst, yst)
    out_toks = []

    def evac_upd(dc, blk, bank):
        evac(dc, blk, bank)
    proj(P, wbuf, lambda c, blk: rs2.ap[:, c, blk * 512:(blk + 1) * 512], [rs2],
         NCH, NCH, evac_upd)
    last_y = ("y", "d", len(P.ops["y"]) - 1)
    P.pop_scope()
    return P, [last_y]




def l3_pack(ins=None):
    vp = VecPack(512)
    if ins is not None:
        vp.add_cols("qn", np.tile(np.asarray(ins["L3_attn_qn_w"], np.float32), 2)[:, None])
        vp.add_cols("kn", np.tile(np.asarray(ins["L3_attn_kn_w"], np.float32), 2)[:, None])
        vp.add_cols("b_proj", _col8(ins["L3_attn_b_proj"]))
        vp.add_cols("b_qkv", _f32(np.asarray(ins["L3_attn_b_qkv"]).reshape(24, 128).T))
        vp.add_cols("w1s", _f32(np.asarray(ins["L3_ffn_w1_s"]).reshape(32, 128).T))
        vp.add_cols("w2s", _f32(np.asarray(ins["L3_ffn_w2_s"]).reshape(32, 128).T))
        vp.add_cols("h1s", _f32(np.asarray(ins["L3_ffn_h1_s"]).reshape(32, 128).T))
        vp.add_cols("h2s", _f32(np.asarray(ins["L3_ffn_h2_s"]).reshape(32, 128).T))
        vp.add_cols("w3s", _col8(ins["L3_ffn_w3_s"]))
    else:
        z = np.zeros((128, 1), np.float32)
        vp.add_cols("qn", z); vp.add_cols("kn", z)
        vp.add_cols("b_proj", np.zeros((128, 8), np.float32))
        vp.add_cols("b_qkv", np.zeros((128, 24), np.float32))
        for n in ("w1s", "w2s", "h1s", "h2s"):
            vp.add_cols(n, np.zeros((128, 32), np.float32))
        vp.add_cols("w3s", np.zeros((128, 8), np.float32))
    vp.add_cols("ones_col", np.ones((128, 1), np.float32))
    vp.add_row("ones_row", np.ones(128, np.float32))
    b2 = np.zeros((2, 128), np.float32)
    b2[0, 0:64] = 1
    b2[1, 64:128] = 1
    vp.add_block("B2", b2)
    return vp


def _bf_consts():
    c = np.zeros((128, 192), BF)
    c[:, 0:128] = np.eye(128, dtype=BF)
    c[0:64, 128] = 1
    c[64:128, 129] = 1
    return c


def build_l3(P=None, x_d=None, y_d=None, pfx="", dump=None):
    if P is None:
        P = Prog("l3").make_arena()
        x_d = P.dram("x_in", [D, T])
        y_d = P.dram("y_out", [D, T], kind="ExternalOutput")
    P.push_scope()
    s_d = P.dram(pfx + "s_spill", [D, T], kind="Internal")
    wqkv_d = P.dram(pfx + "wqkvT", [D, 3 * D])
    wproj_d = P.dram(pfx + "wprojT", [D, D])
    w1_d = P.dram(pfx + "w1T", [D, 4096])
    w2_d = P.dram(pfx + "w2T", [D, 4096])
    h1_d = P.dram(pfx + "h1T", [4096, 4096])
    h2_d = P.dram(pfx + "h2T", [4096, 4096])
    w3_d = P.dram(pfx + "w3T", [4096, D])
    vecs_d = P.dram(pfx + "vecs", [128, 512])
    cbf_d = P.dram(pfx + "cbf", [128, 192], BF16)
    cid_d = P.dram(pfx + "cid32", [128, 128])

    vec = l3_pack().load(P, vecs_d)
    c_ones_col = vec["ones_col"]
    c_ones_row = vec["ones_row"]
    cbf = P.alloc(192, BF16, "cbf")
    P.dma("y", cbf.ap, cbf_d, writes=[cbf])
    cid = P.alloc(128, F32, "cid32")
    P.dma("y", cid.ap, cid_d, writes=[cid])
    ident = cbf.ap[:, 0:128]
    E2 = cbf.ap[:, 128:130]
    obf = P.alloc(128, BF16, "ones_row_bf")
    P.emit("v", lambda E: E.memset(obf.ap[0:1, :], 1.0), writes=[obf])

    def dump_fm(sb3, nch=NCH, guard=0):
        outb = P.alloc(T, F32, "dumpout")
        for c in range(nch):
            sl = sb3.ap[:, c, guard:T + guard]
            P.emit("v", lambda E, sl=sl: E.tensor_copy(outb.ap, sl), reads=[sb3], writes=[outb])
            P.dma("y", y_d[(c % NCH) * 128:((c % NCH) + 1) * 128, :], outb.ap, reads=[outb])
        return P, [("y", "d", len(P.ops["y"]) - 1)]

    xn = alloc_xn(P)
    x = P.alloc(NCH * T, F32, "x")
    x.ap = x.ap.rearrange("p (c t) -> p c t", c=NCH)
    P.dma("y", x.ap, x_d.rearrange("(c p) t -> p c t", p=128), writes=[x])
    global_ln(P, x, xn, c_ones_col, c_ones_row)
    P.free(x)

    qkv = P.alloc(24 * T, BF16, "qkv")
    qkv.ap = qkv.ap.rearrange("p (c t) -> p c t", c=24)
    wslot = P.alloc(NCH * D, BF16, "wslot")
    for j in range(3):
        wb = load_weight(P, wqkv_d, NCH, D, "wqkv", slot=wslot, col0=j * D)

        def evac_qkv(dc, blk, bank, j=j):
            gdc = j * NCH + dc
            sl = slice(blk * 512, (blk + 1) * 512)
            P.emit("v", lambda E, gdc=gdc, sl=sl, bank=bank: E.tensor_scalar(
                qkv.ap[:, gdc, sl], bank.ap, vec["b_qkv"].ap[:, gdc:gdc + 1], None,
                ALU.add, ALU.bypass), reads=[bank, vec["b_qkv"]], writes=[qkv])
        proj(P, wb, lambda c, blk: xn.ap[:, c, 1 + blk * 512:1 + (blk + 1) * 512], [xn],
             NCH, NCH, evac_qkv)
    P.free(wslot)
    if dump == "qkv":
        return dump_fm(qkv, 24)

    sqb = P.alloc(T, BF16, "rms_sq")
    B2 = vec["B2"]
    rows2h = P.alloc(T, F32, "rms_rows")
    rbc = P.alloc(512, F32, "rms_rbc")
    tmul = P.alloc(512, F32, "rms_tmul")

    for kind in ("q", "k"):
        wcol = vec["qn" if kind == "q" else "kn"]
        for c in range(NCH):
            ch = c if kind == "q" else NCH + c
            for blk in range(4):
                sl = slice(blk * 512, (blk + 1) * 512)
                act(P, sqb, sqb.ap[:, sl], [qkv], qkv.ap[:, ch, sl], AF.Square)
                P.emit("t", lambda E, sl=sl, blk=blk: E.matmul(
                    P.banks[blk].ap[0:2, :], E2, sqb.ap[:, sl], start=True, stop=True),
                    reads=[sqb, cbf], writes=[P.banks[blk]])
            for blk in range(4):
                sl = slice(blk * 512, (blk + 1) * 512)
                P.emit("v", lambda E, sl=sl, blk=blk: E.tensor_scalar(
                    rows2h.ap[0:2, sl], P.banks[blk].ap[0:2, :], 1.0 / 64, 1e-6,
                    ALU.mult, ALU.add), reads=[P.banks[blk]], writes=[rows2h])
            P.emit("s", lambda E: E.activation(rows2h.ap[0:2, :], rows2h.ap[0:2, :],
                   AF.Sqrt), reads=[rows2h], writes=[rows2h])
            P.emit("v", lambda E: E.reciprocal(rows2h.ap[0:2, :], rows2h.ap[0:2, :]),
                   reads=[rows2h], writes=[rows2h])
            for blk in range(4):
                sl = slice(blk * 512, (blk + 1) * 512)
                bank = P.banks[4 + blk % 2]
                P.emit("t", lambda E, sl=sl, bank=bank: E.matmul(
                    bank.ap, B2.ap[0:2, :], rows2h.ap[0:2, sl], start=True, stop=True),
                    reads=[rows2h, B2], writes=[bank])
                P.emit("v", lambda E, ch=ch, sl=sl, bank=bank: E.tensor_tensor(
                    tmul.ap, qkv.ap[:, ch, sl], bank.ap, ALU.mult),
                    reads=[qkv, bank], writes=[tmul])
                if kind == "q":
                    P.emit("v", lambda E, ch=ch, sl=sl: E.tensor_scalar(
                        qkv.ap[:, ch, sl], tmul.ap, wcol.ap, 0.125, ALU.mult, ALU.mult),
                        reads=[tmul, wcol], writes=[qkv])
                else:
                    P.emit("v", lambda E, ch=ch, sl=sl: E.tensor_scalar(
                        qkv.ap[:, ch, sl], tmul.ap, wcol.ap, None, ALU.mult, ALU.bypass),
                        reads=[tmul, wcol], writes=[qkv])
    P.free(sqb, rows2h, rbc, tmul)
    if dump == "qkvn":
        return dump_fm(qkv, 24)

    vtm = P.alloc(16 * 16 * 64, BF16, "vtm")
    vtm.ap = vtm.ap.rearrange("p (h k e) -> p h k e", h=16, k=16)
    for h in range(16):
        base = (h % 2) * 64
        vch = 16 + h // 2
        bank = P.banks[h % 2]
        bankb = bank.ap.bitcast(BF16)
        for kt in range(16):
            vsl = qkv.ap[base:base + 64, vch, kt * 128:(kt + 1) * 128]
            idn = ident[base:base + 64, base:base + 64]
            P.emit("t", lambda E, vsl=vsl, idn=idn, kt=kt, bankb=bankb: E.transpose(
                bankb[:, kt * 64:(kt + 1) * 64], vsl, idn), reads=[qkv, cbf], writes=[bank])
        P.emit("v", lambda E, h=h, bankb=bankb: E.tensor_copy(
            vtm.ap[:, h, :, :], bankb.rearrange("p (k e) -> p k e", k=16)),
            reads=[bank], writes=[vtm])

    negmb = P.alloc(8, F32, "negm_cols")
    nrow = P.alloc(2 * 2048, BF16, "negm_row")
    rrow = P.alloc(2 * T, F32, "recip_rows")
    dcol = P.alloc(8, F32, "den_cols")
    scrap = P.alloc(512, F32, "exp_scrap")
    ptb = P.alloc(T, BF16, "pt_sb")
    rbc2 = P.alloc(1024, F32, "o_rbc")

    for hp in range(8):
        for par in range(2):
            h = 2 * hp + par
            base = par * 64
            qch = h // 2
            kch = NCH + h // 2
            Q = qkv.ap[base:base + 64, qch, :]
            K = qkv.ap[base:base + 64, kch, :]
            nrow_h = nrow.ap[0:1, par * 2048:par * 2048 + 2048]
            rrow_h = rrow.ap[0:1, par * T:par * T + T]
            for qt in range(16):
                bank = P.banks[qt % 2]
                for kblk in range(4):
                    ksl = slice(kblk * 512, (kblk + 1) * 512)
                    P.emit("t", lambda E, qt=qt, ksl=ksl, bank=bank, Q=Q, K=K: E.matmul(
                        bank.ap, Q[:, qt * 128:(qt + 1) * 128], K[:, ksl],
                        start=True, stop=True), reads=[qkv], writes=[bank])
                    P.emit("v", lambda E, kblk=kblk, bank=bank: E.tensor_reduce(
                        negmb.ap[:, kblk:kblk + 1], bank.ap, axis=AX.X, op=ALU.max,
                        negate=True), reads=[bank], writes=[negmb])
                P.emit("v", lambda E: E.tensor_reduce(negmb.ap[:, 4:5],
                       negmb.ap[:, 0:4], axis=AX.X, op=ALU.min), reads=[negmb], writes=[negmb])
                P.emit("v", lambda E: E.tensor_copy(negmb.ap[:, 5:6].bitcast(BF16)[:, 0:1],
                       negmb.ap[:, 4:5]), reads=[negmb], writes=[negmb])
                P.emit("v", lambda E: E.tensor_copy(negmb.ap[:, 6:7],
                       negmb.ap[:, 5:6].bitcast(BF16)[:, 0:1]), reads=[negmb], writes=[negmb])
                for kblk in range(4):
                    ksl = slice(kblk * 512, (kblk + 1) * 512)
                    bank = P.banks[2 + kblk % 2]
                    P.emit("t", lambda E, qt=qt, ksl=ksl, bank=bank, Q=Q, K=K: E.matmul(
                        bank.ap, Q[:, qt * 128:(qt + 1) * 128], K[:, ksl],
                        start=True, stop=True), reads=[qkv], writes=[bank])
                    act(P, scrap, scrap.ap, [bank, negmb], bank.ap, AF.Exp,
                        bias=negmb.ap[:, 6:7], accum_ap=dcol.ap[:, kblk:kblk + 1],
                        accum_buf=dcol)
                P.emit("v", lambda E: E.tensor_reduce(dcol.ap[:, 4:5], dcol.ap[:, 0:4],
                       axis=AX.X, op=ALU.add), reads=[dcol], writes=[dcol])
                P.emit("v", lambda E: E.reciprocal(dcol.ap[:, 5:6], dcol.ap[:, 4:5]),
                       reads=[dcol], writes=[dcol])
                P.emit("t", lambda E: E.transpose(P.banks[6].ap[0:1, 0:128],
                       negmb.ap[:, 6:7], cid.ap), reads=[negmb, cid], writes=[P.banks[6]])
                P.emit("v", lambda E, qt=qt, nrow_h=nrow_h: E.tensor_copy(
                    nrow_h[0:1, qt * 128:(qt + 1) * 128],
                    P.banks[6].ap[0:1, 0:128]), reads=[P.banks[6]], writes=[nrow])
                P.emit("t", lambda E: E.transpose(P.banks[7].ap[0:1, 0:128],
                       dcol.ap[:, 5:6], cid.ap), reads=[dcol, cid], writes=[P.banks[7]])
                P.emit("v", lambda E, qt=qt, rrow_h=rrow_h: E.tensor_copy(
                    rrow_h[0:1, qt * 128:(qt + 1) * 128], P.banks[7].ap[0:1, 0:128]),
                    reads=[P.banks[7]], writes=[rrow])
            for kt in range(16):
                for qblk in range(4):
                    qsl = slice(qblk * 512, (qblk + 1) * 512)
                    bank = P.banks[qblk % 2]
                    P.emit("t", lambda E, kt=kt, qsl=qsl, bank=bank, Q=Q, K=K: E.matmul(
                        bank.ap, K[:, kt * 128:(kt + 1) * 128], Q[:, qsl],
                        start=True, stop=False), reads=[qkv], writes=[bank])
                    P.emit("t", lambda E, qsl=qsl, bank=bank, nrow_h=nrow_h: E.matmul(
                        bank.ap, obf.ap[0:1, :], nrow_h[0:1, qsl], start=False, stop=True),
                        reads=[obf, nrow], writes=[bank])
                    act(P, ptb, ptb.ap[:, qsl], [bank], bank.ap, AF.Exp)
                for qblk in range(4):
                    qsl = slice(qblk * 512, (qblk + 1) * 512)
                    obank = P.banks[4 + qblk]
                    P.emit("t", lambda E, kt=kt, qsl=qsl, obank=obank, h=h, par=par:
                           E.matmul(obank.ap[par * 64:par * 64 + 64, :],
                                    vtm.ap[:, h, kt, :], ptb.ap[:, qsl],
                                    start=(kt == 0), stop=(kt == 15),
                                    tile_position=(0, par * 64)),
                           reads=[vtm, ptb], writes=[obank])
        for qblk in range(4):
            qsl = slice(qblk * 512, (qblk + 1) * 512)
            obank = P.banks[4 + qblk]
            for par in range(2):
                rrow_h = rrow.ap[0:1, par * T + qblk * 512:par * T + (qblk + 1) * 512]
                bank = P.banks[par]
                P.emit("t", lambda E, rrow_h=rrow_h, bank=bank: E.matmul(
                    bank.ap, c_ones_row.ap, rrow_h, start=True, stop=True),
                    reads=[rrow, c_ones_row], writes=[bank])
                P.emit("v", lambda E, bank=bank: E.tensor_copy(rbc2.ap[:, 0:512], bank.ap),
                       reads=[bank], writes=[rbc2])
                P.emit("v", lambda E, hp=hp, par=par, qsl=qsl, obank=obank: E.tensor_tensor(
                    qkv.ap[par * 64:par * 64 + 64, 16 + hp, qsl],
                    obank.ap[par * 64:par * 64 + 64, :], rbc2.ap[par * 64:par * 64 + 64, 0:512],
                    ALU.mult), reads=[obank, rbc2], writes=[qkv])
    P.free(vtm, negmb, nrow, rrow, dcol, scrap, ptb, rbc2)
    if dump == "ofm":
        return dump_fm(qkv, 24)

    wslot = P.alloc(NCH * D, BF16, "wslot")
    wb = load_weight(P, wproj_d, NCH, D, "wproj", slot=wslot)
    sst = P.alloc(T, F32, "s_stage")

    def evac_s(dc, blk, bank):
        sl = slice(blk * 512, (blk + 1) * 512)
        P.emit("v", lambda E, dc=dc, sl=sl, bank=bank: E.tensor_scalar(
            sst.ap[:, sl], bank.ap, vec["b_proj"].ap[:, dc:dc + 1], None,
            ALU.add, ALU.bypass), reads=[bank, vec["b_proj"]], writes=[sst])
        if blk == 3:
            P.dma("y", s_d[dc * 128:(dc + 1) * 128, :], sst.ap, reads=[sst])
    proj(P, wb, lambda c, blk: qkv.ap[:, 16 + c, blk * 512:(blk + 1) * 512], [qkv],
         NCH, NCH, evac_s)
    P.free(qkv, wslot, sst)
    if dump == "sdone":
        tmpb = P.alloc(T, F32, "tmpb")
        for c in range(NCH):
            P.dma("y", tmpb.ap, s_d[c * 128:(c + 1) * 128, :], writes=[tmpb])
            P.dma("y", y_d[c * 128:(c + 1) * 128, :], tmpb.ap, reads=[tmpb])
        return P, [("y", "d", len(P.ops["y"]) - 1)]

    xn_d = P.dram(pfx + "xn_spill", [D, T], BF16, kind="Internal")
    for c in range(NCH):
        P.dma("y", xn_d[c * 128:(c + 1) * 128, :], xn.ap[:, c, 1:T + 1], reads=[xn])
    P.free(xn)

    HB = T // 2
    u1 = P.alloc(32 * HB, BF16, "u1")
    u1.ap = u1.ap.rearrange("p (c t) -> p c t", c=32)
    u2 = P.alloc(32 * HB, BF16, "u2")
    u2.ap = u2.ap.rearrange("p (c t) -> p c t", c=32)
    xnh = P.alloc(NCH * HB, BF16, "xnh")
    xnh.ap = xnh.ap.rearrange("p (c t) -> p c t", c=NCH)
    t1 = P.alloc(512, F32, "ffn_t1")
    t2 = P.alloc(512, F32, "ffn_t2")
    wslA = P.alloc(NCH * 512, BF16, "wslA")
    wslH = P.alloc(32 * 256, BF16, "wslH")
    stg = [P.alloc(512, F32, "ffn_stg0"), P.alloc(512, F32, "ffn_stg1")]
    gst = P.alloc(512, BF16, "g_stage")
    xst = P.alloc(HB, F32, "xst")
    sst2 = P.alloc(HB, F32, "sst2")
    yst = P.alloc(HB, F32, "yst")
    NBH = HB // 512

    for half in range(2):
        t0 = half * HB
        for c in range(NCH):
            P.dma("y", xnh.ap[:, c, :], xn_d[c * 128:(c + 1) * 128, t0:t0 + HB],
                  writes=[xnh])
        for mi, (mat_d, scol) in enumerate(((w1_d, "w1s"), (w2_d, "w2s"))):
            for ds in range(8):
                wA = load_weight(P, mat_d, NCH, 512, "wA", slot=wslA, stage=stg,
                                 col0=ds * 512)
                for dsub in range(4):
                    dc = ds * 4 + dsub
                    for blk in range(NBH):
                        usl = slice(blk * 512, (blk + 1) * 512)
                        bank = P.banks[(dc * NBH + blk) % 4]
                        for c in range(NCH):
                            P.emit("t", lambda E, c=c, dsub=dsub, usl=usl, bank=bank,
                                   wA=wA: E.matmul(
                                bank.ap, wA.ap[:, c, dsub * 128:(dsub + 1) * 128],
                                xnh.ap[:, c, usl], start=(c == 0), stop=(c == NCH - 1)),
                                reads=[wA, xnh], writes=[bank])
                        if mi == 0:
                            act(P, u1, u1.ap[:, dc, usl], [bank], bank.ap, AF.Silu,
                                scale=vec["w1s"].ap[:, dc:dc + 1])
                        else:
                            P.emit("v", lambda E, dc=dc, bank=bank: E.tensor_scalar(
                                t2.ap, bank.ap, vec["w2s"].ap[:, dc:dc + 1], None,
                                ALU.mult, ALU.bypass), reads=[bank, vec["w2s"]], writes=[t2])
                            P.emit("v", lambda E, dc=dc, usl=usl: E.tensor_tensor(
                                u1.ap[:, dc, usl], u1.ap[:, dc, usl], t2.ap, ALU.mult),
                                reads=[u1, t2], writes=[u1])
        for (mat_d, scol, src3, dst3) in ((h1_d, "h1s", u1, u2), (h2_d, "h2s", u2, u1)):
            for ds in range(16):
                wH = load_weight(P, mat_d, 32, 256, "wH", slot=wslH, stage=stg,
                                 col0=ds * 256)
                for dsub in range(2):
                    dc = ds * 2 + dsub
                    for blk in range(NBH):
                        usl = slice(blk * 512, (blk + 1) * 512)
                        bank = P.banks[(dc * NBH + blk) % 4]
                        for c in range(32):
                            P.emit("t", lambda E, c=c, dsub=dsub, usl=usl, bank=bank,
                                   wH=wH, src3=src3: E.matmul(
                                bank.ap, wH.ap[:, c, dsub * 128:(dsub + 1) * 128],
                                src3.ap[:, c, usl], start=(c == 0), stop=(c == 31)),
                                reads=[wH, src3], writes=[bank])
                        act(P, dst3, dst3.ap[:, dc, usl], [bank], bank.ap, AF.Gelu,
                            scale=vec[scol].ap[:, dc:dc + 1])
        for ds in range(4):
            wC = load_weight(P, w3_d, 32, 256, "w3", slot=wslH, stage=stg, col0=ds * 256)
            for dsub in range(2):
                dc = ds * 2 + dsub
                P.dma("y", xst.ap, x_d[dc * 128:(dc + 1) * 128, t0:t0 + HB], writes=[xst])
                P.dma("y", sst2.ap, s_d[dc * 128:(dc + 1) * 128, t0:t0 + HB], writes=[sst2])
                for blk in range(NBH):
                    usl = slice(blk * 512, (blk + 1) * 512)
                    bank = P.banks[(dc * NBH + blk) % 4]
                    for c in range(32):
                        P.emit("t", lambda E, c=c, dsub=dsub, usl=usl, bank=bank, wC=wC:
                               E.matmul(bank.ap, wC.ap[:, c, dsub * 128:(dsub + 1) * 128],
                                        u1.ap[:, c, usl], start=(c == 0), stop=(c == 31)),
                               reads=[wC, u1], writes=[bank])
                    act(P, gst, gst.ap[:, :], [bank], bank.ap, AF.Silu,
                        scale=vec["w3s"].ap[:, dc:dc + 1])
                    P.emit("v", lambda E, usl=usl: E.tensor_tensor(
                        t1.ap, gst.ap[:, :], sst2.ap[:, usl], ALU.mult),
                        reads=[gst, sst2], writes=[t1])
                    P.emit("v", lambda E, usl=usl: E.tensor_tensor(
                        yst.ap[:, usl], t1.ap, xst.ap[:, usl], ALU.add),
                        reads=[t1, xst], writes=[yst])
                P.dma("y", y_d[dc * 128:(dc + 1) * 128, t0:t0 + HB], yst.ap, reads=[yst])
    last_y = ("y", "d", len(P.ops["y"]) - 1)
    P.pop_scope()
    return P, [last_y]


def l3_inputs(core_x, ins, pfx=""):
    rd = {
        "x_in": _f32(core_x),
        "wqkvT": _f32(ins["L3_attn_w_qkv"].T),
        "wprojT": _f32(ins["L3_attn_w_proj"].T),
        "w1T": _f32(ins["L3_ffn_w1_w"].T),
        "w2T": _f32(ins["L3_ffn_w2_w"].T),
        "h1T": _f32(ins["L3_ffn_h1_w"].T),
        "h2T": _f32(ins["L3_ffn_h2_w"].T),
        "w3T": _f32(ins["L3_ffn_w3_w"].T),
        "vecs": l3_pack(ins).host,
        "cbf": _bf_consts(),
        "cid32": np.eye(128, dtype=np.float32),
    }
    return {("x_in" if k == "x_in" else pfx + k): v for k, v in rd.items()}




def l2_pack(ins=None):
    vp = VecPack(512)
    if ins is not None:
        vp.add_cols("gate_w", _f32(np.asarray(ins["L2_ffn_gate_w"]).reshape(NCH, 128, 8)
                                   .transpose(1, 0, 2).reshape(128, NCH * 8)))
        vp.add_cols("b_proj", _col8(ins["L2_attn_b_proj"]))
        vp.add_cols("snw", np.tile(np.asarray(ins["L2_attn_sub_norm_w"], np.float32), 2)[:, None])
        vp.add_row("lq1", ins["L2_attn_lambda_q1"])
        vp.add_row("lk1", ins["L2_attn_lambda_k1"])
        vp.add_row("lq2", ins["L2_attn_lambda_q2"])
        vp.add_row("lk2", ins["L2_attn_lambda_k2"])
    else:
        vp.add_cols("gate_w", np.zeros((128, NCH * 8), np.float32))
        vp.add_cols("b_proj", np.zeros((128, 8), np.float32))
        vp.add_cols("snw", np.zeros((128, 1), np.float32))
        for n in ("lq1", "lk1", "lq2", "lk2"):
            vp.add_row(n, np.zeros(32, np.float32))
    vp.add_cols("ones_col", np.ones((128, 1), np.float32))
    vp.add_row("ones_row", np.ones(128, np.float32))
    b2 = np.zeros((2, 128), np.float32)
    b2[0, 0:64] = 1
    b2[1, 64:128] = 1
    vp.add_block("B2", b2)
    return vp


def build_l2(P=None, x_d=None, y_d=None, pfx="", dump=None):
    if P is None:
        P = Prog("l2").make_arena()
        x_d = P.dram("x_in", [D, T])
        y_d = P.dram("y_out", [D, T], kind="ExternalOutput")
    P.push_scope()
    g_d = P.dram(pfx + "g_spill", [D, T], BF16, kind="Internal")
    wqkv_d = P.dram(pfx + "wqkvT4", [D, 4 * D])
    wproj_d = P.dram(pfx + "wprojT", [D, D])
    ew1_d = P.dram(pfx + "ew1", [8, D, 2048])
    ew2_d = P.dram(pfx + "ew2", [8, D, 2048])
    ew3_d = P.dram(pfx + "ew3", [8, 2048, D])
    sw1_d = P.dram(pfx + "sw1", [D, 2048])
    sw2_d = P.dram(pfx + "sw2", [D, 2048])
    sw3_d = P.dram(pfx + "sw3", [2048, D])
    vecs_d = P.dram(pfx + "vecs", [128, 512])
    cbf_d = P.dram(pfx + "cbf", [128, 192], BF16)
    cid_d = P.dram(pfx + "cid32", [128, 128])

    vec = l2_pack().load(P, vecs_d)
    c_ones_col = vec["ones_col"]
    c_ones_row = vec["ones_row"]
    B2 = vec["B2"]
    cbf = P.alloc(192, BF16, "cbf")
    P.dma("y", cbf.ap, cbf_d, writes=[cbf])
    cid = P.alloc(128, F32, "cid32")
    P.dma("y", cid.ap, cid_d, writes=[cid])
    ident = cbf.ap[:, 0:128]
    E2 = cbf.ap[:, 128:130]
    obf = P.alloc(128, BF16, "ones_row_bf")
    P.emit("v", lambda E: E.memset(obf.ap[0:1, :], 1.0), writes=[obf])

    def dump_fm(sb3, nch=NCH, guard=0):
        outb = P.alloc(T, F32, "dumpout")
        for c in range(nch):
            sl = sb3.ap[:, c, guard:T + guard]
            P.emit("v", lambda E, sl=sl: E.tensor_copy(outb.ap, sl), reads=[sb3], writes=[outb])
            P.dma("y", y_d[(c % NCH) * 128:((c % NCH) + 1) * 128, :], outb.ap, reads=[outb])
        return P, [("y", "d", len(P.ops["y"]) - 1)]

    xn = alloc_xn(P)
    x = P.alloc(NCH * T, F32, "x")
    x.ap = x.ap.rearrange("p (c t) -> p c t", c=NCH)
    P.dma("y", x.ap, x_d.rearrange("(c p) t -> p c t", p=128), writes=[x])
    global_ln(P, x, xn, c_ones_col, c_ones_row)
    P.free(x)

    gwb = P.alloc(NCH * 8, BF16, "gate_w_bf")
    gwb.ap = gwb.ap.rearrange("p (c e) -> p c e", c=NCH)
    P.emit("v", lambda E: E.tensor_copy(
        gwb.ap, vec["gate_w"].ap.rearrange("p (c e) -> p c e", c=NCH)),
        reads=[vec["gate_w"]], writes=[gwb])
    logits = P.alloc(16 * 8, F32, "logits")
    logits.ap = logits.ap.rearrange("p (n e) -> p n e", n=16)
    for tt in range(16):
        bank = P.banks[tt % 2]
        for c in range(NCH):
            P.emit("t", lambda E, tt=tt, c=c, bank=bank: E.matmul(
                bank.ap[:, 0:8], xn.ap[:, c, 1 + tt * 128:1 + (tt + 1) * 128], gwb.ap[:, c, :],
                start=(c == 0), stop=(c == NCH - 1)), reads=[xn, gwb], writes=[bank])
        P.emit("v", lambda E, tt=tt, bank=bank: E.tensor_copy(logits.ap[:, tt, :],
               bank.ap[:, 0:8]), reads=[bank], writes=[logits])
    r_m1 = P.alloc(16, F32, "r_m1")
    r_m2 = P.alloc(16, F32, "r_m2")
    r_g1 = P.alloc(16, F32, "r_g1")
    r_g2 = P.alloc(16, F32, "r_g2")
    msk1 = P.alloc(16 * 8, F32, "msk1")
    msk1.ap = msk1.ap.rearrange("p (n e) -> p n e", n=16)
    msk2 = P.alloc(16 * 8, F32, "msk2")
    msk2.ap = msk2.ap.rearrange("p (n e) -> p n e", n=16)
    lg2 = P.alloc(16 * 8, F32, "lg2")
    lg2.ap = lg2.ap.rearrange("p (n e) -> p n e", n=16)
    gates = P.alloc(16 * 8, F32, "gates")
    gates.ap = gates.ap.rearrange("p (n e) -> p n e", n=16)
    P.emit("v", lambda E: E.tensor_reduce(r_m1.ap, logits.ap, axis=AX.X, op=ALU.max),
           reads=[logits], writes=[r_m1])
    P.emit("v", lambda E: E.tensor_tensor(msk1.ap, logits.ap,
           r_m1.ap.unsqueeze(2).broadcast_to([128, 16, 8]), ALU.is_equal),
           reads=[logits, r_m1], writes=[msk1])
    P.emit("v", lambda E: E.scalar_tensor_tensor(lg2.ap, msk1.ap, -1e30, logits.ap,
           ALU.mult, ALU.add), reads=[msk1, logits], writes=[lg2])
    P.emit("v", lambda E: E.tensor_reduce(r_m2.ap, lg2.ap, axis=AX.X, op=ALU.max),
           reads=[lg2], writes=[r_m2])
    P.emit("v", lambda E: E.tensor_tensor(msk2.ap, lg2.ap,
           r_m2.ap.unsqueeze(2).broadcast_to([128, 16, 8]), ALU.is_equal),
           reads=[lg2, r_m2], writes=[msk2])
    P.emit("v", lambda E: E.tensor_tensor(r_g1.ap, r_m1.ap, r_m2.ap, ALU.subtract),
           reads=[r_m1, r_m2], writes=[r_g1])
    act(P, r_g1, r_g1.ap, [r_g1], r_g1.ap, AF.Sigmoid)
    P.emit("v", lambda E: E.tensor_scalar(r_g2.ap, r_g1.ap, -1.0, 1.0, ALU.mult, ALU.add),
           reads=[r_g1], writes=[r_g2])
    P.emit("v", lambda E: E.tensor_tensor(gates.ap, msk1.ap,
           r_g1.ap.unsqueeze(2).broadcast_to([128, 16, 8]), ALU.mult),
           reads=[msk1, r_g1], writes=[gates])
    P.emit("v", lambda E: E.tensor_tensor(msk2.ap, msk2.ap,
           r_g2.ap.unsqueeze(2).broadcast_to([128, 16, 8]), ALU.mult),
           reads=[msk2, r_g2], writes=[msk2])
    P.emit("v", lambda E: E.tensor_tensor(gates.ap, gates.ap, msk2.ap, ALU.add),
           reads=[gates, msk2], writes=[gates])
    grow = P.alloc(T, F32, "gate_row")
    if dump == "gates":
        outb = P.alloc(T, F32, "dumpout")
        for tt in range(16):
            bank = P.banks[2 + tt % 2]
            P.emit("t", lambda E, tt=tt, bank=bank: E.transpose(
                bank.ap[0:1, 0:128], gates.ap[:, tt, 0:1], cid.ap),
                reads=[gates, cid], writes=[bank])
            P.emit("v", lambda E, tt=tt, bank=bank: E.tensor_copy(
                grow.ap[0:1, tt * 128:(tt + 1) * 128], bank.ap[0:1, 0:128]),
                reads=[bank], writes=[grow])
        P.emit("v", lambda E: E.memset(outb.ap, 0.0), writes=[outb])
        P.emit("v", lambda E: E.tensor_copy(outb.ap[0:1, :], grow.ap[0:1, :]),
               reads=[grow], writes=[outb])
        P.dma("y", y_d[0:128, :], outb.ap, reads=[outb])
        return P, [("y", "d", len(P.ops["y"]) - 1)]
    P.free(gwb, logits, r_m1, r_m2, r_g1, r_g2, msk1, msk2, lg2)

    yacc = P.alloc(NCH * T, F32, "yacc")
    yacc.ap = yacc.ap.rearrange("p (c t) -> p c t", c=NCH)
    abuf = P.alloc(16 * T, BF16, "abuf")
    abuf.ap = abuf.ap.rearrange("p (c t) -> p c t", c=16)
    wsl_a = P.alloc(NCH * 256, BF16, "wsl_a")
    wsl_b = P.alloc(NCH * 256, BF16, "wsl_b")
    wsl_3 = P.alloc(16 * 128, BF16, "wsl_3")
    stg = [P.alloc(512, F32, "l2_stg0"), P.alloc(512, F32, "l2_stg1")]
    t1 = P.alloc(512, F32, "l2_t1")
    gbc = P.alloc(T, F32, "l2_gbc")

    for e in range(9):
        shared = (e == 8)
        w1d = sw1_d if shared else ew1_d[e]
        w2d = sw2_d if shared else ew2_d[e]
        w3d = sw3_d if shared else ew3_d[e]
        for ds in range(8):
            wA = load_weight(P, w1d, NCH, 256, "ew1", slot=wsl_a, stage=stg, col0=ds * 256)
            wB = load_weight(P, w2d, NCH, 256, "ew2", slot=wsl_b, stage=stg, col0=ds * 256)
            for dsub in range(2):
                hc = ds * 2 + dsub
                for blk in range(4):
                    sl = slice(blk * 512, (blk + 1) * 512)
                    bankA = P.banks[(hc * 4 + blk) % 2]
                    bankB = P.banks[2 + (hc * 4 + blk) % 2]
                    for c in range(NCH):
                        P.emit("t", lambda E, c=c, dsub=dsub, sl=sl, bankA=bankA, wA=wA:
                               E.matmul(bankA.ap, wA.ap[:, c, dsub * 128:(dsub + 1) * 128],
                                        xn.ap[:, c, 1 + sl.start:1 + sl.stop],
                                        start=(c == 0), stop=(c == NCH - 1)),
                               reads=[wA, xn], writes=[bankA])
                    for c in range(NCH):
                        P.emit("t", lambda E, c=c, dsub=dsub, sl=sl, bankB=bankB, wB=wB:
                               E.matmul(bankB.ap, wB.ap[:, c, dsub * 128:(dsub + 1) * 128],
                                        xn.ap[:, c, 1 + sl.start:1 + sl.stop],
                                        start=(c == 0), stop=(c == NCH - 1)),
                               reads=[wB, xn], writes=[bankB])
                    act(P, t1, t1.ap, [bankA], bankA.ap, AF.Silu)
                    P.emit("v", lambda E, hc=hc, sl=sl, bankB=bankB: E.tensor_tensor(
                        abuf.ap[:, hc, sl], t1.ap, bankB.ap, ALU.mult),
                        reads=[t1, bankB], writes=[abuf])
        if not shared:
            for tt in range(16):
                bank = P.banks[4 + tt % 2]
                P.emit("t", lambda E, tt=tt, e=e, bank=bank: E.transpose(
                    bank.ap[0:1, 0:128], gates.ap[:, tt, e:e + 1], cid.ap),
                    reads=[gates, cid], writes=[bank])
                P.emit("v", lambda E, tt=tt, bank=bank: E.tensor_copy(
                    grow.ap[0:1, tt * 128:(tt + 1) * 128], bank.ap[0:1, 0:128]),
                    reads=[bank], writes=[grow])
            for blk in range(4):
                sl = slice(blk * 512, (blk + 1) * 512)
                bank = P.banks[4 + blk % 2]
                P.emit("t", lambda E, sl=sl, bank=bank: E.matmul(
                    bank.ap, c_ones_row.ap, grow.ap[0:1, sl],
                    start=True, stop=True), reads=[grow, c_ones_row], writes=[bank])
                P.emit("v", lambda E, sl=sl, bank=bank: E.tensor_copy(gbc.ap[:, sl], bank.ap),
                       reads=[bank], writes=[gbc])
        for ds3 in range(8):
            wC = load_weight(P, w3d, 16, 128, "ew3", slot=wsl_3, stage=stg, col0=ds3 * 128)
            for dsub in range(1):
                dc = ds3
                for blk in range(4):
                    sl = slice(blk * 512, (blk + 1) * 512)
                    bank = P.banks[6 + (dc * 4 + blk) % 2]
                    for c in range(16):
                        P.emit("t", lambda E, c=c, dsub=dsub, sl=sl, bank=bank, wC=wC:
                               E.matmul(bank.ap, wC.ap[:, c, dsub * 128:(dsub + 1) * 128],
                                        abuf.ap[:, c, sl], start=(c == 0), stop=(c == 15)),
                               reads=[wC, abuf], writes=[bank])
                    if shared:
                        P.emit("v", lambda E, dc=dc, sl=sl, bank=bank: E.tensor_tensor(
                            yacc.ap[:, dc, sl], yacc.ap[:, dc, sl], bank.ap, ALU.add),
                            reads=[yacc, bank], writes=[yacc])
                    elif e == 0:
                        P.emit("v", lambda E, dc=dc, sl=sl, bank=bank: E.tensor_tensor(
                            yacc.ap[:, dc, sl], bank.ap, gbc.ap[:, sl], ALU.mult),
                            reads=[bank, gbc], writes=[yacc])
                    else:
                        P.emit("v", lambda E, sl=sl, bank=bank: E.tensor_tensor(
                            t1.ap, bank.ap, gbc.ap[:, sl], ALU.mult),
                            reads=[bank, gbc], writes=[t1])
                        P.emit("v", lambda E, dc=dc, sl=sl: E.tensor_tensor(
                            yacc.ap[:, dc, sl], yacc.ap[:, dc, sl], t1.ap, ALU.add),
                            reads=[yacc, t1], writes=[yacc])
    gstage = P.alloc(T, BF16, "gstage")
    for c in range(NCH):
        act(P, gstage, gstage.ap, [yacc], yacc.ap[:, c, :], AF.Silu)
        P.dma("y", g_d[c * 128:(c + 1) * 128, :], gstage.ap, reads=[gstage])
    P.free(yacc, abuf, wsl_a, wsl_b, wsl_3, stg[0], stg[1], t1, gbc, gstage, grow, gates)
    if dump == "gdone":
        tmpb = P.alloc(T, F32, "tmpb")
        gtmp = P.alloc(T, BF16, "gtmp")
        for c in range(NCH):
            P.dma("y", gtmp.ap, g_d[c * 128:(c + 1) * 128, :], writes=[gtmp])
            P.emit("v", lambda E: E.tensor_copy(tmpb.ap, gtmp.ap), reads=[gtmp], writes=[tmpb])
            P.dma("y", y_d[c * 128:(c + 1) * 128, :], tmpb.ap, reads=[tmpb])
        return P, [("y", "d", len(P.ops["y"]) - 1)]

    lamc = P.alloc(64, F32, "lam")
    P.emit("v", lambda E: E.tensor_tensor(lamc.ap[0:1, 0:32], vec["lq1"].ap,
           vec["lk1"].ap, ALU.mult), reads=[vec["lq1"], vec["lk1"]], writes=[lamc])
    P.emit("v", lambda E: E.tensor_tensor(lamc.ap[0:1, 32:64], vec["lq2"].ap,
           vec["lk2"].ap, ALU.mult), reads=[vec["lq2"], vec["lk2"]], writes=[lamc])
    P.emit("v", lambda E: E.tensor_reduce(lamc.ap[0:1, 0:1], lamc.ap[0:1, 0:32],
           axis=AX.X, op=ALU.add), reads=[lamc], writes=[lamc])
    P.emit("v", lambda E: E.tensor_reduce(lamc.ap[0:1, 1:2], lamc.ap[0:1, 32:64],
           axis=AX.X, op=ALU.add), reads=[lamc], writes=[lamc])
    act(P, lamc, lamc.ap[0:1, 2:4], [lamc], lamc.ap[0:1, 0:2], AF.Exp)
    P.emit("v", lambda E: E.tensor_tensor(lamc.ap[0:1, 4:5], lamc.ap[0:1, 3:4],
           lamc.ap[0:1, 2:3], ALU.subtract), reads=[lamc], writes=[lamc])
    P.emit("v", lambda E: E.tensor_scalar(lamc.ap[0:1, 5:6], lamc.ap[0:1, 4:5], -0.2,
           None, ALU.add, ALU.bypass), reads=[lamc], writes=[lamc])
    neglam_col = P.alloc(4, F32, "neglam_col")
    P.emit("t", lambda E: E.matmul(P.banks[0].ap[:, 0:1], c_ones_row.ap,
           lamc.ap[0:1, 5:6], start=True, stop=True), reads=[lamc, c_ones_row],
           writes=[P.banks[0]])
    P.emit("v", lambda E: E.tensor_copy(neglam_col.ap[:, 0:1], P.banks[0].ap[:, 0:1]),
           reads=[P.banks[0]], writes=[neglam_col])

    qkv = P.alloc(32 * T, BF16, "qkv")
    qkv.ap = qkv.ap.rearrange("p (c t) -> p c t", c=32)
    wslot = P.alloc(NCH * D, BF16, "wslot")
    for j in range(4):
        wb = load_weight(P, wqkv_d, NCH, D, "wqkv", slot=wslot, col0=j * D)

        def evac_qkv(dc, blk, bank, j=j):
            gdc = j * NCH + dc
            sl = slice(blk * 512, (blk + 1) * 512)
            if j == 0:
                P.emit("v", lambda E, gdc=gdc, sl=sl, bank=bank: E.tensor_scalar(
                    qkv.ap[:, gdc, sl], bank.ap, 32.0 ** -0.5, None, ALU.mult,
                    ALU.bypass), reads=[bank], writes=[qkv])
            else:
                P.emit("v", lambda E, gdc=gdc, sl=sl, bank=bank: E.tensor_copy(
                    qkv.ap[:, gdc, sl], bank.ap), reads=[bank], writes=[qkv])
        proj(P, wb, lambda c, blk: xn.ap[:, c, 1 + blk * 512:1 + (blk + 1) * 512], [xn],
             NCH, NCH, evac_qkv)
    P.free(wslot, xn)
    if dump == "qkv":
        return dump_fm(qkv, 32)

    vtm = P.alloc(16 * 16 * 64, BF16, "vtm")
    vtm.ap = vtm.ap.rearrange("p (h k e) -> p h k e", h=16, k=16)
    for h in range(16):
        base = (h % 2) * 64
        vch = 24 + h // 2
        bank = P.banks[h % 2]
        bankb = bank.ap.bitcast(BF16)
        for kt in range(16):
            vsl = qkv.ap[base:base + 64, vch, kt * 128:(kt + 1) * 128]
            idn = ident[base:base + 64, base:base + 64]
            P.emit("t", lambda E, vsl=vsl, idn=idn, kt=kt, bankb=bankb: E.transpose(
                bankb[:, kt * 64:(kt + 1) * 64], vsl, idn), reads=[qkv, cbf], writes=[bank])
        P.emit("v", lambda E, h=h, bankb=bankb: E.tensor_copy(
            vtm.ap[:, h, :, :], bankb.rearrange("p (k e) -> p k e", k=16)),
            reads=[bank], writes=[vtm])

    negmb = P.alloc(8, F32, "negm_cols")
    nrow = P.alloc(2 * 2048, BF16, "negm_rows")
    rrow = P.alloc(2 * T, F32, "recip_rows")
    dcol = P.alloc(8, F32, "den_cols")
    scrap = P.alloc(512, F32, "exp_scrap")
    pt1 = P.alloc(512, BF16, "pt1")
    pt2 = P.alloc(512, BF16, "pt2")
    rbc2 = P.alloc(1024, F32, "o_rbc")
    ot1 = P.alloc(512, F32, "o_t1")
    ot2 = P.alloc(512, F32, "o_t2")

    for hp in range(8):
        for par in range(2):
            h = 2 * hp + par
            base = par * 64
            qch = h // 2
            Q = qkv.ap[base:base + 64, qch, :]
            for comp in range(2):
                kch = (8 if comp == 0 else 16) + h // 2
                K = qkv.ap[base:base + 64, kch, :]
                nrow_h = nrow.ap[0:1, comp * 2048:comp * 2048 + 2048]
                rrow_h = rrow.ap[0:1, comp * T:comp * T + T]
                for qt in range(16):
                    bank = P.banks[qt % 2]
                    for kblk in range(4):
                        ksl = slice(kblk * 512, (kblk + 1) * 512)
                        P.emit("t", lambda E, qt=qt, ksl=ksl, bank=bank, Q=Q, K=K: E.matmul(
                            bank.ap, Q[:, qt * 128:(qt + 1) * 128], K[:, ksl],
                            start=True, stop=True), reads=[qkv], writes=[bank])
                        P.emit("v", lambda E, kblk=kblk, bank=bank: E.tensor_reduce(
                            negmb.ap[:, kblk:kblk + 1], bank.ap, axis=AX.X, op=ALU.max,
                            negate=True), reads=[bank], writes=[negmb])
                    P.emit("v", lambda E: E.tensor_reduce(negmb.ap[:, 4:5],
                           negmb.ap[:, 0:4], axis=AX.X, op=ALU.min), reads=[negmb],
                           writes=[negmb])
                    P.emit("v", lambda E: E.tensor_copy(
                        negmb.ap[:, 5:6].bitcast(BF16)[:, 0:1], negmb.ap[:, 4:5]),
                        reads=[negmb], writes=[negmb])
                    P.emit("v", lambda E: E.tensor_copy(negmb.ap[:, 6:7],
                           negmb.ap[:, 5:6].bitcast(BF16)[:, 0:1]), reads=[negmb],
                           writes=[negmb])
                    for kblk in range(4):
                        ksl = slice(kblk * 512, (kblk + 1) * 512)
                        bank = P.banks[2 + kblk % 2]
                        P.emit("t", lambda E, qt=qt, ksl=ksl, bank=bank, Q=Q, K=K: E.matmul(
                            bank.ap, Q[:, qt * 128:(qt + 1) * 128], K[:, ksl],
                            start=True, stop=True), reads=[qkv], writes=[bank])
                        act(P, scrap, scrap.ap, [bank, negmb], bank.ap, AF.Exp,
                            bias=negmb.ap[:, 6:7], accum_ap=dcol.ap[:, kblk:kblk + 1],
                            accum_buf=dcol)
                    P.emit("v", lambda E: E.tensor_reduce(dcol.ap[:, 4:5], dcol.ap[:, 0:4],
                           axis=AX.X, op=ALU.add), reads=[dcol], writes=[dcol])
                    P.emit("v", lambda E: E.reciprocal(dcol.ap[:, 5:6], dcol.ap[:, 4:5]),
                           reads=[dcol], writes=[dcol])
                    P.emit("t", lambda E: E.transpose(P.banks[6].ap[0:1, 0:128],
                           negmb.ap[:, 6:7], cid.ap), reads=[negmb, cid], writes=[P.banks[6]])
                    P.emit("v", lambda E, qt=qt, nrow_h=nrow_h: E.tensor_copy(
                        nrow_h[0:1, qt * 128:(qt + 1) * 128], P.banks[6].ap[0:1, 0:128]),
                        reads=[P.banks[6]], writes=[nrow])
                    P.emit("t", lambda E: E.transpose(P.banks[7].ap[0:1, 0:128],
                           dcol.ap[:, 5:6], cid.ap), reads=[dcol, cid], writes=[P.banks[7]])
                    P.emit("v", lambda E, qt=qt, rrow_h=rrow_h: E.tensor_copy(
                        rrow_h[0:1, qt * 128:(qt + 1) * 128], P.banks[7].ap[0:1, 0:128]),
                        reads=[P.banks[7]], writes=[rrow])
            for qblk in range(4):
                qsl = slice(qblk * 512, (qblk + 1) * 512)
                o1bank = P.banks[4]
                o2bank = P.banks[5]
                for kt in range(16):
                    for comp in range(2):
                        kch = (8 if comp == 0 else 16) + h // 2
                        K = qkv.ap[base:base + 64, kch, :]
                        nrow_h = nrow.ap[0:1, comp * 2048:comp * 2048 + 2048]
                        bank = P.banks[comp]
                        ptx = pt1 if comp == 0 else pt2
                        P.emit("t", lambda E, kt=kt, qsl=qsl, bank=bank, Q=Q, K=K: E.matmul(
                            bank.ap, K[:, kt * 128:(kt + 1) * 128], Q[:, qsl],
                            start=True, stop=False), reads=[qkv], writes=[bank])
                        P.emit("t", lambda E, qsl=qsl, bank=bank, nrow_h=nrow_h: E.matmul(
                            bank.ap, obf.ap[0:1, :], nrow_h[0:1, qsl], start=False,
                            stop=True), reads=[obf, nrow], writes=[bank])
                        act(P, ptx, ptx.ap, [bank], bank.ap, AF.Exp)
                        obank = o1bank if comp == 0 else o2bank
                        P.emit("t", lambda E, kt=kt, obank=obank, h=h, par=par, ptx=ptx:
                               E.matmul(obank.ap[par * 64:par * 64 + 64, :],
                                        vtm.ap[:, h, kt, :], ptx.ap,
                                        start=(kt == 0), stop=(kt == 15),
                                        tile_position=(0, par * 64)),
                               reads=[vtm, ptx], writes=[obank])
                for (comp, obank, ot) in ((0, o1bank, ot1), (1, o2bank, ot2)):
                    rrow_h = rrow.ap[0:1, comp * T + qblk * 512:comp * T + (qblk + 1) * 512]
                    bank = P.banks[2 + comp]
                    P.emit("t", lambda E, rrow_h=rrow_h, bank=bank: E.matmul(
                        bank.ap, c_ones_row.ap, rrow_h, start=True, stop=True),
                        reads=[rrow, c_ones_row], writes=[bank])
                    P.emit("v", lambda E, bank=bank: E.tensor_copy(rbc2.ap[:, 0:512],
                           bank.ap), reads=[bank], writes=[rbc2])
                    P.emit("v", lambda E, obank=obank, ot=ot, par=par: E.tensor_tensor(
                        ot.ap[par * 64:par * 64 + 64, :],
                        obank.ap[par * 64:par * 64 + 64, :],
                        rbc2.ap[par * 64:par * 64 + 64, 0:512], ALU.mult),
                        reads=[obank, rbc2], writes=[ot])
                P.emit("v", lambda E, hp=hp, par=par, qsl=qsl: E.scalar_tensor_tensor(
                    qkv.ap[par * 64:par * 64 + 64, 24 + hp, qsl],
                    ot2.ap[par * 64:par * 64 + 64, :], neglam_col.ap[par * 64:par * 64 + 64, 0:1],
                    ot1.ap[par * 64:par * 64 + 64, :], ALU.mult, ALU.add),
                    reads=[ot1, ot2, neglam_col], writes=[qkv])
    P.free(vtm, negmb, nrow, rrow, dcol, scrap, pt1, pt2, rbc2, ot1, ot2)
    if dump == "ofm":
        return dump_fm(qkv, 32)

    sqb = P.alloc(T, BF16, "sn_sq")
    rows2h = P.alloc(T, F32, "sn_rows")
    tmul = P.alloc(512, F32, "sn_tmul")
    for c in range(NCH):
        ch = 24 + c
        for blk in range(4):
            sl = slice(blk * 512, (blk + 1) * 512)
            act(P, sqb, sqb.ap[:, sl], [qkv], qkv.ap[:, ch, sl], AF.Square)
            P.emit("t", lambda E, sl=sl, blk=blk: E.matmul(
                P.banks[blk].ap[0:2, :], E2, sqb.ap[:, sl], start=True, stop=True),
                reads=[sqb, cbf], writes=[P.banks[blk]])
        for blk in range(4):
            sl = slice(blk * 512, (blk + 1) * 512)
            P.emit("v", lambda E, sl=sl, blk=blk: E.tensor_scalar(
                rows2h.ap[0:2, sl], P.banks[blk].ap[0:2, :], 1.0 / 64, 1e-5,
                ALU.mult, ALU.add), reads=[P.banks[blk]], writes=[rows2h])
        P.emit("s", lambda E: E.activation(rows2h.ap[0:2, :], rows2h.ap[0:2, :], AF.Sqrt),
               reads=[rows2h], writes=[rows2h])
        P.emit("v", lambda E: E.reciprocal(rows2h.ap[0:2, :], rows2h.ap[0:2, :]),
               reads=[rows2h], writes=[rows2h])
        for blk in range(4):
            sl = slice(blk * 512, (blk + 1) * 512)
            bank = P.banks[4 + blk % 2]
            P.emit("t", lambda E, sl=sl, bank=bank: E.matmul(
                bank.ap, B2.ap, rows2h.ap[0:2, sl], start=True, stop=True),
                reads=[rows2h, B2], writes=[bank])
            P.emit("v", lambda E, ch=ch, sl=sl, bank=bank: E.tensor_tensor(
                tmul.ap, qkv.ap[:, ch, sl], bank.ap, ALU.mult),
                reads=[qkv, bank], writes=[tmul])
            P.emit("v", lambda E, ch=ch, sl=sl: E.tensor_scalar(
                qkv.ap[:, ch, sl], tmul.ap, vec["snw"].ap, 0.8, ALU.mult, ALU.mult),
                reads=[tmul, vec["snw"]], writes=[qkv])
    P.free(sqb, rows2h, tmul)
    if dump == "osn":
        return dump_fm(qkv, 32)

    wslot = P.alloc(NCH * D, BF16, "wslot2")
    wb = load_weight(P, wproj_d, NCH, D, "wproj", slot=wslot)
    xst = P.alloc(T, F32, "xst")
    gtmp = P.alloc(T, BF16, "gtmp")
    yst = P.alloc(T, F32, "yst")
    st1 = P.alloc(512, F32, "st1")
    state = {"dc": -1}

    def evac_upd(dc, blk, bank):
        if state["dc"] != dc:
            state["dc"] = dc
            P.dma("y", xst.ap, x_d[dc * 128:(dc + 1) * 128, :], writes=[xst])
            P.dma("y", gtmp.ap, g_d[dc * 128:(dc + 1) * 128, :], writes=[gtmp])
        sl = slice(blk * 512, (blk + 1) * 512)
        P.emit("v", lambda E, dc=dc, sl=sl, bank=bank: E.tensor_scalar(
            st1.ap, bank.ap, vec["b_proj"].ap[:, dc:dc + 1], None, ALU.add, ALU.bypass),
            reads=[bank, vec["b_proj"]], writes=[st1])
        P.emit("v", lambda E, sl=sl: E.tensor_tensor(st1.ap, st1.ap, gtmp.ap[:, sl],
               ALU.mult), reads=[st1, gtmp], writes=[st1])
        P.emit("v", lambda E, sl=sl: E.tensor_tensor(yst.ap[:, sl], st1.ap, xst.ap[:, sl],
               ALU.add), reads=[st1, xst], writes=[yst])
        if blk == 3:
            P.dma("y", y_d[dc * 128:(dc + 1) * 128, :], yst.ap, reads=[yst])
    proj(P, wb, lambda c, blk: qkv.ap[:, 24 + c, blk * 512:(blk + 1) * 512], [qkv],
         NCH, NCH, evac_upd)
    last_y = ("y", "d", len(P.ops["y"]) - 1)
    P.pop_scope()
    return P, [last_y]


def l2_inputs(core_x, ins, pfx=""):
    wq = np.asarray(ins["L2_attn_w_qkv"], np.float32)
    q_w = wq[0:1024]
    k_w = wq[1024:2048]
    v_w = wq[2048:3072]
    kz1 = np.zeros_like(k_w)
    kz2 = np.zeros_like(k_w)
    for h in range(16):
        kz1[64 * h:64 * h + 32] = k_w[64 * h:64 * h + 32]
        kz2[64 * h + 32:64 * h + 64] = k_w[64 * h + 32:64 * h + 64]
    wpack = np.concatenate([q_w, kz1, kz2, v_w], axis=0)
    rd = {
        "x_in": _f32(core_x),
        "wqkvT4": _f32(wpack.T),
        "wprojT": _f32(ins["L2_attn_w_proj"].T),
        "ew1": _f32(ins["L2_ffn_ew1"]),
        "ew2": _f32(ins["L2_ffn_ew2"]),
        "ew3": _f32(ins["L2_ffn_ew3"]),
        "sw1": _f32(ins["L2_ffn_sw1"]),
        "sw2": _f32(ins["L2_ffn_sw2"]),
        "sw3": _f32(ins["L2_ffn_sw3"]),
        "vecs": l2_pack(ins).host,
        "cbf": _bf_consts(),
        "cid32": np.eye(128, dtype=np.float32),
    }
    return {("x_in" if k == "x_in" else pfx + k): v for k, v in rd.items()}





def l0_pack(ins=None):
    vp = VecPack(4096)
    if ins is not None:
        vp.add_cols("lng_c", _col8(ins["L0_attn_ln_g"]))
        vp.add_cols("lnb_c", _col8(ins["L0_attn_ln_b"]))
        vp.add_cols("h1b1", _f32(np.asarray(ins["L0_attn_h1_b1"]).reshape(32, 128).T))
        vp.add_cols("h1b2", _col8(ins["L0_attn_h1_b2"]))
        vp.add_cols("h2b1", _f32(np.asarray(ins["L0_attn_h2_b1"]).reshape(32, 128).T))
        vp.add_cols("h2b2", _col8(ins["L0_attn_h2_b2"]))
        vp.add_cols("fmb1", _f32(np.asarray(ins["L0_attn_fm_b1"]).reshape(32, 128).T))
        vp.add_cols("fmb2", _col8(ins["L0_attn_fm_b2"]))
        vp.add_cols("moeb1", _f32(np.asarray(ins["L0_ffn_b1"]).reshape(8, 32, 128)
                                  .transpose(2, 0, 1).reshape(128, 256)))
        vp.add_row("lng_r", ins["L0_attn_ln_g"])
        vp.add_row("lnb_r", ins["L0_attn_ln_b"])
        vp.add_row("router_b", ins["L0_ffn_router_b"])
        vp.add_cols("router_w", _f32(np.asarray(ins["L0_ffn_router_w"]).reshape(NCH, 128, 8)
                                     .transpose(1, 0, 2).reshape(128, NCH * 8)))
    else:
        vp.add_cols("lng_c", np.zeros((128, 8), np.float32))
        vp.add_cols("lnb_c", np.zeros((128, 8), np.float32))
        for n in ("h1b1", "h1b2", "h2b1", "h2b2", "fmb1", "fmb2"):
            vp.add_cols(n, np.zeros((128, 32 if n.endswith("b1") else 8), np.float32))
        vp.add_cols("moeb1", np.zeros((128, 256), np.float32))
        vp.add_row("lng_r", np.zeros(1024, np.float32))
        vp.add_row("lnb_r", np.zeros(1024, np.float32))
        vp.add_row("router_b", np.zeros(8, np.float32))
        vp.add_cols("router_w", np.zeros((128, NCH * 8), np.float32))
    vp.add_cols("ones_col", np.ones((128, 1), np.float32))
    vp.add_row("ones_row", np.ones(128, np.float32))
    return vp


def _bf_consts_l0():
    c = np.zeros((128, 2304), BF)
    c[:, 0:128] = np.eye(128, dtype=BF)
    c[:, 130] = 1
    zz = np.arange(512)[None, :]
    pp = np.arange(128)[:, None]
    for k in range(4):
        c[:, 256 + 512 * k:256 + 512 * (k + 1)] = (zz >= 128 * k + pp).astype(BF)
    return c


def internal_ln(P, src3, dst3, gcol, bcol, onescol_bf, c_ones_row, gofs=0, eps=1e-5):
    sqb = P.alloc(T, BF16, "iln_sq")
    rows = P.alloc(2 * T, F32, "iln_rows")
    A_bc = P.alloc(T, F32, "iln_Abc")
    B_bc = P.alloc(T, F32, "iln_Bbc")
    t3 = P.alloc(T, F32, "iln_t3")
    t1 = P.alloc(T, F32, "iln_t1")
    bsum = P.banks[:4]
    bsq = P.banks[4:8]
    mean = rows.ap[0:1, 0:T]
    msq = rows.ap[0:1, T:2 * T]
    for c in range(NCH):
        act(P, sqb, sqb.ap, [src3], src3.ap[:, c, gofs:gofs + T], AF.Square)
        for blk in range(4):
            sl = slice(blk * 512, (blk + 1) * 512)
            P.emit("t", lambda E, c=c, sl=sl, blk=blk: E.matmul(
                bsum[blk].ap[0:1, :], onescol_bf, src3.ap[:, c, gofs + sl.start:gofs + sl.stop],
                start=(c == 0), stop=(c == NCH - 1)), reads=[src3], writes=[bsum[blk]])
            P.emit("t", lambda E, c=c, sl=sl, blk=blk: E.matmul(
                bsq[blk].ap[0:1, :], onescol_bf, sqb.ap[:, sl],
                start=(c == 0), stop=(c == NCH - 1)), reads=[sqb], writes=[bsq[blk]])
    for blk in range(4):
        sl = slice(blk * 512, (blk + 1) * 512)
        P.emit("v", lambda E, blk=blk, sl=sl: E.tensor_scalar(
            mean[:, sl], bsum[blk].ap[0:1, :], 1.0 / D, None, ALU.mult, ALU.bypass),
            reads=[bsum[blk]], writes=[rows])
        P.emit("v", lambda E, blk=blk, sl=sl: E.tensor_scalar(
            msq[:, sl], bsq[blk].ap[0:1, :], 1.0 / D, eps, ALU.mult, ALU.add),
            reads=[bsq[blk]], writes=[rows])
    m2row = t3.ap[0:1, 0:T]
    P.emit("v", lambda E: E.tensor_tensor(m2row, mean, mean, ALU.mult), reads=[rows], writes=[t3])
    P.emit("v", lambda E: E.tensor_tensor(msq, msq, m2row, ALU.subtract),
           reads=[rows, t3], writes=[rows])
    P.emit("s", lambda E: E.activation(msq, msq, AF.Sqrt), reads=[rows], writes=[rows])
    P.emit("v", lambda E: E.reciprocal(msq, msq), reads=[rows], writes=[rows])
    P.emit("v", lambda E: E.scalar_tensor_tensor(mean, mean, -1.0, msq, ALU.mult, ALU.mult),
           reads=[rows], writes=[rows])
    for (srow, dst) in ((msq, A_bc), (mean, B_bc)):
        for blk in range(4):
            sl = slice(blk * 512, (blk + 1) * 512)
            bank = P.banks[blk % 2]
            P.emit("t", lambda E, srow=srow, sl=sl, bank=bank: E.matmul(
                bank.ap, c_ones_row.ap, srow[:, sl], start=True, stop=True),
                reads=[rows, c_ones_row], writes=[bank])
            P.emit("v", lambda E, dst=dst, sl=sl, bank=bank: E.tensor_copy(
                dst.ap[:, sl], bank.ap), reads=[bank], writes=[dst])
    for c in range(NCH):
        P.emit("v", lambda E, c=c: E.tensor_scalar(
            t3.ap, B_bc.ap, gcol.ap[:, c:c + 1], bcol.ap[:, c:c + 1], ALU.mult, ALU.add),
            reads=[B_bc, gcol, bcol], writes=[t3])
        P.emit("v", lambda E, c=c: E.tensor_tensor(
            t1.ap, src3.ap[:, c, gofs:gofs + T], A_bc.ap, ALU.mult),
            reads=[src3, A_bc], writes=[t1])
        P.emit("v", lambda E, c=c: E.scalar_tensor_tensor(
            dst3.ap[:, c, :], t1.ap, gcol.ap[:, c:c + 1], t3.ap, ALU.mult, ALU.add),
            reads=[t1, gcol, t3], writes=[dst3])
    P.free(sqb, rows, A_bc, B_bc, t3, t1)


def build_l0(P=None, x_d=None, y_d=None, pfx="", dump=None):
    if P is None:
        P = Prog("l0").make_arena()
        x_d = P.dram("x_in", [D, T])
        y_d = P.dram("y_out", [D, T], kind="ExternalOutput")
    P.push_scope()
    g_d = P.dram(pfx + "g_spill", [D, T], BF16, kind="Internal")
    h1w1_d = P.dram(pfx + "h1w1", [D, 4096])
    h1w2_d = P.dram(pfx + "h1w2", [4096, D])
    h2w1_d = P.dram(pfx + "h2w1", [D, 4096])
    h2w2_d = P.dram(pfx + "h2w2", [4096, D])
    fmw1_d = P.dram(pfx + "fmw1", [D, 4096])
    fmw2_d = P.dram(pfx + "fmw2", [4096, D])
    mw1_d = P.dram(pfx + "mw1", [8, D, 4096])
    mw2_d = P.dram(pfx + "mw2", [8, 4096, D])
    mb2_d = P.dram(pfx + "mb2", [8, D])
    vecs_d = P.dram(pfx + "vecs", [128, 4096])
    cbf_d = P.dram(pfx + "cbf", [128, 2304], BF16)

    vec = l0_pack().load(P, vecs_d)
    c_ones_col = vec["ones_col"]
    c_ones_row = vec["ones_row"]
    cbf = P.alloc(2304, BF16, "cbf")
    P.dma("y", cbf.ap, cbf_d, writes=[cbf])
    ident = cbf.ap[:, 0:128]
    onescol_bf = cbf.ap[:, 130:131]
    masks = [cbf.ap[:, 256 + 512 * k:256 + 512 * (k + 1)] for k in range(4)]
    b2row = P.alloc(D, BF16, "moe_b2row")
    b2f = P.alloc(D, F32, "moe_b2f")

    def dump_fm(sb3, nch=NCH, guard=0):
        outb = P.alloc(T, F32, "dumpout")
        for c in range(nch):
            sl = sb3.ap[:, c, guard:T + guard]
            P.emit("v", lambda E, sl=sl: E.tensor_copy(outb.ap, sl), reads=[sb3], writes=[outb])
            P.dma("y", y_d[(c % NCH) * 128:((c % NCH) + 1) * 128, :], outb.ap, reads=[outb])
        return P, [("y", "d", len(P.ops["y"]) - 1)]

    xn = alloc_xn(P)
    x = P.alloc(NCH * T, F32, "x")
    x.ap = x.ap.rearrange("p (c t) -> p c t", c=NCH)
    P.dma("y", x.ap, x_d.rearrange("(c p) t -> p c t", p=128), writes=[x])
    global_ln(P, x, xn, c_ones_col, c_ones_row)
    P.free(x)

    rwb = P.alloc(NCH * 8, BF16, "router_w_bf")
    rwb.ap = rwb.ap.rearrange("p (c e) -> p c e", c=NCH)
    P.emit("v", lambda E: E.tensor_copy(
        rwb.ap, vec["router_w"].ap.rearrange("p (c e) -> p c e", c=NCH)),
        reads=[vec["router_w"]], writes=[rwb])
    rbrow_bf = P.alloc(8, BF16, "router_b_bf")
    P.emit("v", lambda E: E.tensor_copy(rbrow_bf.ap[0:1, :], vec["router_b"].ap),
           reads=[vec["router_b"]], writes=[rbrow_bf])
    obf = P.alloc(512, BF16, "ones_row_bf")
    P.emit("v", lambda E: E.memset(obf.ap[0:1, :], 1.0), writes=[obf])
    logits = P.alloc(16 * 8, F32, "logits")
    logits.ap = logits.ap.rearrange("p (n e) -> p n e", n=16)
    for tt in range(16):
        bank = P.banks[tt % 2]
        for c in range(NCH):
            P.emit("t", lambda E, tt=tt, c=c, bank=bank: E.matmul(
                bank.ap[:, 0:8], xn.ap[:, c, 1 + tt * 128:1 + (tt + 1) * 128], rwb.ap[:, c, :],
                start=(c == 0), stop=False), reads=[xn, rwb], writes=[bank])
        P.emit("t", lambda E, tt=tt, bank=bank: E.matmul(
            bank.ap[:, 0:8], obf.ap[0:1, 0:128], rbrow_bf.ap[0:1, :], start=False, stop=True),
            reads=[obf, rbrow_bf], writes=[bank])
        P.emit("v", lambda E, tt=tt, bank=bank: E.tensor_copy(logits.ap[:, tt, :],
               bank.ap[:, 0:8]), reads=[bank], writes=[logits])
    r_m1 = P.alloc(16, F32, "r_m1")
    r_m2 = P.alloc(16, F32, "r_m2")
    r_g1 = P.alloc(16, F32, "r_g1")
    r_g2 = P.alloc(16, F32, "r_g2")
    msk1 = P.alloc(16 * 8, F32, "msk1")
    msk1.ap = msk1.ap.rearrange("p (n e) -> p n e", n=16)
    msk2 = P.alloc(16 * 8, F32, "msk2")
    msk2.ap = msk2.ap.rearrange("p (n e) -> p n e", n=16)
    lg2 = P.alloc(16 * 8, F32, "lg2")
    lg2.ap = lg2.ap.rearrange("p (n e) -> p n e", n=16)
    gates = P.alloc(16 * 8, F32, "gates")
    gates.ap = gates.ap.rearrange("p (n e) -> p n e", n=16)
    P.emit("v", lambda E: E.tensor_reduce(r_m1.ap, logits.ap, axis=AX.X, op=ALU.max),
           reads=[logits], writes=[r_m1])
    P.emit("v", lambda E: E.tensor_tensor(msk1.ap, logits.ap,
           r_m1.ap.unsqueeze(2).broadcast_to([128, 16, 8]), ALU.is_equal),
           reads=[logits, r_m1], writes=[msk1])
    P.emit("v", lambda E: E.scalar_tensor_tensor(lg2.ap, msk1.ap, -1e30, logits.ap,
           ALU.mult, ALU.add), reads=[msk1, logits], writes=[lg2])
    P.emit("v", lambda E: E.tensor_reduce(r_m2.ap, lg2.ap, axis=AX.X, op=ALU.max),
           reads=[lg2], writes=[r_m2])
    P.emit("v", lambda E: E.tensor_tensor(msk2.ap, lg2.ap,
           r_m2.ap.unsqueeze(2).broadcast_to([128, 16, 8]), ALU.is_equal),
           reads=[lg2, r_m2], writes=[msk2])
    P.emit("v", lambda E: E.tensor_tensor(r_g1.ap, r_m1.ap, r_m2.ap, ALU.subtract),
           reads=[r_m1, r_m2], writes=[r_g1])
    act(P, r_g1, r_g1.ap, [r_g1], r_g1.ap, AF.Sigmoid)
    P.emit("v", lambda E: E.tensor_scalar(r_g2.ap, r_g1.ap, -1.0, 1.0, ALU.mult, ALU.add),
           reads=[r_g1], writes=[r_g2])
    P.emit("v", lambda E: E.tensor_tensor(gates.ap, msk1.ap,
           r_g1.ap.unsqueeze(2).broadcast_to([128, 16, 8]), ALU.mult),
           reads=[msk1, r_g1], writes=[gates])
    P.emit("v", lambda E: E.tensor_tensor(msk2.ap, msk2.ap,
           r_g2.ap.unsqueeze(2).broadcast_to([128, 16, 8]), ALU.mult),
           reads=[msk2, r_g2], writes=[msk2])
    P.emit("v", lambda E: E.tensor_tensor(gates.ap, gates.ap, msk2.ap, ALU.add),
           reads=[gates, msk2], writes=[gates])
    P.free(rwb, rbrow_bf, logits, r_m1, r_m2, r_g1, r_g2, msk1, msk2, lg2)
    cid = P.alloc(128, F32, "cid32_dyn")
    P.emit("v", lambda E: E.tensor_copy(cid.ap, ident), reads=[cbf], writes=[cid])

    yacc = P.alloc(NCH * T, BF16, "yacc")
    yacc.ap = yacc.ap.rearrange("p (c t) -> p c t", c=NCH)
    HB = T // 2
    hbuf = P.alloc(32 * HB, BF16, "hbuf")
    hbuf.ap = hbuf.ap.rearrange("p (c t) -> p c t", c=32)
    wslA = P.alloc(NCH * 512, BF16, "wslA")
    wslB = P.alloc(32 * 128, BF16, "wslB")
    stg = [P.alloc(512, F32, "l0_stg0"), P.alloc(512, F32, "l0_stg1")]
    grow = P.alloc(T, F32, "gate_row")
    gbc = P.alloc(T, F32, "l0_gbc")
    t1 = P.alloc(512, F32, "l0_t1")

    for e in range(8):
        for tt in range(16):
            bank = P.banks[4 + tt % 2]
            P.emit("t", lambda E, tt=tt, e=e, bank=bank: E.transpose(
                bank.ap[0:1, 0:128], gates.ap[:, tt, e:e + 1], cid.ap),
                reads=[gates, cid], writes=[bank])
            P.emit("v", lambda E, tt=tt, bank=bank: E.tensor_copy(
                grow.ap[0:1, tt * 128:(tt + 1) * 128], bank.ap[0:1, 0:128]),
                reads=[bank], writes=[grow])
        P.dma("y", b2f.ap[0:1, :], mb2_d[e:e + 1, :], writes=[b2f])
        P.emit("v", lambda E: E.tensor_copy(b2row.ap[0:1, :], b2f.ap[0:1, :]),
               reads=[b2f], writes=[b2row])
        for blk in range(4):
            sl = slice(blk * 512, (blk + 1) * 512)
            bank = P.banks[4 + blk % 2]
            P.emit("t", lambda E, sl=sl, bank=bank: E.matmul(
                bank.ap, c_ones_row.ap, grow.ap[0:1, sl], start=True, stop=True),
                reads=[grow, c_ones_row], writes=[bank])
            P.emit("v", lambda E, sl=sl, bank=bank: E.tensor_copy(gbc.ap[:, sl], bank.ap),
                   reads=[bank], writes=[gbc])
        for half in range(2):
            t0 = half * HB
            for ds in range(8):
                wA = load_weight(P, mw1_d[e], NCH, 512, "mw1", slot=wslA, stage=stg,
                                 col0=ds * 512)
                for dsub in range(4):
                    hc = ds * 4 + dsub
                    for blk in range(HB // 512):
                        sl = slice(t0 + blk * 512, t0 + (blk + 1) * 512)
                        bank = P.banks[(hc * 2 + blk) % 4]
                        for c in range(NCH):
                            P.emit("t", lambda E, c=c, dsub=dsub, sl=sl, bank=bank, wA=wA:
                                   E.matmul(bank.ap, wA.ap[:, c, dsub * 128:(dsub + 1) * 128],
                                            xn.ap[:, c, 1 + sl.start:1 + sl.stop],
                                            start=(c == 0), stop=(c == NCH - 1)),
                                   reads=[wA, xn], writes=[bank])
                        act(P, hbuf, hbuf.ap[:, hc, blk * 512:(blk + 1) * 512], [bank],
                            bank.ap, AF.Gelu, bias=vec["moeb1"].ap[:, e * 32 + hc:e * 32 + hc + 1])
            for ds3 in range(8):
                wC = load_weight(P, mw2_d[e], 32, 128, "mw2", slot=wslB, stage=stg,
                                 col0=ds3 * 128)
                dc = ds3
                for blk in range(HB // 512):
                    sl = slice(t0 + blk * 512, t0 + (blk + 1) * 512)
                    bank = P.banks[4 + blk % 2]
                    for c in range(32):
                        P.emit("t", lambda E, c=c, sl=sl, t0=t0, bank=bank, wC=wC,
                               hbuf=hbuf: E.matmul(
                            bank.ap, wC.ap[:, c, 0:128],
                            hbuf.ap[:, c, sl.start - t0:sl.stop - t0],
                            start=(c == 0), stop=False), reads=[wC, hbuf], writes=[bank])
                    P.emit("t", lambda E, dc=dc, bank=bank: E.matmul(
                        bank.ap, b2row.ap[0:1, dc * 128:(dc + 1) * 128],
                        obf.ap[0:1, 0:512], start=False, stop=True),
                        reads=[b2row, obf], writes=[bank])
                    if e == 0:
                        P.emit("v", lambda E, dc=dc, sl=sl, bank=bank: E.tensor_tensor(
                            yacc.ap[:, dc, sl], bank.ap, gbc.ap[:, sl], ALU.mult),
                            reads=[bank, gbc], writes=[yacc])
                    else:
                        P.emit("v", lambda E, sl=sl, bank=bank: E.tensor_tensor(
                            t1.ap, bank.ap, gbc.ap[:, sl], ALU.mult),
                            reads=[bank, gbc], writes=[t1])
                        P.emit("v", lambda E, dc=dc, sl=sl: E.tensor_tensor(
                            yacc.ap[:, dc, sl], yacc.ap[:, dc, sl], t1.ap, ALU.add),
                            reads=[yacc, t1], writes=[yacc])
    gstage = P.alloc(T, BF16, "gstage")
    for c in range(NCH):
        act(P, gstage, gstage.ap, [yacc], yacc.ap[:, c, :], AF.Silu)
        P.dma("y", g_d[c * 128:(c + 1) * 128, :], gstage.ap, reads=[gstage])
    P.free(yacc, hbuf, wslA, wslB, stg[0], stg[1], grow, gbc, t1, gstage, gates, cid, b2row, b2f)
    if dump == "gdone":
        tmpb = P.alloc(T, F32, "tmpb")
        gtmp = P.alloc(T, BF16, "gtmp")
        for c in range(NCH):
            P.dma("y", gtmp.ap, g_d[c * 128:(c + 1) * 128, :], writes=[gtmp])
            P.emit("v", lambda E: E.tensor_copy(tmpb.ap, gtmp.ap), reads=[gtmp], writes=[tmpb])
            P.dma("y", y_d[c * 128:(c + 1) * 128, :], tmpb.ap, reads=[tmpb])
        return P, [("y", "d", len(P.ops["y"]) - 1)]

    n3 = P.alloc(NCH * T, BF16, "n3")
    n3.ap = n3.ap.rearrange("p (c t) -> p c t", c=NCH)
    internal_ln(P, xn, n3, vec["lng_c"], vec["lnb_c"], onescol_bf, c_ones_row, gofs=1)
    if dump == "n3":
        return dump_fm(n3)

    HB = T // 2
    W12 = []
    wslA = P.alloc(NCH * 512, BF16, "wslA2")
    wslB = P.alloc(32 * 128, BF16, "wslB2")
    stg = [P.alloc(512, F32, "l0c_stg0"), P.alloc(512, F32, "l0c_stg1")]
    for (w1d, w2d, b1n, b2n, outname, HBx) in (
            (h1w1_d, h1w2_d, "h1b1", "h1b2", "W1", T // 2),
            (h2w1_d, h2w2_d, "h2b1", "h2b2", "W2", T // 4)):
        Wx = P.alloc(NCH * T, BF16, outname)
        Wx.ap = Wx.ap.rearrange("p (c t) -> p c t", c=NCH)
        W12.append(Wx)
        hbuf = P.alloc(32 * HBx, BF16, "hbuf2")
        hbuf.ap = hbuf.ap.rearrange("p (c t) -> p c t", c=32)
        for half in range(T // HBx):
            t0 = half * HBx
            for ds in range(8):
                wA = load_weight(P, w1d, NCH, 512, "w1", slot=wslA, stage=stg, col0=ds * 512)
                for dsub in range(4):
                    hc = ds * 4 + dsub
                    for blk in range(HBx // 512):
                        sl = slice(t0 + blk * 512, t0 + (blk + 1) * 512)
                        bank = P.banks[(hc * 2 + blk) % 4]
                        for c in range(NCH):
                            P.emit("t", lambda E, c=c, dsub=dsub, sl=sl, bank=bank, wA=wA:
                                   E.matmul(bank.ap, wA.ap[:, c, dsub * 128:(dsub + 1) * 128],
                                            n3.ap[:, c, sl], start=(c == 0),
                                            stop=(c == NCH - 1)), reads=[wA, n3], writes=[bank])
                        act(P, hbuf, hbuf.ap[:, hc, blk * 512:(blk + 1) * 512], [bank],
                            bank.ap, AF.Gelu, bias=vec[b1n].ap[:, hc:hc + 1])
            for ds3 in range(8):
                wC = load_weight(P, w2d, 32, 128, "w2", slot=wslB, stage=stg, col0=ds3 * 128)
                dc = ds3
                for blk in range(HBx // 512):
                    sl = slice(t0 + blk * 512, t0 + (blk + 1) * 512)
                    bank = P.banks[4 + blk % 2]
                    for c in range(32):
                        P.emit("t", lambda E, c=c, sl=sl, t0=t0, bank=bank, wC=wC,
                               hbuf=hbuf: E.matmul(
                            bank.ap, wC.ap[:, c, 0:128],
                            hbuf.ap[:, c, sl.start - t0:sl.stop - t0],
                            start=(c == 0), stop=(c == 31)), reads=[wC, hbuf], writes=[bank])
                    P.emit("v", lambda E, dc=dc, sl=sl, bank=bank, b2n=b2n, Wx=Wx:
                           E.tensor_scalar(
                        Wx.ap[:, dc, sl], bank.ap, vec[b2n].ap[:, dc:dc + 1], None,
                        ALU.add, ALU.bypass), reads=[bank, vec[b2n]], writes=[Wx])
        P.free(hbuf)
        if dump == "n3b" and outname == "W1":
            return dump_fm(n3)
    W1b, W2b = W12
    P.free(wslA, wslB, stg[0], stg[1])
    if dump == "n3c":
        return dump_fm(n3)
    if dump == "w12":
        return dump_fm(W1b)

    ntm = P.alloc(16 * D, BF16, "ntm")
    ntm.ap = ntm.ap.rearrange("p (k m) -> p k m", k=16)
    for tt in range(16):
        bank = P.banks[tt % 2]
        bankb = bank.ap.bitcast(BF16)
        for c in range(NCH):
            P.emit("t", lambda E, tt=tt, c=c, bankb=bankb: E.transpose(
                bankb[:, c * 128:(c + 1) * 128], n3.ap[:, c, tt * 128:(tt + 1) * 128],
                ident), reads=[n3, cbf], writes=[bank])
        P.emit("v", lambda E, tt=tt, bankb=bankb: E.tensor_copy(
            ntm.ap[:, tt, :], bankb[:, 0:1024]), reads=[bank], writes=[ntm])
    P.free(n3)

    offs = []
    off = 0
    for i in range(16):
        offs.append(off)
        off += T - 512 * (i // 4)
    w3t = P.alloc(off, BF16, "w3t")
    for i in range(16):
        z0 = 512 * (i // 4)
        for zb in range(z0, T, 512):
            bank = P.banks[(zb // 512) % 4]
            for c in range(NCH):
                P.emit("t", lambda E, i=i, zb=zb, c=c, bank=bank: E.matmul(
                    bank.ap, W2b.ap[:, c, i * 128:(i + 1) * 128],
                    W1b.ap[:, c, zb:zb + 512], start=(c == 0), stop=(c == NCH - 1)),
                    reads=[W1b, W2b], writes=[bank])
            dsl = slice(offs[i] + (zb - z0), offs[i] + (zb - z0) + 512)
            if zb == z0:
                P.emit("v", lambda E, i=i, dsl=dsl, bank=bank: E.tensor_tensor(
                    w3t.ap[:, dsl], bank.ap, masks[i % 4], ALU.mult),
                    reads=[bank, cbf], writes=[w3t])
            else:
                P.emit("v", lambda E, dsl=dsl, bank=bank: E.tensor_copy(
                    w3t.ap[:, dsl], bank.ap), reads=[bank], writes=[w3t])
    P.free(W1b, W2b)

    x1 = P.alloc(NCH * T, BF16, "x1")
    x1.ap = x1.ap.rearrange("p (c t) -> p c t", c=NCH)
    ttile = P.alloc(D, F32, "ttile")
    tst = P.alloc(8, F32, "t_stats")
    gb_bc = P.alloc(D, F32, "t_gbc")
    bb_bc = P.alloc(D, F32, "t_bbc")
    t2b = P.alloc(D, BF16, "t2b")
    for blk in range(2):
        sl = slice(blk * 512, (blk + 1) * 512)
        bank = P.banks[6]
        P.emit("t", lambda E, sl=sl, bank=bank: E.matmul(
            bank.ap, c_ones_row.ap, vec["lng_r"].ap[:, sl], start=True, stop=True),
            reads=[vec["lng_r"], c_ones_row], writes=[bank])
        P.emit("v", lambda E, sl=sl, bank=bank: E.tensor_copy(gb_bc.ap[:, sl], bank.ap),
               reads=[bank], writes=[gb_bc])
        P.emit("t", lambda E, sl=sl, bank=bank: E.matmul(
            bank.ap, c_ones_row.ap, vec["lnb_r"].ap[:, sl], start=True, stop=True),
            reads=[vec["lnb_r"], c_ones_row], writes=[bank])
        P.emit("v", lambda E, sl=sl, bank=bank: E.tensor_copy(bb_bc.ap[:, sl], bank.ap),
               reads=[bank], writes=[bb_bc])
    for zt in range(16):
        for mb in range(2):
            msl = slice(mb * 512, (mb + 1) * 512)
            bank = P.banks[mb]
            first = True
            for i in range(zt + 1):
                z0 = 512 * (i // 4)
                lsl = slice(offs[i] + zt * 128 - z0, offs[i] + zt * 128 - z0 + 128)
                P.emit("t", lambda E, i=i, zt=zt, lsl=lsl, msl=msl, bank=bank,
                       first=first: E.matmul(bank.ap, w3t.ap[:, lsl], ntm.ap[:, i, msl],
                                             start=first, stop=(i == zt)),
                       reads=[w3t, ntm], writes=[bank])
                first = False
            act(P, ttile, ttile.ap[:, msl], [bank], bank.ap, AF.Gelu)
        P.emit("v", lambda E: E.tensor_reduce(tst.ap[:, 0:1], ttile.ap, axis=AX.X,
               op=ALU.add), reads=[ttile], writes=[tst])
        act(P, t2b, t2b.ap, [ttile], ttile.ap, AF.Square,
            accum_ap=tst.ap[:, 1:2], accum_buf=tst)
        P.emit("v", lambda E: E.tensor_scalar(tst.ap[:, 0:1], tst.ap[:, 0:1], 1.0 / D,
               None, ALU.mult, ALU.bypass), reads=[tst], writes=[tst])
        P.emit("v", lambda E: E.tensor_scalar(tst.ap[:, 1:2], tst.ap[:, 1:2], 1.0 / D,
               1e-5, ALU.mult, ALU.add), reads=[tst], writes=[tst])
        P.emit("v", lambda E: E.tensor_tensor(tst.ap[:, 2:3], tst.ap[:, 0:1],
               tst.ap[:, 0:1], ALU.mult), reads=[tst], writes=[tst])
        P.emit("v", lambda E: E.tensor_tensor(tst.ap[:, 1:2], tst.ap[:, 1:2],
               tst.ap[:, 2:3], ALU.subtract), reads=[tst], writes=[tst])
        P.emit("s", lambda E: E.activation(tst.ap[:, 1:2], tst.ap[:, 1:2], AF.Sqrt),
               reads=[tst], writes=[tst])
        P.emit("v", lambda E: E.reciprocal(tst.ap[:, 1:2], tst.ap[:, 1:2]),
               reads=[tst], writes=[tst])
        P.emit("v", lambda E: E.tensor_scalar(ttile.ap, ttile.ap, tst.ap[:, 0:1],
               tst.ap[:, 1:2], ALU.subtract, ALU.mult), reads=[ttile, tst], writes=[ttile])
        P.emit("v", lambda E: E.tensor_tensor(ttile.ap, ttile.ap, gb_bc.ap, ALU.mult),
               reads=[ttile, gb_bc], writes=[ttile])
        P.emit("v", lambda E: E.tensor_tensor(t2b.ap, ttile.ap, bb_bc.ap, ALU.add),
               reads=[ttile, bb_bc], writes=[t2b])
        for c in range(NCH):
            bank = P.banks[2 + c % 2]
            bankb = bank.ap.bitcast(BF16)
            P.emit("t", lambda E, c=c, bankb=bankb: E.transpose(
                bankb[:, 0:128], t2b.ap[:, c * 128:(c + 1) * 128], ident),
                reads=[t2b, cbf], writes=[bank])
            P.emit("v", lambda E, c=c, zt=zt, bankb=bankb: E.tensor_tensor(
                x1.ap[:, c, zt * 128:(zt + 1) * 128], xn.ap[:, c, 1 + zt * 128:1 + (zt + 1) * 128],
                bankb[:, 0:128], ALU.add), reads=[xn, bank], writes=[x1])
    P.free(w3t, ntm, ttile, tst, gb_bc, bb_bc, t2b)
    if dump == "x1":
        return dump_fm(x1)

    internal_ln(P, x1, x1, vec["lng_c"], vec["lnb_c"], onescol_bf, c_ones_row, gofs=0)
    hbuf = P.alloc(32 * HB, BF16, "hbuf3")
    hbuf.ap = hbuf.ap.rearrange("p (c t) -> p c t", c=32)
    wslA = P.alloc(NCH * 512, BF16, "wslA3")
    wslB = P.alloc(32 * 128, BF16, "wslB3")
    stg = [P.alloc(512, F32, "l0f_stg0"), P.alloc(512, F32, "l0f_stg1")]
    xst = P.alloc(HB, F32, "xst")
    gst = P.alloc(HB, BF16, "gst")
    yst = P.alloc(HB, F32, "yst")
    u1 = P.alloc(512, F32, "l0f_t1")
    for half in range(2):
        t0 = half * HB
        for ds in range(8):
            wA = load_weight(P, fmw1_d, NCH, 512, "fmw1", slot=wslA, stage=stg, col0=ds * 512)
            for dsub in range(4):
                hc = ds * 4 + dsub
                for blk in range(HB // 512):
                    sl = slice(t0 + blk * 512, t0 + (blk + 1) * 512)
                    bank = P.banks[(hc * 2 + blk) % 4]
                    for c in range(NCH):
                        P.emit("t", lambda E, c=c, dsub=dsub, sl=sl, bank=bank, wA=wA:
                               E.matmul(bank.ap, wA.ap[:, c, dsub * 128:(dsub + 1) * 128],
                                        x1.ap[:, c, sl], start=(c == 0),
                                        stop=(c == NCH - 1)), reads=[wA, x1], writes=[bank])
                    act(P, hbuf, hbuf.ap[:, hc, blk * 512:(blk + 1) * 512], [bank],
                        bank.ap, AF.Gelu, bias=vec["fmb1"].ap[:, hc:hc + 1])
        for ds3 in range(8):
            wC = load_weight(P, fmw2_d, 32, 128, "fmw2", slot=wslB, stage=stg, col0=ds3 * 128)
            dc = ds3
            P.dma("y", xst.ap, x_d[dc * 128:(dc + 1) * 128, t0:t0 + HB], writes=[xst])
            P.dma("y", gst.ap, g_d[dc * 128:(dc + 1) * 128, t0:t0 + HB], writes=[gst])
            for blk in range(HB // 512):
                sl = slice(t0 + blk * 512, t0 + (blk + 1) * 512)
                usl = slice(blk * 512, (blk + 1) * 512)
                bank = P.banks[4 + blk % 2]
                for c in range(32):
                    P.emit("t", lambda E, c=c, usl=usl, bank=bank, wC=wC, hbuf=hbuf:
                           E.matmul(
                        bank.ap, wC.ap[:, c, 0:128], hbuf.ap[:, c, usl.start:usl.stop],
                        start=(c == 0), stop=(c == 31)), reads=[wC, hbuf], writes=[bank])
                P.emit("v", lambda E, dc=dc, bank=bank: E.tensor_scalar(
                    u1.ap, bank.ap, vec["fmb2"].ap[:, dc:dc + 1], None, ALU.add,
                    ALU.bypass), reads=[bank, vec["fmb2"]], writes=[u1])
                P.emit("v", lambda E, dc=dc, sl=sl: E.tensor_tensor(
                    u1.ap, u1.ap, xn.ap[:, dc, 1 + sl.start:1 + sl.stop], ALU.add),
                    reads=[u1, xn], writes=[u1])
                P.emit("v", lambda E, usl=usl: E.tensor_tensor(
                    u1.ap, u1.ap, gst.ap[:, usl], ALU.mult), reads=[u1, gst], writes=[u1])
                P.emit("v", lambda E, usl=usl: E.tensor_tensor(
                    yst.ap[:, usl], u1.ap, xst.ap[:, usl], ALU.add),
                    reads=[u1, xst], writes=[yst])
            P.dma("y", y_d[dc * 128:(dc + 1) * 128, t0:t0 + HB], yst.ap, reads=[yst])
    last_y = ("y", "d", len(P.ops["y"]) - 1)
    P.pop_scope()
    return P, [last_y]


def l0_inputs(core_x, ins, pfx=""):
    rd = {
        "x_in": _f32(core_x),
        "h1w1": _f32(ins["L0_attn_h1_w1"]),
        "h1w2": _f32(ins["L0_attn_h1_w2"]),
        "h2w1": _f32(ins["L0_attn_h2_w1"]),
        "h2w2": _f32(ins["L0_attn_h2_w2"]),
        "fmw1": _f32(ins["L0_attn_fm_w1"]),
        "fmw2": _f32(ins["L0_attn_fm_w2"]),
        "mw1": _f32(ins["L0_ffn_w1"]),
        "mw2": _f32(ins["L0_ffn_w2"]),
        "mb2": _f32(ins["L0_ffn_b2"]),
        "vecs": l0_pack(ins).host,
        "cbf": _bf_consts_l0(),
    }
    return {("x_in" if k == "x_in" else pfx + k): v for k, v in rd.items()}





def _f32(a):
    return np.ascontiguousarray(a, dtype=np.float32)


def _col8(v):
    return _f32(np.asarray(v).reshape(NCH, 128).T)


def l1_inputs(core_x, ins, pfx=""):
    rd = {
        "x_in": _f32(core_x),
        "wkT": _f32(ins["L1_attn_Wk"].T),
        "wvT": _f32(ins["L1_attn_Wv"].T),
        "wrT": _f32(ins["L1_attn_Wr"].T),
        "woT": _f32(ins["L1_attn_Wo"].T),
        "frT": _f32(ins["L1_ffn_W_r"].T),
        "fkT": _f32(ins["L1_ffn_W_k"].T),
        "fvT": _f32(ins["L1_ffn_W_v"].T),
        "vecs": l1_pack(ins).host,
    }
    return {("x_in" if k == "x_in" else pfx + k): v for k, v in rd.items()}



def run_layer(build_fn, input_fn, x_bct, ins):
    P, finals = build_fn()
    nc = P.build(finals)
    in_maps = [input_fn(x_bct[i], ins) for i in range(NCORES)]
    res = run_bass_kernel_spmd(nc, in_maps, core_ids=list(range(NCORES)))
    return np.stack([r["y_out"] for r in res.results])


def build_fused():
    P = Prog("mk").make_arena()
    x_d = P.dram("x_in", [D, T])
    xb1 = P.dram("xb1", [D, T], kind="Internal")
    xb2 = P.dram("xb2", [D, T], kind="Internal")
    xb3 = P.dram("xb3", [D, T], kind="Internal")
    y_d = P.dram("y_out", [D, T], kind="ExternalOutput")
    build_l0(P, x_d, xb1, pfx="l0_")
    build_l1(P, xb1, xb2, pfx="l1_")
    build_l2(P, xb2, xb3, pfx="l2_")
    build_l3(P, xb3, y_d, pfx="l3_")
    last_y = ("y", "d", len(P.ops["y"]) - 1)
    return P, [last_y]


def fused_inputs(core_x, ins):
    m = {"x_in": _f32(core_x)}
    m.update(l0_inputs(core_x, ins, "l0_"))
    m.update(l1_inputs(core_x, ins, "l1_"))
    m.update(l2_inputs(core_x, ins, "l2_"))
    m.update(l3_inputs(core_x, ins, "l3_"))
    return m


def kernel(**ins):
    x = ins["L0_attn_x"]
    x_bct = np.ascontiguousarray(x.transpose(0, 2, 1)).astype(np.float32, copy=False)
    P, finals = build_fused()
    nc = P.build(finals)
    in_maps = [fused_inputs(x_bct[i], ins) for i in range(NCORES)]
    res = run_bass_kernel_spmd(nc, in_maps, core_ids=list(range(NCORES)))
    y = np.stack([r["y_out"] for r in res.results])
    return np.ascontiguousarray(y.transpose(0, 2, 1))


# version 04
# speedup vs baseline: 1.0090x; 1.0090x over previous
import numpy as np
import ml_dtypes
import concourse.bass as bass
import concourse.mybir as mybir
from concourse.bass_utils import run_bass_kernel_spmd

F32 = mybir.dt.float32
BF16 = mybir.dt.bfloat16
AF = mybir.ActivationFunctionType
ALU = mybir.AluOpType
AX = mybir.AxisListType

BF = ml_dtypes.bfloat16
NCORES = 8
T = 2048
D = 1024
NCH = D // 128

ENGINES = ("t", "v", "s", "p", "y")


class Buf:
    __slots__ = ("ap", "name", "wtok", "rtoks", "_addr", "_size", "_full")

    def __init__(self, ap, name=""):
        self.ap = ap
        self.name = name
        self.wtok = None
        self.rtoks = []
        self._addr = None
        self._size = None
        self._full = None


class Prog:
    def __init__(self, name="prog"):
        self.nc = bass.Bass("TRN2", target_bir_lowering=False, debug=False)
        self.name = name
        self.ops = {e: [] for e in ENGINES}
        self._ctxs = []

    def emit(self, eng, fn, reads=(), writes=(), kind="c", after=()):
        waits = set()

        def add(tok):
            if tok is None:
                return
            peng, pkind, _ = tok
            if peng == eng and pkind == "d" and kind == "d":
                return
            waits.add(tok)

        for b in reads:
            add(b.wtok)
        for b in writes:
            add(b.wtok)
            for t in b.rtoks:
                add(t)
        for t in after:
            add(t)
        seq = len(self.ops[eng])
        tok = (eng, kind, seq)
        self.ops[eng].append({"fn": fn, "kind": kind, "waits": waits, "inc": False})
        for b in writes:
            b.wtok = tok
            b.rtoks = []
        wset = set(id(b) for b in writes)
        for b in reads:
            if id(b) not in wset:
                b.rtoks.append(tok)
        return tok

    def dma(self, queue, dst_ap, src_ap, reads=(), writes=(), after=()):
        return self.emit(queue, lambda E: E.dma_start(dst_ap, src_ap),
                         reads=reads, writes=writes, kind="d", after=after)

    NDMASEM = 8
    NCSEM = 4

    def build(self, final_waits=()):
        nc = self.nc
        needed = set()
        for e in ENGINES:
            for op in self.ops[e]:
                needed.update(op["waits"])
        needed.update(final_waits)
        for (eng, kind, seq) in needed:
            self.ops[eng][seq]["inc"] = True
        for e in ENGINES:
            for op in self.ops[e]:
                if op["kind"] == "d":
                    op["inc"] = True
        semval = {}
        semslot = {}
        for e in ENGINES:
            cslot_count = [0] * self.NCSEM
            cix = 0
            dslot_count = [0] * self.NDMASEM
            dix = 0
            for i, op in enumerate(self.ops[e]):
                if op["inc"]:
                    if op["kind"] == "d":
                        s = dix % self.NDMASEM
                        dix += 1
                        dslot_count[s] += 16
                        semval[(e, "d", i)] = dslot_count[s]
                        semslot[(e, "d", i)] = s
                    else:
                        s = cix % self.NCSEM
                        cix += 1
                        cslot_count[s] += 1
                        semval[(e, "c", i)] = cslot_count[s]
                        semslot[(e, "c", i)] = s
        sems = {}
        for e in ENGINES:
            for s in range(self.NCSEM):
                sems[(e, "c", s)] = self._enter(nc.semaphore(f"sem_{e}c{s}"))
            for s in range(self.NDMASEM):
                sems[(e, "d", s)] = self._enter(nc.semaphore(f"sem_{e}d{s}"))
        block = self._enter(nc.Block())
        water = {}

        def sem_of(tok):
            pe, pk, _ = tok
            return (pe, pk, semslot.get(tok, 0))

        def run_engine(e, E):
            for i, op in enumerate(self.ops[e]):
                for tok in sorted(op["waits"]):
                    v = semval[tok]
                    sk = sem_of(tok)
                    key = (e,) + sk
                    if water.get(key, 0) >= v:
                        continue
                    water[key] = v
                    E.wait_ge(sems[sk], v)
                if op["kind"] == "d":
                    tok = (e, "d", i)
                    sk = sem_of(tok)
                    v = semval[tok] - 16
                    key = (e,) + sk
                    if v > 0 and water.get(key, 0) < v:
                        water[key] = v
                        E.wait_ge(sems[sk], v)
                inst = op["fn"](E)
                if op["inc"]:
                    k = op["kind"]
                    sk = sem_of((e, k, i))
                    inst.then_inc(sems[sk], 16 if k == "d" else 1)
            if e == "y":
                for tok in sorted(final_waits):
                    v = semval[tok]
                    sk = sem_of(tok)
                    key = (e,) + sk
                    if water.get(key, 0) < v:
                        water[key] = v
                        E.wait_ge(sems[sk], v)

        @block.tensor
        def _(E):
            run_engine("t", E)

        @block.vector
        def _(E):
            run_engine("v", E)

        @block.scalar
        def _(E):
            run_engine("s", E)

        @block.gpsimd
        def _(E):
            run_engine("p", E)

        @block.sync
        def _(E):
            run_engine("y", E)
        return nc

    def _enter(self, cm):
        v = cm.__enter__()
        self._ctxs.append(cm)
        return v

    def make_arena(self, words=53200):
        self.arena_t = self._enter(self.nc.sbuf_tensor("arena", [128, words], F32))
        self.freelist = [(0, words)]
        self.region_hist = []
        self.scopes = []
        self.banks = [Buf(self._enter(self.nc.psum_tensor(f"bank{i}", [128, 512], F32))[:, :],
                          f"bank{i}") for i in range(8)]
        return self

    def alloc(self, cols, dtype=F32, name=""):
        w = cols if dtype == F32 else (cols + 1) // 2
        w = (w + 3) & ~3
        for i, (a, sz) in enumerate(self.freelist):
            if sz >= w:
                if sz == w:
                    self.freelist.pop(i)
                else:
                    self.freelist[i] = (a + w, sz - w)
                region = self.arena_t[:, a:a + w]
                ap = region[:, :cols] if dtype == F32 else region.bitcast(BF16)[:, :cols]
                b = Buf(ap, name)
                b._addr, b._size = a, w
                keep = []
                for (ha, hs, toks) in self.region_hist:
                    if ha < a + w and a < ha + hs:
                        b.rtoks.extend(toks)
                        if ha < a:
                            keep.append((ha, a - ha, toks))
                        if ha + hs > a + w:
                            keep.append((a + w, ha + hs - (a + w), toks))
                    else:
                        keep.append((ha, hs, toks))
                self.region_hist = keep
                if self.scopes:
                    self.scopes[-1].append(b)
                return b
        raise AssertionError(f"arena OOM: {name} needs {w} words; free={self.freelist}")

    def push_scope(self):
        self.scopes.append([])

    def pop_scope(self):
        for b in self.scopes.pop():
            if b._addr is not None:
                self.free(b)

    def free(self, *bufs):
        for b in bufs:
            assert b._addr is not None, f"double free {b.name}"
            toks = list(b.rtoks) + ([b.wtok] if b.wtok else [])
            self.region_hist.append((b._addr, b._size, toks))
            self.freelist.append((b._addr, b._size))
            b._addr = None
            self.freelist.sort()
            merged = []
            for a, sz in self.freelist:
                if merged and merged[-1][0] + merged[-1][1] == a:
                    merged[-1] = (merged[-1][0], merged[-1][1] + sz)
                else:
                    merged.append((a, sz))
            self.freelist = [tuple(m) for m in merged]

    def dram(self, name, shape, dtype=F32, kind="ExternalInput"):
        return self.nc.dram_tensor(name, list(shape), dtype, kind=kind).ap()



def act(P, out_buf, out_ap, in_bufs, in_ap, func, bias=None, scale=None,
        accum_ap=None, accum_buf=None, extra_reads=()):
    kw = {}
    if bias is not None:
        kw["bias"] = bias
    if scale is not None:
        kw["scale"] = scale
    if accum_ap is not None:
        kw["accum_out"] = accum_ap
    wr = [out_buf] + ([accum_buf] if accum_buf is not None else [])
    return P.emit("s", lambda E: E.activation(out_ap, in_ap, func, **kw),
                  reads=list(in_bufs) + list(extra_reads), writes=wr)


def load_weight(P, dram_ap, nch_in, dout, name, queue="y", slot=None, stage=None,
                cast_eng="v", col0=0):
    if slot is None:
        wb = P.alloc(nch_in * dout, BF16, name)
        w3 = wb.ap.rearrange("p (c d) -> p c d", c=nch_in)
    else:
        wb = slot
        if wb._full is None:
            flat = slot.ap
            if len(flat.shape) == 3:
                flat = flat.rearrange("p c d -> p (c d)")
            wb._full = flat
        flat = wb._full
        assert flat.shape[1] >= nch_in * dout, f"wslot too small for {name}"
        w3 = flat[:, :nch_in * dout].rearrange("p (c d) -> p c d", c=nch_in)
    own_stage = stage is None
    if own_stage:
        stage = [P.alloc(dout, F32, name + "_stg0"), P.alloc(dout, F32, name + "_stg1")]
    for c in range(nch_in):
        sg = stage[c % len(stage)]
        P.dma(queue, sg.ap[:, :dout], dram_ap[c * 128:(c + 1) * 128, col0:col0 + dout],
              writes=[sg])
        P.emit(cast_eng, lambda E, c=c, sg=sg: E.tensor_copy(w3[:, c, :], sg.ap[:, :dout]),
               reads=[sg], writes=[wb])
    if own_stage:
        P.free(*stage)
    if slot is None:
        wb.ap = w3
        return wb
    return _SharedView(w3, name, wb)


class WSlots:

    def __init__(self, P, cols, name, n=2):
        self.bufs = [P.alloc(cols, BF16, f"{name}{i}") for i in range(n)]
        self.i = -1

    def __call__(self):
        self.i += 1
        return self.bufs[self.i % len(self.bufs)]


def load_vec(P, dram_ap, name, n=NCH):
    v = P.alloc(n, F32, name)
    P.dma("p", v.ap, dram_ap, writes=[v])
    return v


class VecPack:
    NCOLS = 256

    def __init__(self, ncols=None):
        self.NCOLS = ncols or VecPack.NCOLS
        self.host = np.zeros((128, self.NCOLS), np.float32)
        self.off = 0
        self.slices = {}

    def add_cols(self, name, arr):
        arr = np.asarray(arr, np.float32)
        assert arr.shape[0] == 128
        k = arr.shape[1]
        self.host[:, self.off:self.off + k] = arr
        self.slices[name] = ("c", self.off, k)
        self.off += k
        return self

    def add_row(self, name, arr):
        arr = np.asarray(arr, np.float32).reshape(-1)
        k = len(arr)
        self.host[0, self.off:self.off + k] = arr
        self.slices[name] = ("r", self.off, k)
        self.off += k
        return self

    def add_block(self, name, arr):
        arr = np.asarray(arr, np.float32)
        p, k = arr.shape
        self.host[0:p, self.off:self.off + k] = arr
        self.slices[name] = (("b", p), self.off, k)
        self.off += k
        return self

    def load(self, P, dram_ap):
        buf = P.alloc(self.NCOLS, F32, "vecpack")
        P.dma("y", buf.ap, dram_ap, writes=[buf])
        out = {}
        for name, (kind, off, k) in self.slices.items():
            if kind == "r":
                ap = buf.ap[0:1, off:off + k]
            elif isinstance(kind, tuple) and kind[0] == "b":
                ap = buf.ap[0:kind[1], off:off + k]
            else:
                ap = buf.ap[:, off:off + k]
            out[name] = _SharedView(ap, name, buf)
        return out


class _SharedView:
    __slots__ = ("ap", "name", "parent")

    def __init__(self, ap, name, parent):
        self.ap = ap
        self.name = name
        self.parent = parent

    @property
    def wtok(self):
        return self.parent.wtok

    @property
    def rtoks(self):
        return self.parent.rtoks


def load_row(P, dram_ap, cols, name, dtype=F32, queue="p"):
    v = P.alloc(cols, dtype, name)
    row = v.ap[0:1, :]
    P.dma(queue, row, dram_ap, writes=[v])
    v.ap = row
    return v


def proj(P, wb, rhs_fn, rhs_bufs, nch_in, dout_chunks, evac, blkw=512, nblk=None,
         bank_off=0, nbanks=None):
    nblk = nblk if nblk is not None else (T // blkw)
    banks = P.banks[bank_off:(bank_off + nbanks if nbanks else len(P.banks))]
    for dc in range(dout_chunks):
        for blk in range(nblk):
            bank = banks[(dc * nblk + blk) % len(banks)]
            for c in range(nch_in):
                P.emit("t", lambda E, c=c, dc=dc, blk=blk, bank=bank: E.matmul(
                    bank.ap[:, :blkw], wb.ap[:, c, dc * 128:(dc + 1) * 128],
                    rhs_fn(c, blk), start=(c == 0), stop=(c == nch_in - 1)),
                    reads=[wb] + list(rhs_bufs), writes=[bank])
            evac(dc, blk, bank)


def global_ln(P, x, xn, c_ones_col, c_ones_row):
    x3 = x.ap
    xn3 = xn.ap
    P.emit("v", lambda E: E.memset(xn3[:, :, 0:1], 0.0), writes=[xn])
    sq = P.alloc(T, F32, "ln_sq")
    rows1 = P.alloc(2 * T, F32, "ln_rows")
    rows2 = rows1
    A_bc = P.alloc(T, F32, "ln_Abc")
    B_bc = P.alloc(T, F32, "ln_Bbc")
    bsum = P.banks[:4]
    bsq = P.banks[4:8]
    for c in range(NCH):
        act(P, sq, sq.ap, [x], x3[:, c, :], AF.Square)
        for blk in range(4):
            sl = slice(blk * 512, (blk + 1) * 512)
            P.emit("t", lambda E, c=c, blk=blk, sl=sl: E.matmul(
                bsum[blk].ap[0:1, :], c_ones_col.ap, x3[:, c, sl],
                start=(c == 0), stop=(c == NCH - 1)), reads=[x, c_ones_col], writes=[bsum[blk]])
            P.emit("t", lambda E, c=c, blk=blk, sl=sl: E.matmul(
                bsq[blk].ap[0:1, :], c_ones_col.ap, sq.ap[:, sl],
                start=(c == 0), stop=(c == NCH - 1)), reads=[sq, c_ones_col], writes=[bsq[blk]])
    mean = rows1.ap[0:1, 0 * T:1 * T]
    msq = rows1.ap[0:1, 1 * T:2 * T]
    sd = msq
    A = msq
    for blk in range(4):
        sl = slice(blk * 512, (blk + 1) * 512)
        P.emit("v", lambda E, blk=blk, sl=sl: E.tensor_scalar(
            mean[:, sl], bsum[blk].ap[0:1, :], 1.0 / D, None, ALU.mult, ALU.bypass),
            reads=[bsum[blk]], writes=[rows1])
        P.emit("v", lambda E, blk=blk, sl=sl: E.tensor_scalar(
            msq[:, sl], bsq[blk].ap[0:1, :], 1.0 / D, 1e-5, ALU.mult, ALU.add),
            reads=[bsq[blk]], writes=[rows1])
    m2row = sq.ap[0:1, 0:T]
    P.emit("v", lambda E: E.tensor_tensor(m2row, mean, mean, ALU.mult), reads=[rows1], writes=[sq])
    P.emit("v", lambda E: E.tensor_tensor(msq, msq, m2row, ALU.subtract),
           reads=[rows1, sq], writes=[rows1])
    P.emit("s", lambda E: E.activation(sd, msq, AF.Sqrt), reads=[rows1], writes=[rows1])
    P.emit("v", lambda E: E.reciprocal(A, sd), reads=[rows1], writes=[rows1])
    P.emit("v", lambda E: E.scalar_tensor_tensor(mean, mean, -1.0, A, ALU.mult, ALU.mult),
           reads=[rows1], writes=[rows1])
    for (src_buf, srow, dst) in ((rows2, A, A_bc), (rows1, mean, B_bc)):
        for blk in range(4):
            sl = slice(blk * 512, (blk + 1) * 512)
            bank = P.banks[blk]
            P.emit("t", lambda E, srow=srow, sl=sl, bank=bank: E.matmul(
                bank.ap[:, :], c_ones_row.ap, srow[:, sl], start=True, stop=True),
                reads=[src_buf, c_ones_row], writes=[bank])
            P.emit("v", lambda E, dst=dst, sl=sl, bank=bank: E.tensor_copy(
                dst.ap[:, sl], bank.ap), reads=[bank], writes=[dst])
    for c in range(NCH):
        P.emit("v", lambda E, c=c: E.tensor_tensor(sq.ap, x3[:, c, :], A_bc.ap, ALU.mult),
               reads=[x, A_bc], writes=[sq])
        P.emit("v", lambda E, c=c: E.tensor_tensor(xn3[:, c, 1:T + 1], sq.ap, B_bc.ap, ALU.add),
               reads=[sq, B_bc], writes=[xn])
    P.free(sq, rows1, A_bc, B_bc)


def alloc_xn(P):
    xn = P.alloc(NCH * (T + 1) + 1, BF16, "xn")
    xn.ap = xn.ap[:, :NCH * (T + 1)].rearrange("p (c t) -> p c t", c=NCH)
    return xn


def shared_token_mix(P, xn, mu_cols_names):
    outs = []
    for mu, name in mu_cols_names:
        b = P.alloc(NCH * T, BF16, name)
        b.ap = b.ap.rearrange("p (c t) -> p c t", c=NCH)
        outs.append(b)
    d = P.alloc(T, F32, "mix_d")
    for c in range(NCH):
        cur = xn.ap[:, c, 1:T + 1]
        prev = xn.ap[:, c, 0:T]
        P.emit("v", lambda E, cur=cur, prev=prev: E.tensor_tensor(d.ap, cur, prev, ALU.subtract),
               reads=[xn], writes=[d])
        for (mu, name), ob in zip(mu_cols_names, outs):
            P.emit("v", lambda E, c=c, prev=prev, mu=mu, ob=ob: E.scalar_tensor_tensor(
                ob.ap[:, c, :], d.ap, mu.ap[:, c:c + 1], prev, ALU.mult, ALU.add),
                reads=[d, xn, mu], writes=[ob])
    P.free(d)
    return outs


def fused_update_evac(P, x_d, y_d, g3, xst, yst, sgn=1.0):
    tmp = P.alloc(512, F32, "upd_tmp")
    state = {"dc": -1}

    def evac(dc, blk, bank):
        if state["dc"] != dc:
            state["dc"] = dc
            xb = xst[dc % 2]
            P.dma("y", xb.ap, x_d[dc * 128:(dc + 1) * 128, :], writes=[xb])
        xb = xst[dc % 2]
        yb = yst[dc % 2]
        sl = slice(blk * 512, (blk + 1) * 512)
        P.emit("v", lambda E: E.tensor_tensor(tmp.ap, bank.ap, g3.ap[:, dc, sl], ALU.mult),
               reads=[bank, g3], writes=[tmp])
        P.emit("v", lambda E: E.tensor_tensor(yb.ap[:, sl], tmp.ap, xb.ap[:, sl], ALU.add),
               reads=[tmp, xb], writes=[yb])
        if blk == 3:
            P.dma("y", y_d[dc * 128:(dc + 1) * 128, :], yb.ap, reads=[yb])
    return evac



def l1_pack(ins=None):
    vp = VecPack()
    z8 = np.zeros((128, NCH), np.float32)
    get = (lambda k: _col8(k)) if ins is not None else (lambda k: z8)
    if ins is not None:
        vp.add_cols("v_td", _col8(ins["L1_attn_time_decay"]))
        vp.add_cols("v_tf", _col8(ins["L1_attn_time_first"]))
        vp.add_cols("v_mk", _col8(ins["L1_attn_mix_k"].reshape(-1)))
        vp.add_cols("v_mv", _col8(ins["L1_attn_mix_v"].reshape(-1)))
        vp.add_cols("v_mr", _col8(ins["L1_attn_mix_r"].reshape(-1)))
        vp.add_cols("v_mur", _col8(ins["L1_ffn_mu_r"]))
        vp.add_cols("v_muk", _col8(ins["L1_ffn_mu_k"]))
    else:
        for n in ["v_td", "v_tf", "v_mk", "v_mv", "v_mr", "v_mur", "v_muk"]:
            vp.add_cols(n, z8)
    vp.add_cols("ones_col", np.ones((128, 1), np.float32))
    vp.add_row("ones_row", np.ones(128, np.float32))
    return vp


def build_l1(P=None, x_d=None, y_d=None, pfx="", dump=None):
    if P is None:
        P = Prog("l1").make_arena()
        x_d = P.dram("x_in", [D, T])
        y_d = P.dram("y_out", [D, T], kind="ExternalOutput")
    P.push_scope()
    W = {n: P.dram(pfx + n, [D, D]) for n in ["wkT", "wvT", "wrT", "woT", "frT", "fkT", "fvT"]}
    vecs_d = P.dram(pfx + "vecs", [128, VecPack.NCOLS])
    vec = l1_pack().load(P, vecs_d)
    c_ones_col = vec["ones_col"]
    c_ones_row = vec["ones_row"]

    def dump_vp(tag):
        parent = vec["v_td"].parent
        outb = P.alloc(64, F32, "dumpout")
        P.emit("v", lambda E: E.tensor_copy(outb.ap, parent.ap[:, 0:64]), reads=[parent], writes=[outb])
        P.dma("y", y_d[0:128, 0:64], outb.ap, reads=[outb])
        return P, [("y", "d", len(P.ops["y"]) - 1)]

    def dump_out(sb, guard=0):
        outb = P.alloc(T, F32, "dumpout")
        for c in range(NCH):
            sl = sb.ap[:, c, guard:T + guard] if guard else sb.ap[:, c, :]
            P.emit("v", lambda E, sl=sl: E.tensor_copy(outb.ap, sl), reads=[sb], writes=[outb])
            P.dma("y", y_d[c * 128:(c + 1) * 128, :], outb.ap, reads=[outb])
        return P, [("y", "d", len(P.ops["y"]) - 1)]


    xn = alloc_xn(P)
    g3 = P.alloc(NCH * T, BF16, "g")
    g3.ap = g3.ap.rearrange("p (c t) -> p c t", c=NCH)
    wslot = WSlots(P, NCH * D, "wslot")

    x = P.alloc(NCH * T, F32, "x")
    x.ap = x.ap.rearrange("p (c t) -> p c t", c=NCH)
    P.dma("y", x.ap, x_d.rearrange("(c p) t -> p c t", p=128), writes=[x])
    global_ln(P, x, xn, c_ones_col, c_ones_row)
    P.free(x)

    xr, xk = shared_token_mix(P, xn, [(vec["v_mur"], "xr"), (vec["v_muk"], "xk")])
    wbuf = load_weight(P, W["frT"], NCH, D, "w_fr", slot=wslot())
    rs = P.alloc(NCH * T, BF16, "rs")
    rs.ap = rs.ap.rearrange("p (c t) -> p c t", c=NCH)

    def evac_sigmoid(dc, blk, bank):
        sl = slice(blk * 512, (blk + 1) * 512)
        act(P, rs, rs.ap[:, dc, sl], [bank], bank.ap, AF.Sigmoid)
    proj(P, wbuf, lambda c, blk: xr.ap[:, c, blk * 512:(blk + 1) * 512], [xr],
         NCH, NCH, evac_sigmoid)
    P.free(xr)

    wbuf = load_weight(P, W["fkT"], NCH, D, "w_fk", slot=wslot())
    ka = P.alloc(NCH * T, BF16, "ka")
    ka.ap = ka.ap.rearrange("p (c t) -> p c t", c=NCH)
    relu_t = P.alloc(512, F32, "relu_t")

    def evac_relu_sq(dc, blk, bank):
        sl = slice(blk * 512, (blk + 1) * 512)
        act(P, relu_t, relu_t.ap, [bank], bank.ap, AF.Relu)
        P.emit("v", lambda E: E.tensor_tensor(ka.ap[:, dc, sl], relu_t.ap, relu_t.ap, ALU.mult),
               reads=[relu_t], writes=[ka])
    proj(P, wbuf, lambda c, blk: xk.ap[:, c, blk * 512:(blk + 1) * 512], [xk],
         NCH, NCH, evac_relu_sq)
    P.free(xk, relu_t)

    wbuf = load_weight(P, W["fvT"], NCH, D, "w_fv", slot=wslot())
    wv_t = P.alloc(512, F32, "wv_t")

    def evac_g(dc, blk, bank):
        sl = slice(blk * 512, (blk + 1) * 512)
        P.emit("v", lambda E: E.tensor_tensor(wv_t.ap, bank.ap, rs.ap[:, dc, sl], ALU.mult),
               reads=[bank, rs], writes=[wv_t])
        act(P, g3, g3.ap[:, dc, sl], [wv_t], wv_t.ap, AF.Silu)
    proj(P, wbuf, lambda c, blk: ka.ap[:, c, blk * 512:(blk + 1) * 512], [ka],
         NCH, NCH, evac_g)
    P.free(ka, rs, wv_t)

    if dump == "g":
        return dump_out(g3)
    kin, vin, rin = shared_token_mix(P, xn, [(vec["v_mk"], "kin"), (vec["v_mv"], "vin"),
                                             (vec["v_mr"], "rin")])
    if dump == "kin":
        return dump_out(kin)
    P.free(xn)
    wbuf = load_weight(P, W["wrT"], NCH, D, "w_r", slot=wslot())
    rs2 = P.alloc(NCH * T, BF16, "rs2")
    rs2.ap = rs2.ap.rearrange("p (c t) -> p c t", c=NCH)

    def evac_sig2(dc, blk, bank):
        sl = slice(blk * 512, (blk + 1) * 512)
        act(P, rs2, rs2.ap[:, dc, sl], [bank], bank.ap, AF.Sigmoid)
    proj(P, wbuf, lambda c, blk: rin.ap[:, c, blk * 512:(blk + 1) * 512], [rin],
         NCH, NCH, evac_sig2)
    if dump == "rsig":
        return dump_out(rs2)
    P.free(rin)
    wbuf = load_weight(P, W["wvT"], NCH, D, "w_v", slot=wslot())
    vsb = P.alloc(NCH * T, BF16, "vsb")
    vsb.ap = vsb.ap.rearrange("p (c t) -> p c t", c=NCH)

    def evac_copy(dc, blk, bank):
        sl = slice(blk * 512, (blk + 1) * 512)
        P.emit("v", lambda E: E.tensor_copy(vsb.ap[:, dc, sl], bank.ap), reads=[bank],
               writes=[vsb])
    proj(P, wbuf, lambda c, blk: vin.ap[:, c, blk * 512:(blk + 1) * 512], [vin],
         NCH, NCH, evac_copy)
    if dump == "vsb":
        return dump_out(vsb)
    P.free(vin)

    a_col = P.alloc(NCH, F32, "a_col")
    eu_col = P.alloc(NCH, F32, "eu_col")
    act(P, a_col, a_col.ap, [vec["v_td"]], vec["v_td"].ap, AF.Exp)
    act(P, a_col, a_col.ap, [a_col], a_col.ap, AF.Exp, scale=-1.0)
    act(P, eu_col, eu_col.ap, [vec["v_tf"]], vec["v_tf"].ap, AF.Exp)

    wbuf = load_weight(P, W["wkT"], NCH, D, "w_k", slot=wslot())
    ebuf = P.alloc(T, F32, "ebuf")
    enbuf = P.alloc(T, F32, "enbuf")
    num = P.alloc(T + 4, F32, "num")
    den = P.alloc(T + 4, F32, "den")
    P.emit("v", lambda E: E.memset(num.ap[:, 0:1], 0.0), writes=[num])
    P.emit("v", lambda E: E.memset(den.ap[:, 0:1], 0.0), writes=[den])

    ndc = NCH
    if dump and dump.startswith("ndc"):
        ndc = int(dump[3:])
    for dc in range(ndc):
        for blk in range(4):
            bank = P.banks[blk]
            sl = slice(blk * 512, (blk + 1) * 512)
            for c in range(NCH):
                P.emit("t", lambda E, c=c, dc=dc, bank=bank, sl=sl: E.matmul(
                    bank.ap, wbuf.ap[:, c, dc * 128:(dc + 1) * 128], kin.ap[:, c, sl],
                    start=(c == 0), stop=(c == NCH - 1)), reads=[wbuf, kin], writes=[bank])
            act(P, ebuf, ebuf.ap[:, sl], [bank], bank.ap, AF.Exp)
        P.emit("v", lambda E, dc=dc: E.tensor_tensor(enbuf.ap, ebuf.ap, vsb.ap[:, dc, :],
                                                     ALU.mult), reads=[ebuf, vsb], writes=[enbuf])
        a_b = a_col.ap[:, dc:dc + 1].broadcast_to([128, T])
        P.emit("v", lambda E, a_b=a_b: E.tensor_tensor_scan(
            num.ap[:, 1:T + 1], a_b, enbuf.ap, 0.0, ALU.mult, ALU.add),
            reads=[a_col, enbuf], writes=[num])
        P.emit("v", lambda E, a_b=a_b: E.tensor_tensor_scan(
            den.ap[:, 1:T + 1], a_b, ebuf.ap, 0.0, ALU.mult, ALU.add),
            reads=[a_col, ebuf], writes=[den])
        P.emit("v", lambda E, dc=dc: E.scalar_tensor_tensor(
            enbuf.ap, enbuf.ap, eu_col.ap[:, dc:dc + 1], num.ap[:, 0:T], ALU.mult, ALU.add),
            reads=[enbuf, eu_col, num], writes=[enbuf])
        P.emit("v", lambda E, dc=dc: E.scalar_tensor_tensor(
            ebuf.ap, ebuf.ap, eu_col.ap[:, dc:dc + 1], den.ap[:, 0:T], ALU.mult, ALU.add),
            reads=[ebuf, eu_col, den], writes=[ebuf])
        P.emit("v", lambda E: E.reciprocal(ebuf.ap, ebuf.ap), reads=[ebuf], writes=[ebuf])
        P.emit("v", lambda E: E.tensor_tensor(enbuf.ap, enbuf.ap, ebuf.ap, ALU.mult),
               reads=[enbuf, ebuf], writes=[enbuf])
        P.emit("v", lambda E, dc=dc: E.tensor_tensor(rs2.ap[:, dc, :], rs2.ap[:, dc, :],
                                                     enbuf.ap, ALU.mult),
               reads=[rs2, enbuf], writes=[rs2])
    P.free(ebuf, enbuf, num, den, kin, vsb)

    if dump == "rwkv" or (dump and dump.startswith("ndc")):
        return dump_out(rs2)
    wbuf = load_weight(P, W["woT"], NCH, D, "w_o", slot=wslot())
    xst = [P.alloc(T, F32, f"xst{i}") for i in range(2)]
    yst = [P.alloc(T, F32, f"yst{i}") for i in range(2)]
    evac = fused_update_evac(P, x_d, y_d, g3, xst, yst)
    out_toks = []

    def evac_upd(dc, blk, bank):
        evac(dc, blk, bank)
    proj(P, wbuf, lambda c, blk: rs2.ap[:, c, blk * 512:(blk + 1) * 512], [rs2],
         NCH, NCH, evac_upd)
    last_y = ("y", "d", len(P.ops["y"]) - 1)
    P.pop_scope()
    return P, [last_y]




def l3_pack(ins=None):
    vp = VecPack(512)
    if ins is not None:
        vp.add_cols("qn", np.tile(np.asarray(ins["L3_attn_qn_w"], np.float32), 2)[:, None])
        vp.add_cols("kn", np.tile(np.asarray(ins["L3_attn_kn_w"], np.float32), 2)[:, None])
        vp.add_cols("b_proj", _col8(ins["L3_attn_b_proj"]))
        vp.add_cols("b_qkv", _f32(np.asarray(ins["L3_attn_b_qkv"]).reshape(24, 128).T))
        vp.add_cols("w1s", _f32(np.asarray(ins["L3_ffn_w1_s"]).reshape(32, 128).T))
        vp.add_cols("w2s", _f32(np.asarray(ins["L3_ffn_w2_s"]).reshape(32, 128).T))
        vp.add_cols("h1s", _f32(np.asarray(ins["L3_ffn_h1_s"]).reshape(32, 128).T))
        vp.add_cols("h2s", _f32(np.asarray(ins["L3_ffn_h2_s"]).reshape(32, 128).T))
        vp.add_cols("w3s", _col8(ins["L3_ffn_w3_s"]))
    else:
        z = np.zeros((128, 1), np.float32)
        vp.add_cols("qn", z); vp.add_cols("kn", z)
        vp.add_cols("b_proj", np.zeros((128, 8), np.float32))
        vp.add_cols("b_qkv", np.zeros((128, 24), np.float32))
        for n in ("w1s", "w2s", "h1s", "h2s"):
            vp.add_cols(n, np.zeros((128, 32), np.float32))
        vp.add_cols("w3s", np.zeros((128, 8), np.float32))
    vp.add_cols("ones_col", np.ones((128, 1), np.float32))
    vp.add_row("ones_row", np.ones(128, np.float32))
    b2 = np.zeros((2, 128), np.float32)
    b2[0, 0:64] = 1
    b2[1, 64:128] = 1
    vp.add_block("B2", b2)
    return vp


def _bf_consts():
    c = np.zeros((128, 192), BF)
    c[:, 0:128] = np.eye(128, dtype=BF)
    c[0:64, 128] = 1
    c[64:128, 129] = 1
    return c


def build_l3(P=None, x_d=None, y_d=None, pfx="", dump=None):
    if P is None:
        P = Prog("l3").make_arena()
        x_d = P.dram("x_in", [D, T])
        y_d = P.dram("y_out", [D, T], kind="ExternalOutput")
    P.push_scope()
    s_d = P.dram(pfx + "s_spill", [D, T], kind="Internal")
    wqkv_d = P.dram(pfx + "wqkvT", [D, 3 * D])
    wproj_d = P.dram(pfx + "wprojT", [D, D])
    w1_d = P.dram(pfx + "w1T", [D, 4096])
    w2_d = P.dram(pfx + "w2T", [D, 4096])
    h1_d = P.dram(pfx + "h1T", [4096, 4096])
    h2_d = P.dram(pfx + "h2T", [4096, 4096])
    w3_d = P.dram(pfx + "w3T", [4096, D])
    vecs_d = P.dram(pfx + "vecs", [128, 512])
    cbf_d = P.dram(pfx + "cbf", [128, 192], BF16)
    cid_d = P.dram(pfx + "cid32", [128, 128])

    vec = l3_pack().load(P, vecs_d)
    c_ones_col = vec["ones_col"]
    c_ones_row = vec["ones_row"]
    cbf = P.alloc(192, BF16, "cbf")
    P.dma("y", cbf.ap, cbf_d, writes=[cbf])
    cid = P.alloc(128, F32, "cid32")
    P.dma("y", cid.ap, cid_d, writes=[cid])
    ident = cbf.ap[:, 0:128]
    E2 = cbf.ap[:, 128:130]
    obf = P.alloc(128, BF16, "ones_row_bf")
    P.emit("v", lambda E: E.memset(obf.ap[0:1, :], 1.0), writes=[obf])

    def dump_fm(sb3, nch=NCH, guard=0):
        outb = P.alloc(T, F32, "dumpout")
        for c in range(nch):
            sl = sb3.ap[:, c, guard:T + guard]
            P.emit("v", lambda E, sl=sl: E.tensor_copy(outb.ap, sl), reads=[sb3], writes=[outb])
            P.dma("y", y_d[(c % NCH) * 128:((c % NCH) + 1) * 128, :], outb.ap, reads=[outb])
        return P, [("y", "d", len(P.ops["y"]) - 1)]

    xn = alloc_xn(P)
    x = P.alloc(NCH * T, F32, "x")
    x.ap = x.ap.rearrange("p (c t) -> p c t", c=NCH)
    P.dma("y", x.ap, x_d.rearrange("(c p) t -> p c t", p=128), writes=[x])
    global_ln(P, x, xn, c_ones_col, c_ones_row)
    P.free(x)

    qkv = P.alloc(24 * T, BF16, "qkv")
    qkv.ap = qkv.ap.rearrange("p (c t) -> p c t", c=24)
    wslot = P.alloc(NCH * D, BF16, "wslot")
    for j in range(3):
        wb = load_weight(P, wqkv_d, NCH, D, "wqkv", slot=wslot, col0=j * D)

        def evac_qkv(dc, blk, bank, j=j):
            gdc = j * NCH + dc
            sl = slice(blk * 512, (blk + 1) * 512)
            P.emit("v", lambda E, gdc=gdc, sl=sl, bank=bank: E.tensor_scalar(
                qkv.ap[:, gdc, sl], bank.ap, vec["b_qkv"].ap[:, gdc:gdc + 1], None,
                ALU.add, ALU.bypass), reads=[bank, vec["b_qkv"]], writes=[qkv])
        proj(P, wb, lambda c, blk: xn.ap[:, c, 1 + blk * 512:1 + (blk + 1) * 512], [xn],
             NCH, NCH, evac_qkv)
    P.free(wslot)
    if dump == "qkv":
        return dump_fm(qkv, 24)

    sqb = P.alloc(T, BF16, "rms_sq")
    B2 = vec["B2"]
    rows2h = P.alloc(T, F32, "rms_rows")
    rbc = P.alloc(512, F32, "rms_rbc")
    tmul = P.alloc(512, F32, "rms_tmul")

    for kind in ("q", "k"):
        wcol = vec["qn" if kind == "q" else "kn"]
        for c in range(NCH):
            ch = c if kind == "q" else NCH + c
            for blk in range(4):
                sl = slice(blk * 512, (blk + 1) * 512)
                act(P, sqb, sqb.ap[:, sl], [qkv], qkv.ap[:, ch, sl], AF.Square)
                P.emit("t", lambda E, sl=sl, blk=blk: E.matmul(
                    P.banks[blk].ap[0:2, :], E2, sqb.ap[:, sl], start=True, stop=True),
                    reads=[sqb, cbf], writes=[P.banks[blk]])
            for blk in range(4):
                sl = slice(blk * 512, (blk + 1) * 512)
                P.emit("v", lambda E, sl=sl, blk=blk: E.tensor_scalar(
                    rows2h.ap[0:2, sl], P.banks[blk].ap[0:2, :], 1.0 / 64, 1e-6,
                    ALU.mult, ALU.add), reads=[P.banks[blk]], writes=[rows2h])
            P.emit("s", lambda E: E.activation(rows2h.ap[0:2, :], rows2h.ap[0:2, :],
                   AF.Sqrt), reads=[rows2h], writes=[rows2h])
            P.emit("v", lambda E: E.reciprocal(rows2h.ap[0:2, :], rows2h.ap[0:2, :]),
                   reads=[rows2h], writes=[rows2h])
            for blk in range(4):
                sl = slice(blk * 512, (blk + 1) * 512)
                bank = P.banks[4 + blk % 2]
                P.emit("t", lambda E, sl=sl, bank=bank: E.matmul(
                    bank.ap, B2.ap[0:2, :], rows2h.ap[0:2, sl], start=True, stop=True),
                    reads=[rows2h, B2], writes=[bank])
                P.emit("v", lambda E, ch=ch, sl=sl, bank=bank: E.tensor_tensor(
                    tmul.ap, qkv.ap[:, ch, sl], bank.ap, ALU.mult),
                    reads=[qkv, bank], writes=[tmul])
                if kind == "q":
                    P.emit("v", lambda E, ch=ch, sl=sl: E.tensor_scalar(
                        qkv.ap[:, ch, sl], tmul.ap, wcol.ap, 0.125, ALU.mult, ALU.mult),
                        reads=[tmul, wcol], writes=[qkv])
                else:
                    P.emit("v", lambda E, ch=ch, sl=sl: E.tensor_scalar(
                        qkv.ap[:, ch, sl], tmul.ap, wcol.ap, None, ALU.mult, ALU.bypass),
                        reads=[tmul, wcol], writes=[qkv])
    P.free(sqb, rows2h, rbc, tmul)
    if dump == "qkvn":
        return dump_fm(qkv, 24)

    vtm = P.alloc(16 * 16 * 64, BF16, "vtm")
    vtm.ap = vtm.ap.rearrange("p (h k e) -> p h k e", h=16, k=16)
    for h in range(16):
        base = (h % 2) * 64
        vch = 16 + h // 2
        bank = P.banks[h % 2]
        bankb = bank.ap.bitcast(BF16)
        for kt in range(16):
            vsl = qkv.ap[base:base + 64, vch, kt * 128:(kt + 1) * 128]
            idn = ident[base:base + 64, base:base + 64]
            P.emit("t", lambda E, vsl=vsl, idn=idn, kt=kt, bankb=bankb: E.transpose(
                bankb[:, kt * 64:(kt + 1) * 64], vsl, idn), reads=[qkv, cbf], writes=[bank])
        P.emit("v", lambda E, h=h, bankb=bankb: E.tensor_copy(
            vtm.ap[:, h, :, :], bankb.rearrange("p (k e) -> p k e", k=16)),
            reads=[bank], writes=[vtm])

    negmb = P.alloc(8, F32, "negm_cols")
    nrow = P.alloc(2 * 2048, BF16, "negm_row")
    rrow = P.alloc(2 * T, F32, "recip_rows")
    dcol = P.alloc(8, F32, "den_cols")
    scrap = P.alloc(512, F32, "exp_scrap")
    ptb = P.alloc(T, BF16, "pt_sb")
    rbc2 = P.alloc(1024, F32, "o_rbc")

    for hp in range(8):
        for par in range(2):
            h = 2 * hp + par
            base = par * 64
            qch = h // 2
            kch = NCH + h // 2
            Q = qkv.ap[base:base + 64, qch, :]
            K = qkv.ap[base:base + 64, kch, :]
            nrow_h = nrow.ap[0:1, par * 2048:par * 2048 + 2048]
            rrow_h = rrow.ap[0:1, par * T:par * T + T]
            for qt in range(16):
                bank = P.banks[qt % 2]
                for kblk in range(4):
                    ksl = slice(kblk * 512, (kblk + 1) * 512)
                    P.emit("t", lambda E, qt=qt, ksl=ksl, bank=bank, Q=Q, K=K: E.matmul(
                        bank.ap, Q[:, qt * 128:(qt + 1) * 128], K[:, ksl],
                        start=True, stop=True), reads=[qkv], writes=[bank])
                    P.emit("v", lambda E, kblk=kblk, bank=bank: E.tensor_reduce(
                        negmb.ap[:, kblk:kblk + 1], bank.ap, axis=AX.X, op=ALU.max,
                        negate=True), reads=[bank], writes=[negmb])
                P.emit("v", lambda E: E.tensor_reduce(negmb.ap[:, 4:5],
                       negmb.ap[:, 0:4], axis=AX.X, op=ALU.min), reads=[negmb], writes=[negmb])
                P.emit("v", lambda E: E.tensor_copy(negmb.ap[:, 5:6].bitcast(BF16)[:, 0:1],
                       negmb.ap[:, 4:5]), reads=[negmb], writes=[negmb])
                P.emit("v", lambda E: E.tensor_copy(negmb.ap[:, 6:7],
                       negmb.ap[:, 5:6].bitcast(BF16)[:, 0:1]), reads=[negmb], writes=[negmb])
                for kblk in range(4):
                    ksl = slice(kblk * 512, (kblk + 1) * 512)
                    bank = P.banks[2 + kblk % 2]
                    P.emit("t", lambda E, qt=qt, ksl=ksl, bank=bank, Q=Q, K=K: E.matmul(
                        bank.ap, Q[:, qt * 128:(qt + 1) * 128], K[:, ksl],
                        start=True, stop=True), reads=[qkv], writes=[bank])
                    act(P, scrap, scrap.ap, [bank, negmb], bank.ap, AF.Exp,
                        bias=negmb.ap[:, 6:7], accum_ap=dcol.ap[:, kblk:kblk + 1],
                        accum_buf=dcol)
                P.emit("v", lambda E: E.tensor_reduce(dcol.ap[:, 4:5], dcol.ap[:, 0:4],
                       axis=AX.X, op=ALU.add), reads=[dcol], writes=[dcol])
                P.emit("v", lambda E: E.reciprocal(dcol.ap[:, 5:6], dcol.ap[:, 4:5]),
                       reads=[dcol], writes=[dcol])
                P.emit("t", lambda E: E.transpose(P.banks[6].ap[0:1, 0:128],
                       negmb.ap[:, 6:7], cid.ap), reads=[negmb, cid], writes=[P.banks[6]])
                P.emit("v", lambda E, qt=qt, nrow_h=nrow_h: E.tensor_copy(
                    nrow_h[0:1, qt * 128:(qt + 1) * 128],
                    P.banks[6].ap[0:1, 0:128]), reads=[P.banks[6]], writes=[nrow])
                P.emit("t", lambda E: E.transpose(P.banks[7].ap[0:1, 0:128],
                       dcol.ap[:, 5:6], cid.ap), reads=[dcol, cid], writes=[P.banks[7]])
                P.emit("v", lambda E, qt=qt, rrow_h=rrow_h: E.tensor_copy(
                    rrow_h[0:1, qt * 128:(qt + 1) * 128], P.banks[7].ap[0:1, 0:128]),
                    reads=[P.banks[7]], writes=[rrow])
            for kt in range(16):
                for qblk in range(4):
                    qsl = slice(qblk * 512, (qblk + 1) * 512)
                    bank = P.banks[qblk % 2]
                    P.emit("t", lambda E, kt=kt, qsl=qsl, bank=bank, Q=Q, K=K: E.matmul(
                        bank.ap, K[:, kt * 128:(kt + 1) * 128], Q[:, qsl],
                        start=True, stop=False), reads=[qkv], writes=[bank])
                    P.emit("t", lambda E, qsl=qsl, bank=bank, nrow_h=nrow_h: E.matmul(
                        bank.ap, obf.ap[0:1, :], nrow_h[0:1, qsl], start=False, stop=True),
                        reads=[obf, nrow], writes=[bank])
                    act(P, ptb, ptb.ap[:, qsl], [bank], bank.ap, AF.Exp)
                for qblk in range(4):
                    qsl = slice(qblk * 512, (qblk + 1) * 512)
                    obank = P.banks[4 + qblk]
                    P.emit("t", lambda E, kt=kt, qsl=qsl, obank=obank, h=h, par=par:
                           E.matmul(obank.ap[par * 64:par * 64 + 64, :],
                                    vtm.ap[:, h, kt, :], ptb.ap[:, qsl],
                                    start=(kt == 0), stop=(kt == 15),
                                    tile_position=(0, par * 64)),
                           reads=[vtm, ptb], writes=[obank])
        for qblk in range(4):
            qsl = slice(qblk * 512, (qblk + 1) * 512)
            obank = P.banks[4 + qblk]
            for par in range(2):
                rrow_h = rrow.ap[0:1, par * T + qblk * 512:par * T + (qblk + 1) * 512]
                bank = P.banks[par]
                P.emit("t", lambda E, rrow_h=rrow_h, bank=bank: E.matmul(
                    bank.ap, c_ones_row.ap, rrow_h, start=True, stop=True),
                    reads=[rrow, c_ones_row], writes=[bank])
                P.emit("v", lambda E, bank=bank: E.tensor_copy(rbc2.ap[:, 0:512], bank.ap),
                       reads=[bank], writes=[rbc2])
                P.emit("v", lambda E, hp=hp, par=par, qsl=qsl, obank=obank: E.tensor_tensor(
                    qkv.ap[par * 64:par * 64 + 64, 16 + hp, qsl],
                    obank.ap[par * 64:par * 64 + 64, :], rbc2.ap[par * 64:par * 64 + 64, 0:512],
                    ALU.mult), reads=[obank, rbc2], writes=[qkv])
    P.free(vtm, negmb, nrow, rrow, dcol, scrap, ptb, rbc2)
    if dump == "ofm":
        return dump_fm(qkv, 24)

    wslot = P.alloc(NCH * D, BF16, "wslot")
    wb = load_weight(P, wproj_d, NCH, D, "wproj", slot=wslot)
    sst = P.alloc(T, F32, "s_stage")

    def evac_s(dc, blk, bank):
        sl = slice(blk * 512, (blk + 1) * 512)
        P.emit("v", lambda E, dc=dc, sl=sl, bank=bank: E.tensor_scalar(
            sst.ap[:, sl], bank.ap, vec["b_proj"].ap[:, dc:dc + 1], None,
            ALU.add, ALU.bypass), reads=[bank, vec["b_proj"]], writes=[sst])
        if blk == 3:
            P.dma("y", s_d[dc * 128:(dc + 1) * 128, :], sst.ap, reads=[sst])
    proj(P, wb, lambda c, blk: qkv.ap[:, 16 + c, blk * 512:(blk + 1) * 512], [qkv],
         NCH, NCH, evac_s)
    P.free(qkv, wslot, sst)
    if dump == "sdone":
        tmpb = P.alloc(T, F32, "tmpb")
        for c in range(NCH):
            P.dma("y", tmpb.ap, s_d[c * 128:(c + 1) * 128, :], writes=[tmpb])
            P.dma("y", y_d[c * 128:(c + 1) * 128, :], tmpb.ap, reads=[tmpb])
        return P, [("y", "d", len(P.ops["y"]) - 1)]

    xn_d = P.dram(pfx + "xn_spill", [D, T], BF16, kind="Internal")
    for c in range(NCH):
        P.dma("y", xn_d[c * 128:(c + 1) * 128, :], xn.ap[:, c, 1:T + 1], reads=[xn])
    P.free(xn)

    HB = T // 2
    u1 = P.alloc(32 * HB, BF16, "u1")
    u1.ap = u1.ap.rearrange("p (c t) -> p c t", c=32)
    u2 = P.alloc(32 * HB, BF16, "u2")
    u2.ap = u2.ap.rearrange("p (c t) -> p c t", c=32)
    xnh = P.alloc(NCH * HB, BF16, "xnh")
    xnh.ap = xnh.ap.rearrange("p (c t) -> p c t", c=NCH)
    t1 = P.alloc(512, F32, "ffn_t1")
    t2 = P.alloc(512, F32, "ffn_t2")
    wslH = WSlots(P, 32 * 256, "wslH")
    wslA = wslH
    stg = [P.alloc(512, F32, "ffn_stg0"), P.alloc(512, F32, "ffn_stg1")]
    gst = P.alloc(512, BF16, "g_stage")
    xst = P.alloc(HB, F32, "xst")
    sst2 = P.alloc(HB, F32, "sst2")
    yst = P.alloc(HB, F32, "yst")
    NBH = HB // 512

    for half in range(2):
        t0 = half * HB
        for c in range(NCH):
            P.dma("y", xnh.ap[:, c, :], xn_d[c * 128:(c + 1) * 128, t0:t0 + HB],
                  writes=[xnh])
        for mi, (mat_d, scol) in enumerate(((w1_d, "w1s"), (w2_d, "w2s"))):
            for ds in range(8):
                wA = load_weight(P, mat_d, NCH, 512, "wA", slot=wslA(), stage=stg,
                                 col0=ds * 512)
                for dsub in range(4):
                    dc = ds * 4 + dsub
                    for blk in range(NBH):
                        usl = slice(blk * 512, (blk + 1) * 512)
                        bank = P.banks[(dc * NBH + blk) % 6]
                        for c in range(NCH):
                            P.emit("t", lambda E, c=c, dsub=dsub, usl=usl, bank=bank,
                                   wA=wA: E.matmul(
                                bank.ap, wA.ap[:, c, dsub * 128:(dsub + 1) * 128],
                                xnh.ap[:, c, usl], start=(c == 0), stop=(c == NCH - 1)),
                                reads=[wA, xnh], writes=[bank])
                        if mi == 0:
                            act(P, u1, u1.ap[:, dc, usl], [bank], bank.ap, AF.Silu,
                                scale=vec["w1s"].ap[:, dc:dc + 1])
                        else:
                            P.emit("v", lambda E, dc=dc, bank=bank: E.tensor_scalar(
                                t2.ap, bank.ap, vec["w2s"].ap[:, dc:dc + 1], None,
                                ALU.mult, ALU.bypass), reads=[bank, vec["w2s"]], writes=[t2])
                            P.emit("v", lambda E, dc=dc, usl=usl: E.tensor_tensor(
                                u1.ap[:, dc, usl], u1.ap[:, dc, usl], t2.ap, ALU.mult),
                                reads=[u1, t2], writes=[u1])
        for (mat_d, scol, src3, dst3) in ((h1_d, "h1s", u1, u2), (h2_d, "h2s", u2, u1)):
            for ds in range(16):
                wH = load_weight(P, mat_d, 32, 256, "wH", slot=wslH(), stage=stg,
                                 col0=ds * 256)
                for dsub in range(2):
                    dc = ds * 2 + dsub
                    for blk in range(NBH):
                        usl = slice(blk * 512, (blk + 1) * 512)
                        bank = P.banks[(dc * NBH + blk) % 6]
                        for c in range(32):
                            P.emit("t", lambda E, c=c, dsub=dsub, usl=usl, bank=bank,
                                   wH=wH, src3=src3: E.matmul(
                                bank.ap, wH.ap[:, c, dsub * 128:(dsub + 1) * 128],
                                src3.ap[:, c, usl], start=(c == 0), stop=(c == 31)),
                                reads=[wH, src3], writes=[bank])
                        act(P, dst3, dst3.ap[:, dc, usl], [bank], bank.ap, AF.Gelu,
                            scale=vec[scol].ap[:, dc:dc + 1])
        for ds in range(4):
            wC = load_weight(P, w3_d, 32, 256, "w3", slot=wslH(), stage=stg, col0=ds * 256)
            for dsub in range(2):
                dc = ds * 2 + dsub
                P.dma("y", xst.ap, x_d[dc * 128:(dc + 1) * 128, t0:t0 + HB], writes=[xst])
                P.dma("y", sst2.ap, s_d[dc * 128:(dc + 1) * 128, t0:t0 + HB], writes=[sst2])
                for blk in range(NBH):
                    usl = slice(blk * 512, (blk + 1) * 512)
                    bank = P.banks[(dc * NBH + blk) % 6]
                    for c in range(32):
                        P.emit("t", lambda E, c=c, dsub=dsub, usl=usl, bank=bank, wC=wC:
                               E.matmul(bank.ap, wC.ap[:, c, dsub * 128:(dsub + 1) * 128],
                                        u1.ap[:, c, usl], start=(c == 0), stop=(c == 31)),
                               reads=[wC, u1], writes=[bank])
                    act(P, gst, gst.ap[:, :], [bank], bank.ap, AF.Silu,
                        scale=vec["w3s"].ap[:, dc:dc + 1])
                    P.emit("v", lambda E, usl=usl: E.tensor_tensor(
                        t1.ap, gst.ap[:, :], sst2.ap[:, usl], ALU.mult),
                        reads=[gst, sst2], writes=[t1])
                    P.emit("v", lambda E, usl=usl: E.tensor_tensor(
                        yst.ap[:, usl], t1.ap, xst.ap[:, usl], ALU.add),
                        reads=[t1, xst], writes=[yst])
                P.dma("y", y_d[dc * 128:(dc + 1) * 128, t0:t0 + HB], yst.ap, reads=[yst])
    last_y = ("y", "d", len(P.ops["y"]) - 1)
    P.pop_scope()
    return P, [last_y]


def l3_inputs(core_x, ins, pfx=""):
    rd = {
        "x_in": _f32(core_x),
        "wqkvT": _f32(ins["L3_attn_w_qkv"].T),
        "wprojT": _f32(ins["L3_attn_w_proj"].T),
        "w1T": _f32(ins["L3_ffn_w1_w"].T),
        "w2T": _f32(ins["L3_ffn_w2_w"].T),
        "h1T": _f32(ins["L3_ffn_h1_w"].T),
        "h2T": _f32(ins["L3_ffn_h2_w"].T),
        "w3T": _f32(ins["L3_ffn_w3_w"].T),
        "vecs": l3_pack(ins).host,
        "cbf": _bf_consts(),
        "cid32": np.eye(128, dtype=np.float32),
    }
    return {("x_in" if k == "x_in" else pfx + k): v for k, v in rd.items()}




def l2_pack(ins=None):
    vp = VecPack(512)
    if ins is not None:
        vp.add_cols("gate_w", _f32(np.asarray(ins["L2_ffn_gate_w"]).reshape(NCH, 128, 8)
                                   .transpose(1, 0, 2).reshape(128, NCH * 8)))
        vp.add_cols("b_proj", _col8(ins["L2_attn_b_proj"]))
        vp.add_cols("snw", np.tile(np.asarray(ins["L2_attn_sub_norm_w"], np.float32), 2)[:, None])
        vp.add_row("lq1", ins["L2_attn_lambda_q1"])
        vp.add_row("lk1", ins["L2_attn_lambda_k1"])
        vp.add_row("lq2", ins["L2_attn_lambda_q2"])
        vp.add_row("lk2", ins["L2_attn_lambda_k2"])
    else:
        vp.add_cols("gate_w", np.zeros((128, NCH * 8), np.float32))
        vp.add_cols("b_proj", np.zeros((128, 8), np.float32))
        vp.add_cols("snw", np.zeros((128, 1), np.float32))
        for n in ("lq1", "lk1", "lq2", "lk2"):
            vp.add_row(n, np.zeros(32, np.float32))
    vp.add_cols("ones_col", np.ones((128, 1), np.float32))
    vp.add_row("ones_row", np.ones(128, np.float32))
    b2 = np.zeros((2, 128), np.float32)
    b2[0, 0:64] = 1
    b2[1, 64:128] = 1
    vp.add_block("B2", b2)
    return vp


def build_l2(P=None, x_d=None, y_d=None, pfx="", dump=None):
    if P is None:
        P = Prog("l2").make_arena()
        x_d = P.dram("x_in", [D, T])
        y_d = P.dram("y_out", [D, T], kind="ExternalOutput")
    P.push_scope()
    g_d = P.dram(pfx + "g_spill", [D, T], BF16, kind="Internal")
    wqkv_d = P.dram(pfx + "wqkvT4", [D, 4 * D])
    wproj_d = P.dram(pfx + "wprojT", [D, D])
    ew1_d = P.dram(pfx + "ew1", [8, D, 2048])
    ew2_d = P.dram(pfx + "ew2", [8, D, 2048])
    ew3_d = P.dram(pfx + "ew3", [8, 2048, D])
    sw1_d = P.dram(pfx + "sw1", [D, 2048])
    sw2_d = P.dram(pfx + "sw2", [D, 2048])
    sw3_d = P.dram(pfx + "sw3", [2048, D])
    vecs_d = P.dram(pfx + "vecs", [128, 512])
    cbf_d = P.dram(pfx + "cbf", [128, 192], BF16)
    cid_d = P.dram(pfx + "cid32", [128, 128])

    vec = l2_pack().load(P, vecs_d)
    c_ones_col = vec["ones_col"]
    c_ones_row = vec["ones_row"]
    B2 = vec["B2"]
    cbf = P.alloc(192, BF16, "cbf")
    P.dma("y", cbf.ap, cbf_d, writes=[cbf])
    cid = P.alloc(128, F32, "cid32")
    P.dma("y", cid.ap, cid_d, writes=[cid])
    ident = cbf.ap[:, 0:128]
    E2 = cbf.ap[:, 128:130]
    obf = P.alloc(128, BF16, "ones_row_bf")
    P.emit("v", lambda E: E.memset(obf.ap[0:1, :], 1.0), writes=[obf])

    def dump_fm(sb3, nch=NCH, guard=0):
        outb = P.alloc(T, F32, "dumpout")
        for c in range(nch):
            sl = sb3.ap[:, c, guard:T + guard]
            P.emit("v", lambda E, sl=sl: E.tensor_copy(outb.ap, sl), reads=[sb3], writes=[outb])
            P.dma("y", y_d[(c % NCH) * 128:((c % NCH) + 1) * 128, :], outb.ap, reads=[outb])
        return P, [("y", "d", len(P.ops["y"]) - 1)]

    xn = alloc_xn(P)
    x = P.alloc(NCH * T, F32, "x")
    x.ap = x.ap.rearrange("p (c t) -> p c t", c=NCH)
    P.dma("y", x.ap, x_d.rearrange("(c p) t -> p c t", p=128), writes=[x])
    global_ln(P, x, xn, c_ones_col, c_ones_row)
    P.free(x)

    gwb = P.alloc(NCH * 8, BF16, "gate_w_bf")
    gwb.ap = gwb.ap.rearrange("p (c e) -> p c e", c=NCH)
    P.emit("v", lambda E: E.tensor_copy(
        gwb.ap, vec["gate_w"].ap.rearrange("p (c e) -> p c e", c=NCH)),
        reads=[vec["gate_w"]], writes=[gwb])
    logits = P.alloc(16 * 8, F32, "logits")
    logits.ap = logits.ap.rearrange("p (n e) -> p n e", n=16)
    for tt in range(16):
        bank = P.banks[tt % 2]
        for c in range(NCH):
            P.emit("t", lambda E, tt=tt, c=c, bank=bank: E.matmul(
                bank.ap[:, 0:8], xn.ap[:, c, 1 + tt * 128:1 + (tt + 1) * 128], gwb.ap[:, c, :],
                start=(c == 0), stop=(c == NCH - 1)), reads=[xn, gwb], writes=[bank])
        P.emit("v", lambda E, tt=tt, bank=bank: E.tensor_copy(logits.ap[:, tt, :],
               bank.ap[:, 0:8]), reads=[bank], writes=[logits])
    r_m1 = P.alloc(16, F32, "r_m1")
    r_m2 = P.alloc(16, F32, "r_m2")
    r_g1 = P.alloc(16, F32, "r_g1")
    r_g2 = P.alloc(16, F32, "r_g2")
    msk1 = P.alloc(16 * 8, F32, "msk1")
    msk1.ap = msk1.ap.rearrange("p (n e) -> p n e", n=16)
    msk2 = P.alloc(16 * 8, F32, "msk2")
    msk2.ap = msk2.ap.rearrange("p (n e) -> p n e", n=16)
    lg2 = P.alloc(16 * 8, F32, "lg2")
    lg2.ap = lg2.ap.rearrange("p (n e) -> p n e", n=16)
    gates = P.alloc(16 * 8, F32, "gates")
    gates.ap = gates.ap.rearrange("p (n e) -> p n e", n=16)
    P.emit("v", lambda E: E.tensor_reduce(r_m1.ap, logits.ap, axis=AX.X, op=ALU.max),
           reads=[logits], writes=[r_m1])
    P.emit("v", lambda E: E.tensor_tensor(msk1.ap, logits.ap,
           r_m1.ap.unsqueeze(2).broadcast_to([128, 16, 8]), ALU.is_equal),
           reads=[logits, r_m1], writes=[msk1])
    P.emit("v", lambda E: E.scalar_tensor_tensor(lg2.ap, msk1.ap, -1e30, logits.ap,
           ALU.mult, ALU.add), reads=[msk1, logits], writes=[lg2])
    P.emit("v", lambda E: E.tensor_reduce(r_m2.ap, lg2.ap, axis=AX.X, op=ALU.max),
           reads=[lg2], writes=[r_m2])
    P.emit("v", lambda E: E.tensor_tensor(msk2.ap, lg2.ap,
           r_m2.ap.unsqueeze(2).broadcast_to([128, 16, 8]), ALU.is_equal),
           reads=[lg2, r_m2], writes=[msk2])
    P.emit("v", lambda E: E.tensor_tensor(r_g1.ap, r_m1.ap, r_m2.ap, ALU.subtract),
           reads=[r_m1, r_m2], writes=[r_g1])
    act(P, r_g1, r_g1.ap, [r_g1], r_g1.ap, AF.Sigmoid)
    P.emit("v", lambda E: E.tensor_scalar(r_g2.ap, r_g1.ap, -1.0, 1.0, ALU.mult, ALU.add),
           reads=[r_g1], writes=[r_g2])
    P.emit("v", lambda E: E.tensor_tensor(gates.ap, msk1.ap,
           r_g1.ap.unsqueeze(2).broadcast_to([128, 16, 8]), ALU.mult),
           reads=[msk1, r_g1], writes=[gates])
    P.emit("v", lambda E: E.tensor_tensor(msk2.ap, msk2.ap,
           r_g2.ap.unsqueeze(2).broadcast_to([128, 16, 8]), ALU.mult),
           reads=[msk2, r_g2], writes=[msk2])
    P.emit("v", lambda E: E.tensor_tensor(gates.ap, gates.ap, msk2.ap, ALU.add),
           reads=[gates, msk2], writes=[gates])
    grow = P.alloc(T, F32, "gate_row")
    if dump == "gates":
        outb = P.alloc(T, F32, "dumpout")
        for tt in range(16):
            bank = P.banks[2 + tt % 2]
            P.emit("t", lambda E, tt=tt, bank=bank: E.transpose(
                bank.ap[0:1, 0:128], gates.ap[:, tt, 0:1], cid.ap),
                reads=[gates, cid], writes=[bank])
            P.emit("v", lambda E, tt=tt, bank=bank: E.tensor_copy(
                grow.ap[0:1, tt * 128:(tt + 1) * 128], bank.ap[0:1, 0:128]),
                reads=[bank], writes=[grow])
        P.emit("v", lambda E: E.memset(outb.ap, 0.0), writes=[outb])
        P.emit("v", lambda E: E.tensor_copy(outb.ap[0:1, :], grow.ap[0:1, :]),
               reads=[grow], writes=[outb])
        P.dma("y", y_d[0:128, :], outb.ap, reads=[outb])
        return P, [("y", "d", len(P.ops["y"]) - 1)]
    P.free(gwb, logits, r_m1, r_m2, r_g1, r_g2, msk1, msk2, lg2)

    yacc = P.alloc(NCH * T, F32, "yacc")
    yacc.ap = yacc.ap.rearrange("p (c t) -> p c t", c=NCH)
    abuf = P.alloc(16 * T, BF16, "abuf")
    abuf.ap = abuf.ap.rearrange("p (c t) -> p c t", c=16)
    wsl_a = WSlots(P, NCH * 256, "wsl_a")
    wsl_b = WSlots(P, NCH * 256, "wsl_b")
    wsl_3 = wsl_a
    stg = [P.alloc(512, F32, "l2_stg0"), P.alloc(512, F32, "l2_stg1")]
    t1 = P.alloc(512, F32, "l2_t1")
    t1b = P.alloc(512, F32, "l2_t1b")
    gbc = P.alloc(T, F32, "l2_gbc")

    for e in range(9):
        shared = (e == 8)
        w1d = sw1_d if shared else ew1_d[e]
        w2d = sw2_d if shared else ew2_d[e]
        w3d = sw3_d if shared else ew3_d[e]
        for ds in range(8):
            wA = load_weight(P, w1d, NCH, 256, "ew1", slot=wsl_a(), stage=stg, col0=ds * 256)
            wB = load_weight(P, w2d, NCH, 256, "ew2", slot=wsl_b(), stage=stg, col0=ds * 256)
            for dsub in range(2):
                hc = ds * 2 + dsub
                for blk in range(4):
                    sl = slice(blk * 512, (blk + 1) * 512)
                    bankA = P.banks[(hc * 4 + blk) % 3]
                    bankB = P.banks[3 + (hc * 4 + blk) % 3]
                    for c in range(NCH):
                        P.emit("t", lambda E, c=c, dsub=dsub, sl=sl, bankA=bankA, wA=wA:
                               E.matmul(bankA.ap, wA.ap[:, c, dsub * 128:(dsub + 1) * 128],
                                        xn.ap[:, c, 1 + sl.start:1 + sl.stop],
                                        start=(c == 0), stop=(c == NCH - 1)),
                               reads=[wA, xn], writes=[bankA])
                    for c in range(NCH):
                        P.emit("t", lambda E, c=c, dsub=dsub, sl=sl, bankB=bankB, wB=wB:
                               E.matmul(bankB.ap, wB.ap[:, c, dsub * 128:(dsub + 1) * 128],
                                        xn.ap[:, c, 1 + sl.start:1 + sl.stop],
                                        start=(c == 0), stop=(c == NCH - 1)),
                               reads=[wB, xn], writes=[bankB])
                    tx = t1 if (hc * 4 + blk) % 2 == 0 else t1b
                    act(P, tx, tx.ap, [bankA], bankA.ap, AF.Silu)
                    P.emit("v", lambda E, hc=hc, sl=sl, bankB=bankB, tx=tx: E.tensor_tensor(
                        abuf.ap[:, hc, sl], tx.ap, bankB.ap, ALU.mult),
                        reads=[tx, bankB], writes=[abuf])
        if not shared:
            for tt in range(16):
                bank = P.banks[4 + tt % 2]
                P.emit("t", lambda E, tt=tt, e=e, bank=bank: E.transpose(
                    bank.ap[0:1, 0:128], gates.ap[:, tt, e:e + 1], cid.ap),
                    reads=[gates, cid], writes=[bank])
                P.emit("v", lambda E, tt=tt, bank=bank: E.tensor_copy(
                    grow.ap[0:1, tt * 128:(tt + 1) * 128], bank.ap[0:1, 0:128]),
                    reads=[bank], writes=[grow])
            for blk in range(4):
                sl = slice(blk * 512, (blk + 1) * 512)
                bank = P.banks[4 + blk % 2]
                P.emit("t", lambda E, sl=sl, bank=bank: E.matmul(
                    bank.ap, c_ones_row.ap, grow.ap[0:1, sl],
                    start=True, stop=True), reads=[grow, c_ones_row], writes=[bank])
                P.emit("v", lambda E, sl=sl, bank=bank: E.tensor_copy(gbc.ap[:, sl], bank.ap),
                       reads=[bank], writes=[gbc])
        for ds3 in range(8):
            wC = load_weight(P, w3d, 16, 128, "ew3", slot=wsl_3(), stage=stg, col0=ds3 * 128)
            for dsub in range(1):
                dc = ds3
                for blk in range(4):
                    sl = slice(blk * 512, (blk + 1) * 512)
                    bank = P.banks[6 + (dc * 4 + blk) % 2]
                    for c in range(16):
                        P.emit("t", lambda E, c=c, dsub=dsub, sl=sl, bank=bank, wC=wC:
                               E.matmul(bank.ap, wC.ap[:, c, dsub * 128:(dsub + 1) * 128],
                                        abuf.ap[:, c, sl], start=(c == 0), stop=(c == 15)),
                               reads=[wC, abuf], writes=[bank])
                    if shared:
                        P.emit("v", lambda E, dc=dc, sl=sl, bank=bank: E.tensor_tensor(
                            yacc.ap[:, dc, sl], yacc.ap[:, dc, sl], bank.ap, ALU.add),
                            reads=[yacc, bank], writes=[yacc])
                    elif e == 0:
                        P.emit("v", lambda E, dc=dc, sl=sl, bank=bank: E.tensor_tensor(
                            yacc.ap[:, dc, sl], bank.ap, gbc.ap[:, sl], ALU.mult),
                            reads=[bank, gbc], writes=[yacc])
                    else:
                        P.emit("v", lambda E, sl=sl, bank=bank: E.tensor_tensor(
                            t1.ap, bank.ap, gbc.ap[:, sl], ALU.mult),
                            reads=[bank, gbc], writes=[t1])
                        P.emit("v", lambda E, dc=dc, sl=sl: E.tensor_tensor(
                            yacc.ap[:, dc, sl], yacc.ap[:, dc, sl], t1.ap, ALU.add),
                            reads=[yacc, t1], writes=[yacc])
    P.free(grow, t1, t1b, gbc)
    gstage = P.alloc(T, BF16, "gstage")
    for c in range(NCH):
        act(P, gstage, gstage.ap, [yacc], yacc.ap[:, c, :], AF.Silu)
        P.dma("y", g_d[c * 128:(c + 1) * 128, :], gstage.ap, reads=[gstage])
    P.free(yacc, abuf, *wsl_a.bufs, *wsl_b.bufs, stg[0], stg[1],
           gstage, gates)
    if dump == "gdone":
        tmpb = P.alloc(T, F32, "tmpb")
        gtmp = P.alloc(T, BF16, "gtmp")
        for c in range(NCH):
            P.dma("y", gtmp.ap, g_d[c * 128:(c + 1) * 128, :], writes=[gtmp])
            P.emit("v", lambda E: E.tensor_copy(tmpb.ap, gtmp.ap), reads=[gtmp], writes=[tmpb])
            P.dma("y", y_d[c * 128:(c + 1) * 128, :], tmpb.ap, reads=[tmpb])
        return P, [("y", "d", len(P.ops["y"]) - 1)]

    lamc = P.alloc(64, F32, "lam")
    P.emit("v", lambda E: E.tensor_tensor(lamc.ap[0:1, 0:32], vec["lq1"].ap,
           vec["lk1"].ap, ALU.mult), reads=[vec["lq1"], vec["lk1"]], writes=[lamc])
    P.emit("v", lambda E: E.tensor_tensor(lamc.ap[0:1, 32:64], vec["lq2"].ap,
           vec["lk2"].ap, ALU.mult), reads=[vec["lq2"], vec["lk2"]], writes=[lamc])
    P.emit("v", lambda E: E.tensor_reduce(lamc.ap[0:1, 0:1], lamc.ap[0:1, 0:32],
           axis=AX.X, op=ALU.add), reads=[lamc], writes=[lamc])
    P.emit("v", lambda E: E.tensor_reduce(lamc.ap[0:1, 1:2], lamc.ap[0:1, 32:64],
           axis=AX.X, op=ALU.add), reads=[lamc], writes=[lamc])
    act(P, lamc, lamc.ap[0:1, 2:4], [lamc], lamc.ap[0:1, 0:2], AF.Exp)
    P.emit("v", lambda E: E.tensor_tensor(lamc.ap[0:1, 4:5], lamc.ap[0:1, 3:4],
           lamc.ap[0:1, 2:3], ALU.subtract), reads=[lamc], writes=[lamc])
    P.emit("v", lambda E: E.tensor_scalar(lamc.ap[0:1, 5:6], lamc.ap[0:1, 4:5], -0.2,
           None, ALU.add, ALU.bypass), reads=[lamc], writes=[lamc])
    neglam_col = P.alloc(4, F32, "neglam_col")
    P.emit("t", lambda E: E.matmul(P.banks[0].ap[:, 0:1], c_ones_row.ap,
           lamc.ap[0:1, 5:6], start=True, stop=True), reads=[lamc, c_ones_row],
           writes=[P.banks[0]])
    P.emit("v", lambda E: E.tensor_copy(neglam_col.ap[:, 0:1], P.banks[0].ap[:, 0:1]),
           reads=[P.banks[0]], writes=[neglam_col])

    qkv = P.alloc(32 * T, BF16, "qkv")
    qkv.ap = qkv.ap.rearrange("p (c t) -> p c t", c=32)
    wslot = P.alloc(NCH * D, BF16, "wslot")
    for j in range(4):
        wb = load_weight(P, wqkv_d, NCH, D, "wqkv", slot=wslot, col0=j * D)

        def evac_qkv(dc, blk, bank, j=j):
            gdc = j * NCH + dc
            sl = slice(blk * 512, (blk + 1) * 512)
            if j == 0:
                P.emit("v", lambda E, gdc=gdc, sl=sl, bank=bank: E.tensor_scalar(
                    qkv.ap[:, gdc, sl], bank.ap, 32.0 ** -0.5, None, ALU.mult,
                    ALU.bypass), reads=[bank], writes=[qkv])
            else:
                P.emit("v", lambda E, gdc=gdc, sl=sl, bank=bank: E.tensor_copy(
                    qkv.ap[:, gdc, sl], bank.ap), reads=[bank], writes=[qkv])
        proj(P, wb, lambda c, blk: xn.ap[:, c, 1 + blk * 512:1 + (blk + 1) * 512], [xn],
             NCH, NCH, evac_qkv)
    P.free(wslot, xn)
    if dump == "qkv":
        return dump_fm(qkv, 32)

    vtm = P.alloc(16 * 16 * 64, BF16, "vtm")
    vtm.ap = vtm.ap.rearrange("p (h k e) -> p h k e", h=16, k=16)
    for h in range(16):
        base = (h % 2) * 64
        vch = 24 + h // 2
        bank = P.banks[h % 2]
        bankb = bank.ap.bitcast(BF16)
        for kt in range(16):
            vsl = qkv.ap[base:base + 64, vch, kt * 128:(kt + 1) * 128]
            idn = ident[base:base + 64, base:base + 64]
            P.emit("t", lambda E, vsl=vsl, idn=idn, kt=kt, bankb=bankb: E.transpose(
                bankb[:, kt * 64:(kt + 1) * 64], vsl, idn), reads=[qkv, cbf], writes=[bank])
        P.emit("v", lambda E, h=h, bankb=bankb: E.tensor_copy(
            vtm.ap[:, h, :, :], bankb.rearrange("p (k e) -> p k e", k=16)),
            reads=[bank], writes=[vtm])

    negmb = P.alloc(8, F32, "negm_cols")
    nrow = P.alloc(2 * 2048, BF16, "negm_rows")
    rrow = P.alloc(2 * T, F32, "recip_rows")
    dcol = P.alloc(8, F32, "den_cols")
    scrap = P.alloc(512, F32, "exp_scrap")
    pt1 = P.alloc(512, BF16, "pt1")
    pt2 = P.alloc(512, BF16, "pt2")
    rbc2 = P.alloc(1024, F32, "o_rbc")
    ot1 = P.alloc(512, F32, "o_t1")
    ot2 = P.alloc(512, F32, "o_t2")

    for hp in range(8):
        for par in range(2):
            h = 2 * hp + par
            base = par * 64
            qch = h // 2
            Q = qkv.ap[base:base + 64, qch, :]
            for comp in range(2):
                kch = (8 if comp == 0 else 16) + h // 2
                K = qkv.ap[base:base + 64, kch, :]
                nrow_h = nrow.ap[0:1, comp * 2048:comp * 2048 + 2048]
                rrow_h = rrow.ap[0:1, comp * T:comp * T + T]
                for qt in range(16):
                    bank = P.banks[qt % 2]
                    for kblk in range(4):
                        ksl = slice(kblk * 512, (kblk + 1) * 512)
                        P.emit("t", lambda E, qt=qt, ksl=ksl, bank=bank, Q=Q, K=K: E.matmul(
                            bank.ap, Q[:, qt * 128:(qt + 1) * 128], K[:, ksl],
                            start=True, stop=True), reads=[qkv], writes=[bank])
                        P.emit("v", lambda E, kblk=kblk, bank=bank: E.tensor_reduce(
                            negmb.ap[:, kblk:kblk + 1], bank.ap, axis=AX.X, op=ALU.max,
                            negate=True), reads=[bank], writes=[negmb])
                    P.emit("v", lambda E: E.tensor_reduce(negmb.ap[:, 4:5],
                           negmb.ap[:, 0:4], axis=AX.X, op=ALU.min), reads=[negmb],
                           writes=[negmb])
                    P.emit("v", lambda E: E.tensor_copy(
                        negmb.ap[:, 5:6].bitcast(BF16)[:, 0:1], negmb.ap[:, 4:5]),
                        reads=[negmb], writes=[negmb])
                    P.emit("v", lambda E: E.tensor_copy(negmb.ap[:, 6:7],
                           negmb.ap[:, 5:6].bitcast(BF16)[:, 0:1]), reads=[negmb],
                           writes=[negmb])
                    for kblk in range(4):
                        ksl = slice(kblk * 512, (kblk + 1) * 512)
                        bank = P.banks[2 + kblk % 2]
                        P.emit("t", lambda E, qt=qt, ksl=ksl, bank=bank, Q=Q, K=K: E.matmul(
                            bank.ap, Q[:, qt * 128:(qt + 1) * 128], K[:, ksl],
                            start=True, stop=True), reads=[qkv], writes=[bank])
                        act(P, scrap, scrap.ap, [bank, negmb], bank.ap, AF.Exp,
                            bias=negmb.ap[:, 6:7], accum_ap=dcol.ap[:, kblk:kblk + 1],
                            accum_buf=dcol)
                    P.emit("v", lambda E: E.tensor_reduce(dcol.ap[:, 4:5], dcol.ap[:, 0:4],
                           axis=AX.X, op=ALU.add), reads=[dcol], writes=[dcol])
                    P.emit("v", lambda E: E.reciprocal(dcol.ap[:, 5:6], dcol.ap[:, 4:5]),
                           reads=[dcol], writes=[dcol])
                    P.emit("t", lambda E: E.transpose(P.banks[6].ap[0:1, 0:128],
                           negmb.ap[:, 6:7], cid.ap), reads=[negmb, cid], writes=[P.banks[6]])
                    P.emit("v", lambda E, qt=qt, nrow_h=nrow_h: E.tensor_copy(
                        nrow_h[0:1, qt * 128:(qt + 1) * 128], P.banks[6].ap[0:1, 0:128]),
                        reads=[P.banks[6]], writes=[nrow])
                    P.emit("t", lambda E: E.transpose(P.banks[7].ap[0:1, 0:128],
                           dcol.ap[:, 5:6], cid.ap), reads=[dcol, cid], writes=[P.banks[7]])
                    P.emit("v", lambda E, qt=qt, rrow_h=rrow_h: E.tensor_copy(
                        rrow_h[0:1, qt * 128:(qt + 1) * 128], P.banks[7].ap[0:1, 0:128]),
                        reads=[P.banks[7]], writes=[rrow])
            for qblk in range(4):
                qsl = slice(qblk * 512, (qblk + 1) * 512)
                o1bank = P.banks[4]
                o2bank = P.banks[5]
                for kt in range(16):
                    for comp in range(2):
                        kch = (8 if comp == 0 else 16) + h // 2
                        K = qkv.ap[base:base + 64, kch, :]
                        nrow_h = nrow.ap[0:1, comp * 2048:comp * 2048 + 2048]
                        bank = P.banks[comp]
                        ptx = pt1 if comp == 0 else pt2
                        P.emit("t", lambda E, kt=kt, qsl=qsl, bank=bank, Q=Q, K=K: E.matmul(
                            bank.ap, K[:, kt * 128:(kt + 1) * 128], Q[:, qsl],
                            start=True, stop=False), reads=[qkv], writes=[bank])
                        P.emit("t", lambda E, qsl=qsl, bank=bank, nrow_h=nrow_h: E.matmul(
                            bank.ap, obf.ap[0:1, :], nrow_h[0:1, qsl], start=False,
                            stop=True), reads=[obf, nrow], writes=[bank])
                        act(P, ptx, ptx.ap, [bank], bank.ap, AF.Exp)
                        obank = o1bank if comp == 0 else o2bank
                        P.emit("t", lambda E, kt=kt, obank=obank, h=h, par=par, ptx=ptx:
                               E.matmul(obank.ap[par * 64:par * 64 + 64, :],
                                        vtm.ap[:, h, kt, :], ptx.ap,
                                        start=(kt == 0), stop=(kt == 15),
                                        tile_position=(0, par * 64)),
                               reads=[vtm, ptx], writes=[obank])
                for (comp, obank, ot) in ((0, o1bank, ot1), (1, o2bank, ot2)):
                    rrow_h = rrow.ap[0:1, comp * T + qblk * 512:comp * T + (qblk + 1) * 512]
                    bank = P.banks[2 + comp]
                    P.emit("t", lambda E, rrow_h=rrow_h, bank=bank: E.matmul(
                        bank.ap, c_ones_row.ap, rrow_h, start=True, stop=True),
                        reads=[rrow, c_ones_row], writes=[bank])
                    P.emit("v", lambda E, bank=bank: E.tensor_copy(rbc2.ap[:, 0:512],
                           bank.ap), reads=[bank], writes=[rbc2])
                    P.emit("v", lambda E, obank=obank, ot=ot, par=par: E.tensor_tensor(
                        ot.ap[par * 64:par * 64 + 64, :],
                        obank.ap[par * 64:par * 64 + 64, :],
                        rbc2.ap[par * 64:par * 64 + 64, 0:512], ALU.mult),
                        reads=[obank, rbc2], writes=[ot])
                P.emit("v", lambda E, hp=hp, par=par, qsl=qsl: E.scalar_tensor_tensor(
                    qkv.ap[par * 64:par * 64 + 64, 24 + hp, qsl],
                    ot2.ap[par * 64:par * 64 + 64, :], neglam_col.ap[par * 64:par * 64 + 64, 0:1],
                    ot1.ap[par * 64:par * 64 + 64, :], ALU.mult, ALU.add),
                    reads=[ot1, ot2, neglam_col], writes=[qkv])
    P.free(vtm, negmb, nrow, rrow, dcol, scrap, pt1, pt2, rbc2, ot1, ot2)
    if dump == "ofm":
        return dump_fm(qkv, 32)

    sqb = P.alloc(T, BF16, "sn_sq")
    rows2h = P.alloc(T, F32, "sn_rows")
    tmul = P.alloc(512, F32, "sn_tmul")
    for c in range(NCH):
        ch = 24 + c
        for blk in range(4):
            sl = slice(blk * 512, (blk + 1) * 512)
            act(P, sqb, sqb.ap[:, sl], [qkv], qkv.ap[:, ch, sl], AF.Square)
            P.emit("t", lambda E, sl=sl, blk=blk: E.matmul(
                P.banks[blk].ap[0:2, :], E2, sqb.ap[:, sl], start=True, stop=True),
                reads=[sqb, cbf], writes=[P.banks[blk]])
        for blk in range(4):
            sl = slice(blk * 512, (blk + 1) * 512)
            P.emit("v", lambda E, sl=sl, blk=blk: E.tensor_scalar(
                rows2h.ap[0:2, sl], P.banks[blk].ap[0:2, :], 1.0 / 64, 1e-5,
                ALU.mult, ALU.add), reads=[P.banks[blk]], writes=[rows2h])
        P.emit("s", lambda E: E.activation(rows2h.ap[0:2, :], rows2h.ap[0:2, :], AF.Sqrt),
               reads=[rows2h], writes=[rows2h])
        P.emit("v", lambda E: E.reciprocal(rows2h.ap[0:2, :], rows2h.ap[0:2, :]),
               reads=[rows2h], writes=[rows2h])
        for blk in range(4):
            sl = slice(blk * 512, (blk + 1) * 512)
            bank = P.banks[4 + blk % 2]
            P.emit("t", lambda E, sl=sl, bank=bank: E.matmul(
                bank.ap, B2.ap, rows2h.ap[0:2, sl], start=True, stop=True),
                reads=[rows2h, B2], writes=[bank])
            P.emit("v", lambda E, ch=ch, sl=sl, bank=bank: E.tensor_tensor(
                tmul.ap, qkv.ap[:, ch, sl], bank.ap, ALU.mult),
                reads=[qkv, bank], writes=[tmul])
            P.emit("v", lambda E, ch=ch, sl=sl: E.tensor_scalar(
                qkv.ap[:, ch, sl], tmul.ap, vec["snw"].ap, 0.8, ALU.mult, ALU.mult),
                reads=[tmul, vec["snw"]], writes=[qkv])
    P.free(sqb, rows2h, tmul)
    if dump == "osn":
        return dump_fm(qkv, 32)

    wslot = P.alloc(NCH * D, BF16, "wslot2")
    wb = load_weight(P, wproj_d, NCH, D, "wproj", slot=wslot)
    xst = P.alloc(T, F32, "xst")
    gtmp = P.alloc(T, BF16, "gtmp")
    yst = P.alloc(T, F32, "yst")
    st1 = P.alloc(512, F32, "st1")
    state = {"dc": -1}

    def evac_upd(dc, blk, bank):
        if state["dc"] != dc:
            state["dc"] = dc
            P.dma("y", xst.ap, x_d[dc * 128:(dc + 1) * 128, :], writes=[xst])
            P.dma("y", gtmp.ap, g_d[dc * 128:(dc + 1) * 128, :], writes=[gtmp])
        sl = slice(blk * 512, (blk + 1) * 512)
        P.emit("v", lambda E, dc=dc, sl=sl, bank=bank: E.tensor_scalar(
            st1.ap, bank.ap, vec["b_proj"].ap[:, dc:dc + 1], None, ALU.add, ALU.bypass),
            reads=[bank, vec["b_proj"]], writes=[st1])
        P.emit("v", lambda E, sl=sl: E.tensor_tensor(st1.ap, st1.ap, gtmp.ap[:, sl],
               ALU.mult), reads=[st1, gtmp], writes=[st1])
        P.emit("v", lambda E, sl=sl: E.tensor_tensor(yst.ap[:, sl], st1.ap, xst.ap[:, sl],
               ALU.add), reads=[st1, xst], writes=[yst])
        if blk == 3:
            P.dma("y", y_d[dc * 128:(dc + 1) * 128, :], yst.ap, reads=[yst])
    proj(P, wb, lambda c, blk: qkv.ap[:, 24 + c, blk * 512:(blk + 1) * 512], [qkv],
         NCH, NCH, evac_upd)
    last_y = ("y", "d", len(P.ops["y"]) - 1)
    P.pop_scope()
    return P, [last_y]


def l2_inputs(core_x, ins, pfx=""):
    wq = np.asarray(ins["L2_attn_w_qkv"], np.float32)
    q_w = wq[0:1024]
    k_w = wq[1024:2048]
    v_w = wq[2048:3072]
    kz1 = np.zeros_like(k_w)
    kz2 = np.zeros_like(k_w)
    for h in range(16):
        kz1[64 * h:64 * h + 32] = k_w[64 * h:64 * h + 32]
        kz2[64 * h + 32:64 * h + 64] = k_w[64 * h + 32:64 * h + 64]
    wpack = np.concatenate([q_w, kz1, kz2, v_w], axis=0)
    rd = {
        "x_in": _f32(core_x),
        "wqkvT4": _f32(wpack.T),
        "wprojT": _f32(ins["L2_attn_w_proj"].T),
        "ew1": _f32(ins["L2_ffn_ew1"]),
        "ew2": _f32(ins["L2_ffn_ew2"]),
        "ew3": _f32(ins["L2_ffn_ew3"]),
        "sw1": _f32(ins["L2_ffn_sw1"]),
        "sw2": _f32(ins["L2_ffn_sw2"]),
        "sw3": _f32(ins["L2_ffn_sw3"]),
        "vecs": l2_pack(ins).host,
        "cbf": _bf_consts(),
        "cid32": np.eye(128, dtype=np.float32),
    }
    return {("x_in" if k == "x_in" else pfx + k): v for k, v in rd.items()}





def l0_pack(ins=None):
    vp = VecPack(4096)
    if ins is not None:
        vp.add_cols("lng_c", _col8(ins["L0_attn_ln_g"]))
        vp.add_cols("lnb_c", _col8(ins["L0_attn_ln_b"]))
        vp.add_cols("h1b1", _f32(np.asarray(ins["L0_attn_h1_b1"]).reshape(32, 128).T))
        vp.add_cols("h1b2", _col8(ins["L0_attn_h1_b2"]))
        vp.add_cols("h2b1", _f32(np.asarray(ins["L0_attn_h2_b1"]).reshape(32, 128).T))
        vp.add_cols("h2b2", _col8(ins["L0_attn_h2_b2"]))
        vp.add_cols("fmb1", _f32(np.asarray(ins["L0_attn_fm_b1"]).reshape(32, 128).T))
        vp.add_cols("fmb2", _col8(ins["L0_attn_fm_b2"]))
        vp.add_cols("moeb1", _f32(np.asarray(ins["L0_ffn_b1"]).reshape(8, 32, 128)
                                  .transpose(2, 0, 1).reshape(128, 256)))
        vp.add_row("lng_r", ins["L0_attn_ln_g"])
        vp.add_row("lnb_r", ins["L0_attn_ln_b"])
        vp.add_row("router_b", ins["L0_ffn_router_b"])
        vp.add_cols("router_w", _f32(np.asarray(ins["L0_ffn_router_w"]).reshape(NCH, 128, 8)
                                     .transpose(1, 0, 2).reshape(128, NCH * 8)))
    else:
        vp.add_cols("lng_c", np.zeros((128, 8), np.float32))
        vp.add_cols("lnb_c", np.zeros((128, 8), np.float32))
        for n in ("h1b1", "h1b2", "h2b1", "h2b2", "fmb1", "fmb2"):
            vp.add_cols(n, np.zeros((128, 32 if n.endswith("b1") else 8), np.float32))
        vp.add_cols("moeb1", np.zeros((128, 256), np.float32))
        vp.add_row("lng_r", np.zeros(1024, np.float32))
        vp.add_row("lnb_r", np.zeros(1024, np.float32))
        vp.add_row("router_b", np.zeros(8, np.float32))
        vp.add_cols("router_w", np.zeros((128, NCH * 8), np.float32))
    vp.add_cols("ones_col", np.ones((128, 1), np.float32))
    vp.add_row("ones_row", np.ones(128, np.float32))
    return vp


def _bf_consts_l0():
    c = np.zeros((128, 2304), BF)
    c[:, 0:128] = np.eye(128, dtype=BF)
    c[:, 130] = 1
    zz = np.arange(512)[None, :]
    pp = np.arange(128)[:, None]
    for k in range(4):
        c[:, 256 + 512 * k:256 + 512 * (k + 1)] = (zz >= 128 * k + pp).astype(BF)
    return c


def internal_ln(P, src3, dst3, gcol, bcol, onescol_bf, c_ones_row, gofs=0, eps=1e-5):
    sqb = P.alloc(T, BF16, "iln_sq")
    rows = P.alloc(2 * T, F32, "iln_rows")
    A_bc = P.alloc(T, F32, "iln_Abc")
    B_bc = P.alloc(T, F32, "iln_Bbc")
    t3 = P.alloc(T, F32, "iln_t3")
    t1 = P.alloc(T, F32, "iln_t1")
    bsum = P.banks[:4]
    bsq = P.banks[4:8]
    mean = rows.ap[0:1, 0:T]
    msq = rows.ap[0:1, T:2 * T]
    for c in range(NCH):
        act(P, sqb, sqb.ap, [src3], src3.ap[:, c, gofs:gofs + T], AF.Square)
        for blk in range(4):
            sl = slice(blk * 512, (blk + 1) * 512)
            P.emit("t", lambda E, c=c, sl=sl, blk=blk: E.matmul(
                bsum[blk].ap[0:1, :], onescol_bf, src3.ap[:, c, gofs + sl.start:gofs + sl.stop],
                start=(c == 0), stop=(c == NCH - 1)), reads=[src3], writes=[bsum[blk]])
            P.emit("t", lambda E, c=c, sl=sl, blk=blk: E.matmul(
                bsq[blk].ap[0:1, :], onescol_bf, sqb.ap[:, sl],
                start=(c == 0), stop=(c == NCH - 1)), reads=[sqb], writes=[bsq[blk]])
    for blk in range(4):
        sl = slice(blk * 512, (blk + 1) * 512)
        P.emit("v", lambda E, blk=blk, sl=sl: E.tensor_scalar(
            mean[:, sl], bsum[blk].ap[0:1, :], 1.0 / D, None, ALU.mult, ALU.bypass),
            reads=[bsum[blk]], writes=[rows])
        P.emit("v", lambda E, blk=blk, sl=sl: E.tensor_scalar(
            msq[:, sl], bsq[blk].ap[0:1, :], 1.0 / D, eps, ALU.mult, ALU.add),
            reads=[bsq[blk]], writes=[rows])
    m2row = t3.ap[0:1, 0:T]
    P.emit("v", lambda E: E.tensor_tensor(m2row, mean, mean, ALU.mult), reads=[rows], writes=[t3])
    P.emit("v", lambda E: E.tensor_tensor(msq, msq, m2row, ALU.subtract),
           reads=[rows, t3], writes=[rows])
    P.emit("s", lambda E: E.activation(msq, msq, AF.Sqrt), reads=[rows], writes=[rows])
    P.emit("v", lambda E: E.reciprocal(msq, msq), reads=[rows], writes=[rows])
    P.emit("v", lambda E: E.scalar_tensor_tensor(mean, mean, -1.0, msq, ALU.mult, ALU.mult),
           reads=[rows], writes=[rows])
    for (srow, dst) in ((msq, A_bc), (mean, B_bc)):
        for blk in range(4):
            sl = slice(blk * 512, (blk + 1) * 512)
            bank = P.banks[blk % 2]
            P.emit("t", lambda E, srow=srow, sl=sl, bank=bank: E.matmul(
                bank.ap, c_ones_row.ap, srow[:, sl], start=True, stop=True),
                reads=[rows, c_ones_row], writes=[bank])
            P.emit("v", lambda E, dst=dst, sl=sl, bank=bank: E.tensor_copy(
                dst.ap[:, sl], bank.ap), reads=[bank], writes=[dst])
    for c in range(NCH):
        P.emit("v", lambda E, c=c: E.tensor_scalar(
            t3.ap, B_bc.ap, gcol.ap[:, c:c + 1], bcol.ap[:, c:c + 1], ALU.mult, ALU.add),
            reads=[B_bc, gcol, bcol], writes=[t3])
        P.emit("v", lambda E, c=c: E.tensor_tensor(
            t1.ap, src3.ap[:, c, gofs:gofs + T], A_bc.ap, ALU.mult),
            reads=[src3, A_bc], writes=[t1])
        P.emit("v", lambda E, c=c: E.scalar_tensor_tensor(
            dst3.ap[:, c, :], t1.ap, gcol.ap[:, c:c + 1], t3.ap, ALU.mult, ALU.add),
            reads=[t1, gcol, t3], writes=[dst3])
    P.free(sqb, rows, A_bc, B_bc, t3, t1)


def build_l0(P=None, x_d=None, y_d=None, pfx="", dump=None):
    if P is None:
        P = Prog("l0").make_arena()
        x_d = P.dram("x_in", [D, T])
        y_d = P.dram("y_out", [D, T], kind="ExternalOutput")
    P.push_scope()
    g_d = P.dram(pfx + "g_spill", [D, T], BF16, kind="Internal")
    h1w1_d = P.dram(pfx + "h1w1", [D, 4096])
    h1w2_d = P.dram(pfx + "h1w2", [4096, D])
    h2w1_d = P.dram(pfx + "h2w1", [D, 4096])
    h2w2_d = P.dram(pfx + "h2w2", [4096, D])
    fmw1_d = P.dram(pfx + "fmw1", [D, 4096])
    fmw2_d = P.dram(pfx + "fmw2", [4096, D])
    mw1_d = P.dram(pfx + "mw1", [8, D, 4096])
    mw2_d = P.dram(pfx + "mw2", [8, 4096, D])
    mb2_d = P.dram(pfx + "mb2", [8, D])
    vecs_d = P.dram(pfx + "vecs", [128, 4096])
    cbf_d = P.dram(pfx + "cbf", [128, 2304], BF16)

    vec = l0_pack().load(P, vecs_d)
    c_ones_col = vec["ones_col"]
    c_ones_row = vec["ones_row"]
    cbf = P.alloc(2304, BF16, "cbf")
    P.dma("y", cbf.ap, cbf_d, writes=[cbf])
    ident = cbf.ap[:, 0:128]
    onescol_bf = cbf.ap[:, 130:131]
    masks = [cbf.ap[:, 256 + 512 * k:256 + 512 * (k + 1)] for k in range(4)]
    b2row = P.alloc(D, BF16, "moe_b2row")
    b2f = P.alloc(D, F32, "moe_b2f")

    def dump_fm(sb3, nch=NCH, guard=0):
        outb = P.alloc(T, F32, "dumpout")
        for c in range(nch):
            sl = sb3.ap[:, c, guard:T + guard]
            P.emit("v", lambda E, sl=sl: E.tensor_copy(outb.ap, sl), reads=[sb3], writes=[outb])
            P.dma("y", y_d[(c % NCH) * 128:((c % NCH) + 1) * 128, :], outb.ap, reads=[outb])
        return P, [("y", "d", len(P.ops["y"]) - 1)]

    xn = alloc_xn(P)
    x = P.alloc(NCH * T, F32, "x")
    x.ap = x.ap.rearrange("p (c t) -> p c t", c=NCH)
    P.dma("y", x.ap, x_d.rearrange("(c p) t -> p c t", p=128), writes=[x])
    global_ln(P, x, xn, c_ones_col, c_ones_row)
    P.free(x)

    rwb = P.alloc(NCH * 8, BF16, "router_w_bf")
    rwb.ap = rwb.ap.rearrange("p (c e) -> p c e", c=NCH)
    P.emit("v", lambda E: E.tensor_copy(
        rwb.ap, vec["router_w"].ap.rearrange("p (c e) -> p c e", c=NCH)),
        reads=[vec["router_w"]], writes=[rwb])
    rbrow_bf = P.alloc(8, BF16, "router_b_bf")
    P.emit("v", lambda E: E.tensor_copy(rbrow_bf.ap[0:1, :], vec["router_b"].ap),
           reads=[vec["router_b"]], writes=[rbrow_bf])
    obf = P.alloc(512, BF16, "ones_row_bf")
    P.emit("v", lambda E: E.memset(obf.ap[0:1, :], 1.0), writes=[obf])
    logits = P.alloc(16 * 8, F32, "logits")
    logits.ap = logits.ap.rearrange("p (n e) -> p n e", n=16)
    for tt in range(16):
        bank = P.banks[tt % 2]
        for c in range(NCH):
            P.emit("t", lambda E, tt=tt, c=c, bank=bank: E.matmul(
                bank.ap[:, 0:8], xn.ap[:, c, 1 + tt * 128:1 + (tt + 1) * 128], rwb.ap[:, c, :],
                start=(c == 0), stop=False), reads=[xn, rwb], writes=[bank])
        P.emit("t", lambda E, tt=tt, bank=bank: E.matmul(
            bank.ap[:, 0:8], obf.ap[0:1, 0:128], rbrow_bf.ap[0:1, :], start=False, stop=True),
            reads=[obf, rbrow_bf], writes=[bank])
        P.emit("v", lambda E, tt=tt, bank=bank: E.tensor_copy(logits.ap[:, tt, :],
               bank.ap[:, 0:8]), reads=[bank], writes=[logits])
    r_m1 = P.alloc(16, F32, "r_m1")
    r_m2 = P.alloc(16, F32, "r_m2")
    r_g1 = P.alloc(16, F32, "r_g1")
    r_g2 = P.alloc(16, F32, "r_g2")
    msk1 = P.alloc(16 * 8, F32, "msk1")
    msk1.ap = msk1.ap.rearrange("p (n e) -> p n e", n=16)
    msk2 = P.alloc(16 * 8, F32, "msk2")
    msk2.ap = msk2.ap.rearrange("p (n e) -> p n e", n=16)
    lg2 = P.alloc(16 * 8, F32, "lg2")
    lg2.ap = lg2.ap.rearrange("p (n e) -> p n e", n=16)
    gates = P.alloc(16 * 8, F32, "gates")
    gates.ap = gates.ap.rearrange("p (n e) -> p n e", n=16)
    P.emit("v", lambda E: E.tensor_reduce(r_m1.ap, logits.ap, axis=AX.X, op=ALU.max),
           reads=[logits], writes=[r_m1])
    P.emit("v", lambda E: E.tensor_tensor(msk1.ap, logits.ap,
           r_m1.ap.unsqueeze(2).broadcast_to([128, 16, 8]), ALU.is_equal),
           reads=[logits, r_m1], writes=[msk1])
    P.emit("v", lambda E: E.scalar_tensor_tensor(lg2.ap, msk1.ap, -1e30, logits.ap,
           ALU.mult, ALU.add), reads=[msk1, logits], writes=[lg2])
    P.emit("v", lambda E: E.tensor_reduce(r_m2.ap, lg2.ap, axis=AX.X, op=ALU.max),
           reads=[lg2], writes=[r_m2])
    P.emit("v", lambda E: E.tensor_tensor(msk2.ap, lg2.ap,
           r_m2.ap.unsqueeze(2).broadcast_to([128, 16, 8]), ALU.is_equal),
           reads=[lg2, r_m2], writes=[msk2])
    P.emit("v", lambda E: E.tensor_tensor(r_g1.ap, r_m1.ap, r_m2.ap, ALU.subtract),
           reads=[r_m1, r_m2], writes=[r_g1])
    act(P, r_g1, r_g1.ap, [r_g1], r_g1.ap, AF.Sigmoid)
    P.emit("v", lambda E: E.tensor_scalar(r_g2.ap, r_g1.ap, -1.0, 1.0, ALU.mult, ALU.add),
           reads=[r_g1], writes=[r_g2])
    P.emit("v", lambda E: E.tensor_tensor(gates.ap, msk1.ap,
           r_g1.ap.unsqueeze(2).broadcast_to([128, 16, 8]), ALU.mult),
           reads=[msk1, r_g1], writes=[gates])
    P.emit("v", lambda E: E.tensor_tensor(msk2.ap, msk2.ap,
           r_g2.ap.unsqueeze(2).broadcast_to([128, 16, 8]), ALU.mult),
           reads=[msk2, r_g2], writes=[msk2])
    P.emit("v", lambda E: E.tensor_tensor(gates.ap, gates.ap, msk2.ap, ALU.add),
           reads=[gates, msk2], writes=[gates])
    P.free(rwb, rbrow_bf, logits, r_m1, r_m2, r_g1, r_g2, msk1, msk2, lg2)
    cid = P.alloc(128, F32, "cid32_dyn")
    P.emit("v", lambda E: E.tensor_copy(cid.ap, ident), reads=[cbf], writes=[cid])

    yacc = P.alloc(NCH * T, BF16, "yacc")
    yacc.ap = yacc.ap.rearrange("p (c t) -> p c t", c=NCH)
    HB = T // 2
    hbuf = P.alloc(32 * HB, BF16, "hbuf")
    hbuf.ap = hbuf.ap.rearrange("p (c t) -> p c t", c=32)
    stg = [P.alloc(512, F32, "l0_stg0"), P.alloc(512, F32, "l0_stg1")]
    grow = P.alloc(T, F32, "gate_row")
    gbc = P.alloc(T, F32, "l0_gbc")
    t1 = P.alloc(512, F32, "l0_t1")
    wslA = WSlots(P, NCH * 512, "wslA")
    wslB = wslA

    for e in range(8):
        for tt in range(16):
            bank = P.banks[4 + tt % 2]
            P.emit("t", lambda E, tt=tt, e=e, bank=bank: E.transpose(
                bank.ap[0:1, 0:128], gates.ap[:, tt, e:e + 1], cid.ap),
                reads=[gates, cid], writes=[bank])
            P.emit("v", lambda E, tt=tt, bank=bank: E.tensor_copy(
                grow.ap[0:1, tt * 128:(tt + 1) * 128], bank.ap[0:1, 0:128]),
                reads=[bank], writes=[grow])
        P.dma("y", b2f.ap[0:1, :], mb2_d[e:e + 1, :], writes=[b2f])
        P.emit("v", lambda E: E.tensor_copy(b2row.ap[0:1, :], b2f.ap[0:1, :]),
               reads=[b2f], writes=[b2row])
        for blk in range(4):
            sl = slice(blk * 512, (blk + 1) * 512)
            bank = P.banks[4 + blk % 2]
            P.emit("t", lambda E, sl=sl, bank=bank: E.matmul(
                bank.ap, c_ones_row.ap, grow.ap[0:1, sl], start=True, stop=True),
                reads=[grow, c_ones_row], writes=[bank])
            P.emit("v", lambda E, sl=sl, bank=bank: E.tensor_copy(gbc.ap[:, sl], bank.ap),
                   reads=[bank], writes=[gbc])
        for half in range(2):
            t0 = half * HB
            for ds in range(8):
                wA = load_weight(P, mw1_d[e], NCH, 512, "mw1", slot=wslA(), stage=stg,
                                 col0=ds * 512)
                for dsub in range(4):
                    hc = ds * 4 + dsub
                    for blk in range(HB // 512):
                        sl = slice(t0 + blk * 512, t0 + (blk + 1) * 512)
                        bank = P.banks[(hc * 2 + blk) % 6]
                        for c in range(NCH):
                            P.emit("t", lambda E, c=c, dsub=dsub, sl=sl, bank=bank, wA=wA:
                                   E.matmul(bank.ap, wA.ap[:, c, dsub * 128:(dsub + 1) * 128],
                                            xn.ap[:, c, 1 + sl.start:1 + sl.stop],
                                            start=(c == 0), stop=(c == NCH - 1)),
                                   reads=[wA, xn], writes=[bank])
                        act(P, hbuf, hbuf.ap[:, hc, blk * 512:(blk + 1) * 512], [bank],
                            bank.ap, AF.Gelu, bias=vec["moeb1"].ap[:, e * 32 + hc:e * 32 + hc + 1])
            for ds3 in range(8):
                wC = load_weight(P, mw2_d[e], 32, 128, "mw2", slot=wslB(), stage=stg,
                                 col0=ds3 * 128)
                dc = ds3
                for blk in range(HB // 512):
                    sl = slice(t0 + blk * 512, t0 + (blk + 1) * 512)
                    bank = P.banks[6 + blk % 2]
                    for c in range(32):
                        P.emit("t", lambda E, c=c, sl=sl, t0=t0, bank=bank, wC=wC,
                               hbuf=hbuf: E.matmul(
                            bank.ap, wC.ap[:, c, 0:128],
                            hbuf.ap[:, c, sl.start - t0:sl.stop - t0],
                            start=(c == 0), stop=False), reads=[wC, hbuf], writes=[bank])
                    P.emit("t", lambda E, dc=dc, bank=bank: E.matmul(
                        bank.ap, b2row.ap[0:1, dc * 128:(dc + 1) * 128],
                        obf.ap[0:1, 0:512], start=False, stop=True),
                        reads=[b2row, obf], writes=[bank])
                    if e == 0:
                        P.emit("v", lambda E, dc=dc, sl=sl, bank=bank: E.tensor_tensor(
                            yacc.ap[:, dc, sl], bank.ap, gbc.ap[:, sl], ALU.mult),
                            reads=[bank, gbc], writes=[yacc])
                    else:
                        P.emit("v", lambda E, sl=sl, bank=bank: E.tensor_tensor(
                            t1.ap, bank.ap, gbc.ap[:, sl], ALU.mult),
                            reads=[bank, gbc], writes=[t1])
                        P.emit("v", lambda E, dc=dc, sl=sl: E.tensor_tensor(
                            yacc.ap[:, dc, sl], yacc.ap[:, dc, sl], t1.ap, ALU.add),
                            reads=[yacc, t1], writes=[yacc])
    P.free(*wslA.bufs, stg[0], stg[1], grow, gbc, t1)
    gstage = P.alloc(T, BF16, "gstage")
    for c in range(NCH):
        act(P, gstage, gstage.ap, [yacc], yacc.ap[:, c, :], AF.Silu)
        P.dma("y", g_d[c * 128:(c + 1) * 128, :], gstage.ap, reads=[gstage])
    P.free(yacc, hbuf, gstage, gates, cid, b2row, b2f)
    if dump == "gdone":
        tmpb = P.alloc(T, F32, "tmpb")
        gtmp = P.alloc(T, BF16, "gtmp")
        for c in range(NCH):
            P.dma("y", gtmp.ap, g_d[c * 128:(c + 1) * 128, :], writes=[gtmp])
            P.emit("v", lambda E: E.tensor_copy(tmpb.ap, gtmp.ap), reads=[gtmp], writes=[tmpb])
            P.dma("y", y_d[c * 128:(c + 1) * 128, :], tmpb.ap, reads=[tmpb])
        return P, [("y", "d", len(P.ops["y"]) - 1)]

    n3 = P.alloc(NCH * T, BF16, "n3")
    n3.ap = n3.ap.rearrange("p (c t) -> p c t", c=NCH)
    internal_ln(P, xn, n3, vec["lng_c"], vec["lnb_c"], onescol_bf, c_ones_row, gofs=1)
    if dump == "n3":
        return dump_fm(n3)

    HB = T // 2
    W12 = []
    for outname in ("W1", "W2"):
        Wx = P.alloc(NCH * T, BF16, outname)
        Wx.ap = Wx.ap.rearrange("p (c t) -> p c t", c=NCH)
        W12.append(Wx)
    stg = [P.alloc(512, F32, "l0c_stg0")]
    hbuf = P.alloc(32 * (T // 4), BF16, "hbuf2")
    hbuf.ap = hbuf.ap.rearrange("p (c t) -> p c t", c=32)
    wslA = WSlots(P, NCH * 512, "wslA2")
    wslB = wslA
    for mi, (w1d, w2d, b1n, b2n, outname, HBx) in enumerate((
            (h1w1_d, h1w2_d, "h1b1", "h1b2", "W1", T // 4),
            (h2w1_d, h2w2_d, "h2b1", "h2b2", "W2", T // 4))):
        Wx = W12[mi]
        for half in range(T // HBx):
            t0 = half * HBx
            for ds in range(8):
                wA = load_weight(P, w1d, NCH, 512, "w1", slot=wslA(), stage=stg, col0=ds * 512)
                for dsub in range(4):
                    hc = ds * 4 + dsub
                    for blk in range(HBx // 512):
                        sl = slice(t0 + blk * 512, t0 + (blk + 1) * 512)
                        bank = P.banks[(hc * 2 + blk) % 6]
                        for c in range(NCH):
                            P.emit("t", lambda E, c=c, dsub=dsub, sl=sl, bank=bank, wA=wA:
                                   E.matmul(bank.ap, wA.ap[:, c, dsub * 128:(dsub + 1) * 128],
                                            n3.ap[:, c, sl], start=(c == 0),
                                            stop=(c == NCH - 1)), reads=[wA, n3], writes=[bank])
                        act(P, hbuf, hbuf.ap[:, hc, blk * 512:(blk + 1) * 512], [bank],
                            bank.ap, AF.Gelu, bias=vec[b1n].ap[:, hc:hc + 1])
            for ds3 in range(8):
                wC = load_weight(P, w2d, 32, 128, "w2", slot=wslB(), stage=stg, col0=ds3 * 128)
                dc = ds3
                for blk in range(HBx // 512):
                    sl = slice(t0 + blk * 512, t0 + (blk + 1) * 512)
                    bank = P.banks[6 + blk % 2]
                    for c in range(32):
                        P.emit("t", lambda E, c=c, sl=sl, t0=t0, bank=bank, wC=wC,
                               hbuf=hbuf: E.matmul(
                            bank.ap, wC.ap[:, c, 0:128],
                            hbuf.ap[:, c, sl.start - t0:sl.stop - t0],
                            start=(c == 0), stop=(c == 31)), reads=[wC, hbuf], writes=[bank])
                    P.emit("v", lambda E, dc=dc, sl=sl, bank=bank, b2n=b2n, Wx=Wx:
                           E.tensor_scalar(
                        Wx.ap[:, dc, sl], bank.ap, vec[b2n].ap[:, dc:dc + 1], None,
                        ALU.add, ALU.bypass), reads=[bank, vec[b2n]], writes=[Wx])
        if dump == "n3b" and outname == "W1":
            return dump_fm(n3)
    W1b, W2b = W12
    P.free(hbuf, *wslA.bufs, stg[0])
    if dump == "n3c":
        return dump_fm(n3)
    if dump == "w12":
        return dump_fm(W1b)

    offs = []
    off = 0
    for i in range(16):
        offs.append(off)
        off += T - 512 * (i // 4)
    w3t = P.alloc(off, BF16, "w3t")
    for i in range(16):
        z0 = 512 * (i // 4)
        for zb in range(z0, T, 512):
            bank = P.banks[(zb // 512) % 4]
            for c in range(NCH):
                P.emit("t", lambda E, i=i, zb=zb, c=c, bank=bank: E.matmul(
                    bank.ap, W2b.ap[:, c, i * 128:(i + 1) * 128],
                    W1b.ap[:, c, zb:zb + 512], start=(c == 0), stop=(c == NCH - 1)),
                    reads=[W1b, W2b], writes=[bank])
            dsl = slice(offs[i] + (zb - z0), offs[i] + (zb - z0) + 512)
            if zb == z0:
                P.emit("v", lambda E, i=i, dsl=dsl, bank=bank: E.tensor_tensor(
                    w3t.ap[:, dsl], bank.ap, masks[i % 4], ALU.mult),
                    reads=[bank, cbf], writes=[w3t])
            else:
                P.emit("v", lambda E, dsl=dsl, bank=bank: E.tensor_copy(
                    w3t.ap[:, dsl], bank.ap), reads=[bank], writes=[w3t])
    P.free(W1b, W2b)

    ntm = P.alloc(16 * D, BF16, "ntm")
    ntm.ap = ntm.ap.rearrange("p (k m) -> p k m", k=16)
    for tt in range(16):
        bank = P.banks[tt % 2]
        bankb = bank.ap.bitcast(BF16)
        for c in range(NCH):
            P.emit("t", lambda E, tt=tt, c=c, bankb=bankb: E.transpose(
                bankb[:, c * 128:(c + 1) * 128], n3.ap[:, c, tt * 128:(tt + 1) * 128],
                ident), reads=[n3, cbf], writes=[bank])
        P.emit("v", lambda E, tt=tt, bankb=bankb: E.tensor_copy(
            ntm.ap[:, tt, :], bankb[:, 0:1024]), reads=[bank], writes=[ntm])
    P.free(n3)

    x1 = P.alloc(NCH * T, BF16, "x1")
    x1.ap = x1.ap.rearrange("p (c t) -> p c t", c=NCH)
    ttile = P.alloc(D, F32, "ttile")
    tst = P.alloc(8, F32, "t_stats")
    gb_bc = P.alloc(D, F32, "t_gbc")
    bb_bc = P.alloc(D, F32, "t_bbc")
    t2b = P.alloc(D, BF16, "t2b")
    for blk in range(2):
        sl = slice(blk * 512, (blk + 1) * 512)
        bank = P.banks[6]
        P.emit("t", lambda E, sl=sl, bank=bank: E.matmul(
            bank.ap, c_ones_row.ap, vec["lng_r"].ap[:, sl], start=True, stop=True),
            reads=[vec["lng_r"], c_ones_row], writes=[bank])
        P.emit("v", lambda E, sl=sl, bank=bank: E.tensor_copy(gb_bc.ap[:, sl], bank.ap),
               reads=[bank], writes=[gb_bc])
        P.emit("t", lambda E, sl=sl, bank=bank: E.matmul(
            bank.ap, c_ones_row.ap, vec["lnb_r"].ap[:, sl], start=True, stop=True),
            reads=[vec["lnb_r"], c_ones_row], writes=[bank])
        P.emit("v", lambda E, sl=sl, bank=bank: E.tensor_copy(bb_bc.ap[:, sl], bank.ap),
               reads=[bank], writes=[bb_bc])
    for zt in range(16):
        for mb in range(2):
            msl = slice(mb * 512, (mb + 1) * 512)
            bank = P.banks[mb]
            first = True
            for i in range(zt + 1):
                z0 = 512 * (i // 4)
                lsl = slice(offs[i] + zt * 128 - z0, offs[i] + zt * 128 - z0 + 128)
                P.emit("t", lambda E, i=i, zt=zt, lsl=lsl, msl=msl, bank=bank,
                       first=first: E.matmul(bank.ap, w3t.ap[:, lsl], ntm.ap[:, i, msl],
                                             start=first, stop=(i == zt)),
                       reads=[w3t, ntm], writes=[bank])
                first = False
            act(P, ttile, ttile.ap[:, msl], [bank], bank.ap, AF.Gelu)
        P.emit("v", lambda E: E.tensor_reduce(tst.ap[:, 0:1], ttile.ap, axis=AX.X,
               op=ALU.add), reads=[ttile], writes=[tst])
        act(P, t2b, t2b.ap, [ttile], ttile.ap, AF.Square,
            accum_ap=tst.ap[:, 1:2], accum_buf=tst)
        P.emit("v", lambda E: E.tensor_scalar(tst.ap[:, 0:1], tst.ap[:, 0:1], 1.0 / D,
               None, ALU.mult, ALU.bypass), reads=[tst], writes=[tst])
        P.emit("v", lambda E: E.tensor_scalar(tst.ap[:, 1:2], tst.ap[:, 1:2], 1.0 / D,
               1e-5, ALU.mult, ALU.add), reads=[tst], writes=[tst])
        P.emit("v", lambda E: E.tensor_tensor(tst.ap[:, 2:3], tst.ap[:, 0:1],
               tst.ap[:, 0:1], ALU.mult), reads=[tst], writes=[tst])
        P.emit("v", lambda E: E.tensor_tensor(tst.ap[:, 1:2], tst.ap[:, 1:2],
               tst.ap[:, 2:3], ALU.subtract), reads=[tst], writes=[tst])
        P.emit("s", lambda E: E.activation(tst.ap[:, 1:2], tst.ap[:, 1:2], AF.Sqrt),
               reads=[tst], writes=[tst])
        P.emit("v", lambda E: E.reciprocal(tst.ap[:, 1:2], tst.ap[:, 1:2]),
               reads=[tst], writes=[tst])
        P.emit("v", lambda E: E.tensor_scalar(ttile.ap, ttile.ap, tst.ap[:, 0:1],
               tst.ap[:, 1:2], ALU.subtract, ALU.mult), reads=[ttile, tst], writes=[ttile])
        P.emit("v", lambda E: E.tensor_tensor(ttile.ap, ttile.ap, gb_bc.ap, ALU.mult),
               reads=[ttile, gb_bc], writes=[ttile])
        P.emit("v", lambda E: E.tensor_tensor(t2b.ap, ttile.ap, bb_bc.ap, ALU.add),
               reads=[ttile, bb_bc], writes=[t2b])
        for c in range(NCH):
            bank = P.banks[2 + c % 2]
            bankb = bank.ap.bitcast(BF16)
            P.emit("t", lambda E, c=c, bankb=bankb: E.transpose(
                bankb[:, 0:128], t2b.ap[:, c * 128:(c + 1) * 128], ident),
                reads=[t2b, cbf], writes=[bank])
            P.emit("v", lambda E, c=c, zt=zt, bankb=bankb: E.tensor_tensor(
                x1.ap[:, c, zt * 128:(zt + 1) * 128], xn.ap[:, c, 1 + zt * 128:1 + (zt + 1) * 128],
                bankb[:, 0:128], ALU.add), reads=[xn, bank], writes=[x1])
    P.free(w3t, ntm, ttile, tst, gb_bc, bb_bc, t2b)
    if dump == "x1":
        return dump_fm(x1)

    internal_ln(P, x1, x1, vec["lng_c"], vec["lnb_c"], onescol_bf, c_ones_row, gofs=0)
    hbuf = P.alloc(32 * HB, BF16, "hbuf3")
    hbuf.ap = hbuf.ap.rearrange("p (c t) -> p c t", c=32)
    wslA = WSlots(P, NCH * 512, "wslA3")
    wslB = wslA
    stg = [P.alloc(512, F32, "l0f_stg0"), P.alloc(512, F32, "l0f_stg1")]
    xst = P.alloc(HB, F32, "xst")
    gst = P.alloc(HB, BF16, "gst")
    yst = P.alloc(HB, F32, "yst")
    u1 = P.alloc(512, F32, "l0f_t1")
    for half in range(2):
        t0 = half * HB
        for ds in range(8):
            wA = load_weight(P, fmw1_d, NCH, 512, "fmw1", slot=wslA(), stage=stg, col0=ds * 512)
            for dsub in range(4):
                hc = ds * 4 + dsub
                for blk in range(HB // 512):
                    sl = slice(t0 + blk * 512, t0 + (blk + 1) * 512)
                    bank = P.banks[(hc * 2 + blk) % 6]
                    for c in range(NCH):
                        P.emit("t", lambda E, c=c, dsub=dsub, sl=sl, bank=bank, wA=wA:
                               E.matmul(bank.ap, wA.ap[:, c, dsub * 128:(dsub + 1) * 128],
                                        x1.ap[:, c, sl], start=(c == 0),
                                        stop=(c == NCH - 1)), reads=[wA, x1], writes=[bank])
                    act(P, hbuf, hbuf.ap[:, hc, blk * 512:(blk + 1) * 512], [bank],
                        bank.ap, AF.Gelu, bias=vec["fmb1"].ap[:, hc:hc + 1])
        for ds3 in range(8):
            wC = load_weight(P, fmw2_d, 32, 128, "fmw2", slot=wslB(), stage=stg, col0=ds3 * 128)
            dc = ds3
            P.dma("y", xst.ap, x_d[dc * 128:(dc + 1) * 128, t0:t0 + HB], writes=[xst])
            P.dma("y", gst.ap, g_d[dc * 128:(dc + 1) * 128, t0:t0 + HB], writes=[gst])
            for blk in range(HB // 512):
                sl = slice(t0 + blk * 512, t0 + (blk + 1) * 512)
                usl = slice(blk * 512, (blk + 1) * 512)
                bank = P.banks[4 + blk % 2]
                for c in range(32):
                    P.emit("t", lambda E, c=c, usl=usl, bank=bank, wC=wC, hbuf=hbuf:
                           E.matmul(
                        bank.ap, wC.ap[:, c, 0:128], hbuf.ap[:, c, usl.start:usl.stop],
                        start=(c == 0), stop=(c == 31)), reads=[wC, hbuf], writes=[bank])
                P.emit("v", lambda E, dc=dc, bank=bank: E.tensor_scalar(
                    u1.ap, bank.ap, vec["fmb2"].ap[:, dc:dc + 1], None, ALU.add,
                    ALU.bypass), reads=[bank, vec["fmb2"]], writes=[u1])
                P.emit("v", lambda E, dc=dc, sl=sl: E.tensor_tensor(
                    u1.ap, u1.ap, xn.ap[:, dc, 1 + sl.start:1 + sl.stop], ALU.add),
                    reads=[u1, xn], writes=[u1])
                P.emit("v", lambda E, usl=usl: E.tensor_tensor(
                    u1.ap, u1.ap, gst.ap[:, usl], ALU.mult), reads=[u1, gst], writes=[u1])
                P.emit("v", lambda E, usl=usl: E.tensor_tensor(
                    yst.ap[:, usl], u1.ap, xst.ap[:, usl], ALU.add),
                    reads=[u1, xst], writes=[yst])
            P.dma("y", y_d[dc * 128:(dc + 1) * 128, t0:t0 + HB], yst.ap, reads=[yst])
    last_y = ("y", "d", len(P.ops["y"]) - 1)
    P.pop_scope()
    return P, [last_y]


def l0_inputs(core_x, ins, pfx=""):
    rd = {
        "x_in": _f32(core_x),
        "h1w1": _f32(ins["L0_attn_h1_w1"]),
        "h1w2": _f32(ins["L0_attn_h1_w2"]),
        "h2w1": _f32(ins["L0_attn_h2_w1"]),
        "h2w2": _f32(ins["L0_attn_h2_w2"]),
        "fmw1": _f32(ins["L0_attn_fm_w1"]),
        "fmw2": _f32(ins["L0_attn_fm_w2"]),
        "mw1": _f32(ins["L0_ffn_w1"]),
        "mw2": _f32(ins["L0_ffn_w2"]),
        "mb2": _f32(ins["L0_ffn_b2"]),
        "vecs": l0_pack(ins).host,
        "cbf": _bf_consts_l0(),
    }
    return {("x_in" if k == "x_in" else pfx + k): v for k, v in rd.items()}





def _f32(a):
    return np.ascontiguousarray(a, dtype=np.float32)


def _col8(v):
    return _f32(np.asarray(v).reshape(NCH, 128).T)


def l1_inputs(core_x, ins, pfx=""):
    rd = {
        "x_in": _f32(core_x),
        "wkT": _f32(ins["L1_attn_Wk"].T),
        "wvT": _f32(ins["L1_attn_Wv"].T),
        "wrT": _f32(ins["L1_attn_Wr"].T),
        "woT": _f32(ins["L1_attn_Wo"].T),
        "frT": _f32(ins["L1_ffn_W_r"].T),
        "fkT": _f32(ins["L1_ffn_W_k"].T),
        "fvT": _f32(ins["L1_ffn_W_v"].T),
        "vecs": l1_pack(ins).host,
    }
    return {("x_in" if k == "x_in" else pfx + k): v for k, v in rd.items()}



def run_layer(build_fn, input_fn, x_bct, ins):
    P, finals = build_fn()
    nc = P.build(finals)
    in_maps = [input_fn(x_bct[i], ins) for i in range(NCORES)]
    res = run_bass_kernel_spmd(nc, in_maps, core_ids=list(range(NCORES)))
    return np.stack([r["y_out"] for r in res.results])


def build_fused():
    P = Prog("mk").make_arena()
    x_d = P.dram("x_in", [D, T])
    xb1 = P.dram("xb1", [D, T], kind="Internal")
    xb2 = P.dram("xb2", [D, T], kind="Internal")
    xb3 = P.dram("xb3", [D, T], kind="Internal")
    y_d = P.dram("y_out", [D, T], kind="ExternalOutput")
    build_l0(P, x_d, xb1, pfx="l0_")
    build_l1(P, xb1, xb2, pfx="l1_")
    build_l2(P, xb2, xb3, pfx="l2_")
    build_l3(P, xb3, y_d, pfx="l3_")
    last_y = ("y", "d", len(P.ops["y"]) - 1)
    return P, [last_y]


def fused_inputs(core_x, ins):
    m = {"x_in": _f32(core_x)}
    m.update(l0_inputs(core_x, ins, "l0_"))
    m.update(l1_inputs(core_x, ins, "l1_"))
    m.update(l2_inputs(core_x, ins, "l2_"))
    m.update(l3_inputs(core_x, ins, "l3_"))
    return m


def kernel(**ins):
    x = ins["L0_attn_x"]
    x_bct = np.ascontiguousarray(x.transpose(0, 2, 1)).astype(np.float32, copy=False)
    P, finals = build_fused()
    nc = P.build(finals)
    in_maps = [fused_inputs(x_bct[i], ins) for i in range(NCORES)]
    res = run_bass_kernel_spmd(nc, in_maps, core_ids=list(range(NCORES)))
    y = np.stack([r["y_out"] for r in res.results])
    return np.ascontiguousarray(y.transpose(0, 2, 1))


# version 05 (final)
# speedup vs baseline: 1.3169x; 1.3052x over previous
import numpy as np
import ml_dtypes
import concourse.bass as bass
import concourse.mybir as mybir
from concourse.bass_utils import run_bass_kernel_spmd

F32 = mybir.dt.float32
BF16 = mybir.dt.bfloat16
AF = mybir.ActivationFunctionType
ALU = mybir.AluOpType
AX = mybir.AxisListType

BF = ml_dtypes.bfloat16
NCORES = 8
T = 2048
D = 1024
NCH = D // 128

ENGINES = ("t", "v", "s", "p", "y")


class Buf:
    __slots__ = ("ap", "name", "wtok", "rtoks", "_addr", "_size", "_full")

    def __init__(self, ap, name=""):
        self.ap = ap
        self.name = name
        self.wtok = None
        self.rtoks = []
        self._addr = None
        self._size = None
        self._full = None


class Prog:
    def __init__(self, name="prog"):
        self.nc = bass.Bass("TRN2", target_bir_lowering=False, debug=False)
        self.name = name
        self.ops = {e: [] for e in ENGINES}
        self._ctxs = []

    def emit(self, eng, fn, reads=(), writes=(), kind="c", after=()):
        waits = set()

        def add(tok):
            if tok is None:
                return
            peng, pkind, _ = tok
            if peng == eng and pkind == "d" and kind == "d":
                return
            waits.add(tok)

        for b in reads:
            add(b.wtok)
        for b in writes:
            add(b.wtok)
            for t in b.rtoks:
                add(t)
        for t in after:
            add(t)
        seq = len(self.ops[eng])
        tok = (eng, kind, seq)
        self.ops[eng].append({"fn": fn, "kind": kind, "waits": waits, "inc": False})
        for b in writes:
            b.wtok = tok
            b.rtoks = []
        wset = set(id(b) for b in writes)
        for b in reads:
            if id(b) not in wset:
                b.rtoks.append(tok)
        return tok

    def dma(self, queue, dst_ap, src_ap, reads=(), writes=(), after=()):
        return self.emit(queue, lambda E: E.dma_start(dst_ap, src_ap),
                         reads=reads, writes=writes, kind="d", after=after)

    NDMASEM = 8
    NCSEM = 4

    def build(self, final_waits=()):
        nc = self.nc
        needed = set()
        for e in ENGINES:
            for op in self.ops[e]:
                needed.update(op["waits"])
        needed.update(final_waits)
        for (eng, kind, seq) in needed:
            self.ops[eng][seq]["inc"] = True
        for e in ENGINES:
            for op in self.ops[e]:
                if op["kind"] == "d":
                    op["inc"] = True
        semval = {}
        semslot = {}
        for e in ENGINES:
            cslot_count = [0] * self.NCSEM
            cix = 0
            dslot_count = [0] * self.NDMASEM
            dix = 0
            for i, op in enumerate(self.ops[e]):
                if op["inc"]:
                    if op["kind"] == "d":
                        s = dix % self.NDMASEM
                        dix += 1
                        dslot_count[s] += 16
                        semval[(e, "d", i)] = dslot_count[s]
                        semslot[(e, "d", i)] = s
                    else:
                        s = cix % self.NCSEM
                        cix += 1
                        cslot_count[s] += 1
                        semval[(e, "c", i)] = cslot_count[s]
                        semslot[(e, "c", i)] = s
        sems = {}
        for e in ENGINES:
            for s in range(self.NCSEM):
                sems[(e, "c", s)] = self._enter(nc.semaphore(f"sem_{e}c{s}"))
            for s in range(self.NDMASEM):
                sems[(e, "d", s)] = self._enter(nc.semaphore(f"sem_{e}d{s}"))
        block = self._enter(nc.Block())
        water = {}

        def sem_of(tok):
            pe, pk, _ = tok
            return (pe, pk, semslot.get(tok, 0))

        def run_engine(e, E):
            prev_dma = None
            for i, op in enumerate(self.ops[e]):
                for tok in sorted(op["waits"]):
                    v = semval[tok]
                    sk = sem_of(tok)
                    key = (e,) + sk
                    if water.get(key, 0) >= v:
                        continue
                    water[key] = v
                    E.wait_ge(sems[sk], v)
                if op["kind"] == "d":
                    if e == "y":
                        if prev_dma is not None:
                            v = semval[prev_dma]
                            sk = sem_of(prev_dma)
                            key = (e,) + sk
                            if water.get(key, 0) < v:
                                water[key] = v
                                E.wait_ge(sems[sk], v)
                        prev_dma = (e, "d", i)
                    else:
                        tok = (e, "d", i)
                        sk = sem_of(tok)
                        v = semval[tok] - 16
                        key = (e,) + sk
                        if v > 0 and water.get(key, 0) < v:
                            water[key] = v
                            E.wait_ge(sems[sk], v)
                inst = op["fn"](E)
                if op["inc"]:
                    k = op["kind"]
                    sk = sem_of((e, k, i))
                    inst.then_inc(sems[sk], 16 if k == "d" else 1)
            if e == "y":
                for tok in sorted(final_waits):
                    v = semval[tok]
                    sk = sem_of(tok)
                    key = (e,) + sk
                    if water.get(key, 0) < v:
                        water[key] = v
                        E.wait_ge(sems[sk], v)

        @block.tensor
        def _(E):
            run_engine("t", E)

        @block.vector
        def _(E):
            run_engine("v", E)

        @block.scalar
        def _(E):
            run_engine("s", E)

        @block.gpsimd
        def _(E):
            run_engine("p", E)

        @block.sync
        def _(E):
            run_engine("y", E)
        return nc

    def _enter(self, cm):
        v = cm.__enter__()
        self._ctxs.append(cm)
        return v

    def make_arena(self, words=53200):
        self.arena_t = self._enter(self.nc.sbuf_tensor("arena", [128, words], F32))
        self.freelist = [(0, words)]
        self.region_hist = []
        self.scopes = []
        self.banks = [Buf(self._enter(self.nc.psum_tensor(f"bank{i}", [128, 512], F32))[:, :],
                          f"bank{i}") for i in range(8)]
        return self

    def alloc(self, cols, dtype=F32, name=""):
        w = cols if dtype == F32 else (cols + 1) // 2
        w = (w + 3) & ~3
        for i, (a, sz) in enumerate(self.freelist):
            if sz >= w:
                if sz == w:
                    self.freelist.pop(i)
                else:
                    self.freelist[i] = (a + w, sz - w)
                region = self.arena_t[:, a:a + w]
                ap = region[:, :cols] if dtype == F32 else region.bitcast(BF16)[:, :cols]
                b = Buf(ap, name)
                b._addr, b._size = a, w
                keep = []
                for (ha, hs, toks) in self.region_hist:
                    if ha < a + w and a < ha + hs:
                        b.rtoks.extend(toks)
                        if ha < a:
                            keep.append((ha, a - ha, toks))
                        if ha + hs > a + w:
                            keep.append((a + w, ha + hs - (a + w), toks))
                    else:
                        keep.append((ha, hs, toks))
                self.region_hist = keep
                if self.scopes:
                    self.scopes[-1].append(b)
                return b
        raise AssertionError(f"arena OOM: {name} needs {w} words; free={self.freelist}")

    def push_scope(self):
        self.scopes.append([])

    def pop_scope(self):
        for b in self.scopes.pop():
            if b._addr is not None:
                self.free(b)

    def free(self, *bufs):
        for b in bufs:
            assert b._addr is not None, f"double free {b.name}"
            toks = list(b.rtoks) + ([b.wtok] if b.wtok else [])
            self.region_hist.append((b._addr, b._size, toks))
            self.freelist.append((b._addr, b._size))
            b._addr = None
            self.freelist.sort()
            merged = []
            for a, sz in self.freelist:
                if merged and merged[-1][0] + merged[-1][1] == a:
                    merged[-1] = (merged[-1][0], merged[-1][1] + sz)
                else:
                    merged.append((a, sz))
            self.freelist = [tuple(m) for m in merged]

    def dram(self, name, shape, dtype=F32, kind="ExternalInput"):
        return self.nc.dram_tensor(name, list(shape), dtype, kind=kind).ap()



def act(P, out_buf, out_ap, in_bufs, in_ap, func, bias=None, scale=None,
        accum_ap=None, accum_buf=None, extra_reads=()):
    kw = {}
    if bias is not None:
        kw["bias"] = bias
    if scale is not None:
        kw["scale"] = scale
    if accum_ap is not None:
        kw["accum_out"] = accum_ap
    wr = [out_buf] + ([accum_buf] if accum_buf is not None else [])
    return P.emit("s", lambda E: E.activation(out_ap, in_ap, func, **kw),
                  reads=list(in_bufs) + list(extra_reads), writes=wr)


def load_weight(P, dram_ap, nch_in, dout, name, queue="y", slot=None, stage=None,
                cast_eng="v", col0=0):
    if slot is None:
        wb = P.alloc(nch_in * dout, BF16, name)
        w3 = wb.ap.rearrange("p (c d) -> p c d", c=nch_in)
    else:
        wb = slot
        if wb._full is None:
            flat = slot.ap
            if len(flat.shape) == 3:
                flat = flat.rearrange("p c d -> p (c d)")
            wb._full = flat
        flat = wb._full
        assert flat.shape[1] >= nch_in * dout, f"wslot too small for {name}"
        w3 = flat[:, :nch_in * dout].rearrange("p (c d) -> p c d", c=nch_in)
    src = dram_ap[0:nch_in * 128, col0:col0 + dout].rearrange(
        "(c p) d -> p c d", p=128)
    P.dma("p", w3, src, writes=[wb])
    if slot is None:
        wb.ap = w3
        return wb
    return _SharedView(w3, name, wb)


class WSlots:

    def __init__(self, P, cols, name, n=2):
        self.bufs = [P.alloc(cols, BF16, f"{name}{i}") for i in range(n)]
        self.i = -1

    def __call__(self):
        self.i += 1
        return self.bufs[self.i % len(self.bufs)]


def load_vec(P, dram_ap, name, n=NCH):
    v = P.alloc(n, F32, name)
    P.dma("p", v.ap, dram_ap, writes=[v])
    return v


class VecPack:
    NCOLS = 256

    def __init__(self, ncols=None):
        self.NCOLS = ncols or VecPack.NCOLS
        self.host = np.zeros((128, self.NCOLS), np.float32)
        self.off = 0
        self.slices = {}

    def add_cols(self, name, arr):
        arr = np.asarray(arr, np.float32)
        assert arr.shape[0] == 128
        k = arr.shape[1]
        self.host[:, self.off:self.off + k] = arr
        self.slices[name] = ("c", self.off, k)
        self.off += k
        return self

    def add_row(self, name, arr):
        arr = np.asarray(arr, np.float32).reshape(-1)
        k = len(arr)
        self.host[0, self.off:self.off + k] = arr
        self.slices[name] = ("r", self.off, k)
        self.off += k
        return self

    def add_block(self, name, arr):
        arr = np.asarray(arr, np.float32)
        p, k = arr.shape
        self.host[0:p, self.off:self.off + k] = arr
        self.slices[name] = (("b", p), self.off, k)
        self.off += k
        return self

    def load(self, P, dram_ap):
        buf = P.alloc(self.NCOLS, F32, "vecpack")
        P.dma("y", buf.ap, dram_ap, writes=[buf])
        out = {}
        for name, (kind, off, k) in self.slices.items():
            if kind == "r":
                ap = buf.ap[0:1, off:off + k]
            elif isinstance(kind, tuple) and kind[0] == "b":
                ap = buf.ap[0:kind[1], off:off + k]
            else:
                ap = buf.ap[:, off:off + k]
            out[name] = _SharedView(ap, name, buf)
        return out


class _SharedView:
    __slots__ = ("ap", "name", "parent")

    def __init__(self, ap, name, parent):
        self.ap = ap
        self.name = name
        self.parent = parent

    @property
    def wtok(self):
        return self.parent.wtok

    @property
    def rtoks(self):
        return self.parent.rtoks


def load_row(P, dram_ap, cols, name, dtype=F32, queue="p"):
    v = P.alloc(cols, dtype, name)
    row = v.ap[0:1, :]
    P.dma(queue, row, dram_ap, writes=[v])
    v.ap = row
    return v


def proj(P, wb, rhs_fn, rhs_bufs, nch_in, dout_chunks, evac, blkw=512, nblk=None,
         bank_off=0, nbanks=None):
    nblk = nblk if nblk is not None else (T // blkw)
    banks = P.banks[bank_off:(bank_off + nbanks if nbanks else len(P.banks))]
    for dc in range(dout_chunks):
        for blk in range(nblk):
            bank = banks[(dc * nblk + blk) % len(banks)]
            for c in range(nch_in):
                P.emit("t", lambda E, c=c, dc=dc, blk=blk, bank=bank: E.matmul(
                    bank.ap[:, :blkw], wb.ap[:, c, dc * 128:(dc + 1) * 128],
                    rhs_fn(c, blk), start=(c == 0), stop=(c == nch_in - 1)),
                    reads=[wb] + list(rhs_bufs), writes=[bank])
            evac(dc, blk, bank)


def global_ln(P, x, xn, c_ones_col, c_ones_row):
    x3 = x.ap
    xn3 = xn.ap
    P.emit("v", lambda E: E.memset(xn3[:, :, 0:1], 0.0), writes=[xn])
    sq = P.alloc(T, F32, "ln_sq")
    rows1 = P.alloc(2 * T, F32, "ln_rows")
    rows2 = rows1
    A_bc = P.alloc(T, F32, "ln_Abc")
    B_bc = P.alloc(T, F32, "ln_Bbc")
    bsum = P.banks[:4]
    bsq = P.banks[4:8]
    for c in range(NCH):
        act(P, sq, sq.ap, [x], x3[:, c, :], AF.Square)
        for blk in range(4):
            sl = slice(blk * 512, (blk + 1) * 512)
            P.emit("t", lambda E, c=c, blk=blk, sl=sl: E.matmul(
                bsum[blk].ap[0:1, :], c_ones_col.ap, x3[:, c, sl],
                start=(c == 0), stop=(c == NCH - 1)), reads=[x, c_ones_col], writes=[bsum[blk]])
            P.emit("t", lambda E, c=c, blk=blk, sl=sl: E.matmul(
                bsq[blk].ap[0:1, :], c_ones_col.ap, sq.ap[:, sl],
                start=(c == 0), stop=(c == NCH - 1)), reads=[sq, c_ones_col], writes=[bsq[blk]])
    mean = rows1.ap[0:1, 0 * T:1 * T]
    msq = rows1.ap[0:1, 1 * T:2 * T]
    sd = msq
    A = msq
    for blk in range(4):
        sl = slice(blk * 512, (blk + 1) * 512)
        P.emit("v", lambda E, blk=blk, sl=sl: E.tensor_scalar(
            mean[:, sl], bsum[blk].ap[0:1, :], 1.0 / D, None, ALU.mult, ALU.bypass),
            reads=[bsum[blk]], writes=[rows1])
        P.emit("v", lambda E, blk=blk, sl=sl: E.tensor_scalar(
            msq[:, sl], bsq[blk].ap[0:1, :], 1.0 / D, 1e-5, ALU.mult, ALU.add),
            reads=[bsq[blk]], writes=[rows1])
    m2row = sq.ap[0:1, 0:T]
    P.emit("v", lambda E: E.tensor_tensor(m2row, mean, mean, ALU.mult), reads=[rows1], writes=[sq])
    P.emit("v", lambda E: E.tensor_tensor(msq, msq, m2row, ALU.subtract),
           reads=[rows1, sq], writes=[rows1])
    P.emit("s", lambda E: E.activation(sd, msq, AF.Sqrt), reads=[rows1], writes=[rows1])
    P.emit("v", lambda E: E.reciprocal(A, sd), reads=[rows1], writes=[rows1])
    P.emit("v", lambda E: E.scalar_tensor_tensor(mean, mean, -1.0, A, ALU.mult, ALU.mult),
           reads=[rows1], writes=[rows1])
    for (src_buf, srow, dst) in ((rows2, A, A_bc), (rows1, mean, B_bc)):
        for blk in range(4):
            sl = slice(blk * 512, (blk + 1) * 512)
            bank = P.banks[blk]
            P.emit("t", lambda E, srow=srow, sl=sl, bank=bank: E.matmul(
                bank.ap[:, :], c_ones_row.ap, srow[:, sl], start=True, stop=True),
                reads=[src_buf, c_ones_row], writes=[bank])
            P.emit("v", lambda E, dst=dst, sl=sl, bank=bank: E.tensor_copy(
                dst.ap[:, sl], bank.ap), reads=[bank], writes=[dst])
    for c in range(NCH):
        P.emit("v", lambda E, c=c: E.tensor_tensor(sq.ap, x3[:, c, :], A_bc.ap, ALU.mult),
               reads=[x, A_bc], writes=[sq])
        P.emit("v", lambda E, c=c: E.tensor_tensor(xn3[:, c, 1:T + 1], sq.ap, B_bc.ap, ALU.add),
               reads=[sq, B_bc], writes=[xn])
    P.free(sq, rows1, A_bc, B_bc)


def alloc_xn(P):
    xn = P.alloc(NCH * (T + 1) + 1, BF16, "xn")
    xn.ap = xn.ap[:, :NCH * (T + 1)].rearrange("p (c t) -> p c t", c=NCH)
    return xn


def shared_token_mix(P, xn, mu_cols_names):
    outs = []
    for mu, name in mu_cols_names:
        b = P.alloc(NCH * T, BF16, name)
        b.ap = b.ap.rearrange("p (c t) -> p c t", c=NCH)
        outs.append(b)
    d = P.alloc(T, F32, "mix_d")
    for c in range(NCH):
        cur = xn.ap[:, c, 1:T + 1]
        prev = xn.ap[:, c, 0:T]
        P.emit("v", lambda E, cur=cur, prev=prev: E.tensor_tensor(d.ap, cur, prev, ALU.subtract),
               reads=[xn], writes=[d])
        for (mu, name), ob in zip(mu_cols_names, outs):
            P.emit("v", lambda E, c=c, prev=prev, mu=mu, ob=ob: E.scalar_tensor_tensor(
                ob.ap[:, c, :], d.ap, mu.ap[:, c:c + 1], prev, ALU.mult, ALU.add),
                reads=[d, xn, mu], writes=[ob])
    P.free(d)
    return outs


def fused_update_evac(P, x_d, y_d, g3, xst, yst, sgn=1.0):
    tmp = P.alloc(512, F32, "upd_tmp")
    state = {"dc": -1}

    def evac(dc, blk, bank):
        if state["dc"] != dc:
            state["dc"] = dc
            xb = xst[dc % 2]
            P.dma("y", xb.ap, x_d[dc * 128:(dc + 1) * 128, :], writes=[xb])
        xb = xst[dc % 2]
        yb = yst[dc % 2]
        sl = slice(blk * 512, (blk + 1) * 512)
        P.emit("v", lambda E: E.tensor_tensor(tmp.ap, bank.ap, g3.ap[:, dc, sl], ALU.mult),
               reads=[bank, g3], writes=[tmp])
        P.emit("v", lambda E: E.tensor_tensor(yb.ap[:, sl], tmp.ap, xb.ap[:, sl], ALU.add),
               reads=[tmp, xb], writes=[yb])
        if blk == 3:
            P.dma("y", y_d[dc * 128:(dc + 1) * 128, :], yb.ap, reads=[yb])
    return evac



def l1_pack(ins=None):
    vp = VecPack()
    z8 = np.zeros((128, NCH), np.float32)
    get = (lambda k: _col8(k)) if ins is not None else (lambda k: z8)
    if ins is not None:
        vp.add_cols("v_td", _col8(ins["L1_attn_time_decay"]))
        vp.add_cols("v_tf", _col8(ins["L1_attn_time_first"]))
        vp.add_cols("v_mk", _col8(ins["L1_attn_mix_k"].reshape(-1)))
        vp.add_cols("v_mv", _col8(ins["L1_attn_mix_v"].reshape(-1)))
        vp.add_cols("v_mr", _col8(ins["L1_attn_mix_r"].reshape(-1)))
        vp.add_cols("v_mur", _col8(ins["L1_ffn_mu_r"]))
        vp.add_cols("v_muk", _col8(ins["L1_ffn_mu_k"]))
    else:
        for n in ["v_td", "v_tf", "v_mk", "v_mv", "v_mr", "v_mur", "v_muk"]:
            vp.add_cols(n, z8)
    vp.add_cols("ones_col", np.ones((128, 1), np.float32))
    vp.add_row("ones_row", np.ones(128, np.float32))
    return vp


def build_l1(P=None, x_d=None, y_d=None, pfx="", dump=None):
    if P is None:
        P = Prog("l1").make_arena()
        x_d = P.dram("x_in", [D, T])
        y_d = P.dram("y_out", [D, T], kind="ExternalOutput")
    P.push_scope()
    W = {n: P.dram(pfx + n, [D, D]) for n in ["wkT", "wvT", "wrT", "woT", "frT", "fkT", "fvT"]}
    vecs_d = P.dram(pfx + "vecs", [128, VecPack.NCOLS])
    vec = l1_pack().load(P, vecs_d)
    c_ones_col = vec["ones_col"]
    c_ones_row = vec["ones_row"]

    def dump_vp(tag):
        parent = vec["v_td"].parent
        outb = P.alloc(64, F32, "dumpout")
        P.emit("v", lambda E: E.tensor_copy(outb.ap, parent.ap[:, 0:64]), reads=[parent], writes=[outb])
        P.dma("y", y_d[0:128, 0:64], outb.ap, reads=[outb])
        return P, [("y", "d", len(P.ops["y"]) - 1)]

    def dump_out(sb, guard=0):
        outb = P.alloc(T, F32, "dumpout")
        for c in range(NCH):
            sl = sb.ap[:, c, guard:T + guard] if guard else sb.ap[:, c, :]
            P.emit("v", lambda E, sl=sl: E.tensor_copy(outb.ap, sl), reads=[sb], writes=[outb])
            P.dma("y", y_d[c * 128:(c + 1) * 128, :], outb.ap, reads=[outb])
        return P, [("y", "d", len(P.ops["y"]) - 1)]


    xn = alloc_xn(P)
    g3 = P.alloc(NCH * T, BF16, "g")
    g3.ap = g3.ap.rearrange("p (c t) -> p c t", c=NCH)
    wslot = WSlots(P, NCH * D, "wslot")

    x = P.alloc(NCH * T, F32, "x")
    x.ap = x.ap.rearrange("p (c t) -> p c t", c=NCH)
    P.dma("y", x.ap, x_d.rearrange("(c p) t -> p c t", p=128), writes=[x])
    global_ln(P, x, xn, c_ones_col, c_ones_row)
    P.free(x)

    xr, xk = shared_token_mix(P, xn, [(vec["v_mur"], "xr"), (vec["v_muk"], "xk")])
    wbuf = load_weight(P, W["frT"], NCH, D, "w_fr", slot=wslot())
    rs = P.alloc(NCH * T, BF16, "rs")
    rs.ap = rs.ap.rearrange("p (c t) -> p c t", c=NCH)

    def evac_sigmoid(dc, blk, bank):
        sl = slice(blk * 512, (blk + 1) * 512)
        act(P, rs, rs.ap[:, dc, sl], [bank], bank.ap, AF.Sigmoid)
    proj(P, wbuf, lambda c, blk: xr.ap[:, c, blk * 512:(blk + 1) * 512], [xr],
         NCH, NCH, evac_sigmoid)
    P.free(xr)

    wbuf = load_weight(P, W["fkT"], NCH, D, "w_fk", slot=wslot())
    ka = P.alloc(NCH * T, BF16, "ka")
    ka.ap = ka.ap.rearrange("p (c t) -> p c t", c=NCH)
    relu_t = P.alloc(512, F32, "relu_t")

    def evac_relu_sq(dc, blk, bank):
        sl = slice(blk * 512, (blk + 1) * 512)
        act(P, relu_t, relu_t.ap, [bank], bank.ap, AF.Relu)
        P.emit("v", lambda E: E.tensor_tensor(ka.ap[:, dc, sl], relu_t.ap, relu_t.ap, ALU.mult),
               reads=[relu_t], writes=[ka])
    proj(P, wbuf, lambda c, blk: xk.ap[:, c, blk * 512:(blk + 1) * 512], [xk],
         NCH, NCH, evac_relu_sq)
    P.free(xk, relu_t)

    wbuf = load_weight(P, W["fvT"], NCH, D, "w_fv", slot=wslot())
    wv_t = P.alloc(512, F32, "wv_t")

    def evac_g(dc, blk, bank):
        sl = slice(blk * 512, (blk + 1) * 512)
        P.emit("v", lambda E: E.tensor_tensor(wv_t.ap, bank.ap, rs.ap[:, dc, sl], ALU.mult),
               reads=[bank, rs], writes=[wv_t])
        act(P, g3, g3.ap[:, dc, sl], [wv_t], wv_t.ap, AF.Silu)
    proj(P, wbuf, lambda c, blk: ka.ap[:, c, blk * 512:(blk + 1) * 512], [ka],
         NCH, NCH, evac_g)
    P.free(ka, rs, wv_t)

    if dump == "g":
        return dump_out(g3)
    kin, vin, rin = shared_token_mix(P, xn, [(vec["v_mk"], "kin"), (vec["v_mv"], "vin"),
                                             (vec["v_mr"], "rin")])
    if dump == "kin":
        return dump_out(kin)
    P.free(xn)
    wbuf = load_weight(P, W["wrT"], NCH, D, "w_r", slot=wslot())
    rs2 = P.alloc(NCH * T, BF16, "rs2")
    rs2.ap = rs2.ap.rearrange("p (c t) -> p c t", c=NCH)

    def evac_sig2(dc, blk, bank):
        sl = slice(blk * 512, (blk + 1) * 512)
        act(P, rs2, rs2.ap[:, dc, sl], [bank], bank.ap, AF.Sigmoid)
    proj(P, wbuf, lambda c, blk: rin.ap[:, c, blk * 512:(blk + 1) * 512], [rin],
         NCH, NCH, evac_sig2)
    if dump == "rsig":
        return dump_out(rs2)
    P.free(rin)
    wbuf = load_weight(P, W["wvT"], NCH, D, "w_v", slot=wslot())
    vsb = P.alloc(NCH * T, BF16, "vsb")
    vsb.ap = vsb.ap.rearrange("p (c t) -> p c t", c=NCH)

    def evac_copy(dc, blk, bank):
        sl = slice(blk * 512, (blk + 1) * 512)
        P.emit("v", lambda E: E.tensor_copy(vsb.ap[:, dc, sl], bank.ap), reads=[bank],
               writes=[vsb])
    proj(P, wbuf, lambda c, blk: vin.ap[:, c, blk * 512:(blk + 1) * 512], [vin],
         NCH, NCH, evac_copy)
    if dump == "vsb":
        return dump_out(vsb)
    P.free(vin)

    a_col = P.alloc(NCH, F32, "a_col")
    eu_col = P.alloc(NCH, F32, "eu_col")
    act(P, a_col, a_col.ap, [vec["v_td"]], vec["v_td"].ap, AF.Exp)
    act(P, a_col, a_col.ap, [a_col], a_col.ap, AF.Exp, scale=-1.0)
    act(P, eu_col, eu_col.ap, [vec["v_tf"]], vec["v_tf"].ap, AF.Exp)

    wbuf = load_weight(P, W["wkT"], NCH, D, "w_k", slot=wslot())
    ebuf = P.alloc(T, F32, "ebuf")
    enbuf = P.alloc(T, F32, "enbuf")
    num = P.alloc(T + 4, F32, "num")
    den = P.alloc(T + 4, F32, "den")
    P.emit("v", lambda E: E.memset(num.ap[:, 0:1], 0.0), writes=[num])
    P.emit("v", lambda E: E.memset(den.ap[:, 0:1], 0.0), writes=[den])

    ndc = NCH
    if dump and dump.startswith("ndc"):
        ndc = int(dump[3:])
    for dc in range(ndc):
        for blk in range(4):
            bank = P.banks[blk]
            sl = slice(blk * 512, (blk + 1) * 512)
            for c in range(NCH):
                P.emit("t", lambda E, c=c, dc=dc, bank=bank, sl=sl: E.matmul(
                    bank.ap, wbuf.ap[:, c, dc * 128:(dc + 1) * 128], kin.ap[:, c, sl],
                    start=(c == 0), stop=(c == NCH - 1)), reads=[wbuf, kin], writes=[bank])
            act(P, ebuf, ebuf.ap[:, sl], [bank], bank.ap, AF.Exp)
        P.emit("v", lambda E, dc=dc: E.tensor_tensor(enbuf.ap, ebuf.ap, vsb.ap[:, dc, :],
                                                     ALU.mult), reads=[ebuf, vsb], writes=[enbuf])
        a_b = a_col.ap[:, dc:dc + 1].broadcast_to([128, T])
        P.emit("v", lambda E, a_b=a_b: E.tensor_tensor_scan(
            num.ap[:, 1:T + 1], a_b, enbuf.ap, 0.0, ALU.mult, ALU.add),
            reads=[a_col, enbuf], writes=[num])
        P.emit("v", lambda E, a_b=a_b: E.tensor_tensor_scan(
            den.ap[:, 1:T + 1], a_b, ebuf.ap, 0.0, ALU.mult, ALU.add),
            reads=[a_col, ebuf], writes=[den])
        P.emit("v", lambda E, dc=dc: E.scalar_tensor_tensor(
            enbuf.ap, enbuf.ap, eu_col.ap[:, dc:dc + 1], num.ap[:, 0:T], ALU.mult, ALU.add),
            reads=[enbuf, eu_col, num], writes=[enbuf])
        P.emit("v", lambda E, dc=dc: E.scalar_tensor_tensor(
            ebuf.ap, ebuf.ap, eu_col.ap[:, dc:dc + 1], den.ap[:, 0:T], ALU.mult, ALU.add),
            reads=[ebuf, eu_col, den], writes=[ebuf])
        P.emit("v", lambda E: E.reciprocal(ebuf.ap, ebuf.ap), reads=[ebuf], writes=[ebuf])
        P.emit("v", lambda E: E.tensor_tensor(enbuf.ap, enbuf.ap, ebuf.ap, ALU.mult),
               reads=[enbuf, ebuf], writes=[enbuf])
        P.emit("v", lambda E, dc=dc: E.tensor_tensor(rs2.ap[:, dc, :], rs2.ap[:, dc, :],
                                                     enbuf.ap, ALU.mult),
               reads=[rs2, enbuf], writes=[rs2])
    P.free(ebuf, enbuf, num, den, kin, vsb)

    if dump == "rwkv" or (dump and dump.startswith("ndc")):
        return dump_out(rs2)
    wbuf = load_weight(P, W["woT"], NCH, D, "w_o", slot=wslot())
    xst = [P.alloc(T, F32, f"xst{i}") for i in range(2)]
    yst = [P.alloc(T, F32, f"yst{i}") for i in range(2)]
    evac = fused_update_evac(P, x_d, y_d, g3, xst, yst)
    out_toks = []

    def evac_upd(dc, blk, bank):
        evac(dc, blk, bank)
    proj(P, wbuf, lambda c, blk: rs2.ap[:, c, blk * 512:(blk + 1) * 512], [rs2],
         NCH, NCH, evac_upd)
    last_y = ("y", "d", len(P.ops["y"]) - 1)
    P.pop_scope()
    return P, [last_y]




def l3_pack(ins=None):
    vp = VecPack(512)
    if ins is not None:
        vp.add_cols("qn", np.tile(np.asarray(ins["L3_attn_qn_w"], np.float32), 2)[:, None])
        vp.add_cols("kn", np.tile(np.asarray(ins["L3_attn_kn_w"], np.float32), 2)[:, None])
        vp.add_cols("b_proj", _col8(ins["L3_attn_b_proj"]))
        vp.add_cols("b_qkv", _f32(np.asarray(ins["L3_attn_b_qkv"]).reshape(24, 128).T))
        vp.add_cols("w1s", _f32(np.asarray(ins["L3_ffn_w1_s"]).reshape(32, 128).T))
        vp.add_cols("w2s", _f32(np.asarray(ins["L3_ffn_w2_s"]).reshape(32, 128).T))
        vp.add_cols("h1s", _f32(np.asarray(ins["L3_ffn_h1_s"]).reshape(32, 128).T))
        vp.add_cols("h2s", _f32(np.asarray(ins["L3_ffn_h2_s"]).reshape(32, 128).T))
        vp.add_cols("w3s", _col8(ins["L3_ffn_w3_s"]))
    else:
        z = np.zeros((128, 1), np.float32)
        vp.add_cols("qn", z); vp.add_cols("kn", z)
        vp.add_cols("b_proj", np.zeros((128, 8), np.float32))
        vp.add_cols("b_qkv", np.zeros((128, 24), np.float32))
        for n in ("w1s", "w2s", "h1s", "h2s"):
            vp.add_cols(n, np.zeros((128, 32), np.float32))
        vp.add_cols("w3s", np.zeros((128, 8), np.float32))
    vp.add_cols("ones_col", np.ones((128, 1), np.float32))
    vp.add_row("ones_row", np.ones(128, np.float32))
    b2 = np.zeros((2, 128), np.float32)
    b2[0, 0:64] = 1
    b2[1, 64:128] = 1
    vp.add_block("B2", b2)
    return vp


def _bf_consts():
    c = np.zeros((128, 192), BF)
    c[:, 0:128] = np.eye(128, dtype=BF)
    c[0:64, 128] = 1
    c[64:128, 129] = 1
    return c


def build_l3(P=None, x_d=None, y_d=None, pfx="", dump=None):
    if P is None:
        P = Prog("l3").make_arena()
        x_d = P.dram("x_in", [D, T])
        y_d = P.dram("y_out", [D, T], kind="ExternalOutput")
    P.push_scope()
    s_d = P.dram(pfx + "s_spill", [D, T], kind="Internal")
    wqkv_d = P.dram(pfx + "wqkvT", [D, 3 * D])
    wproj_d = P.dram(pfx + "wprojT", [D, D])
    w1_d = P.dram(pfx + "w1T", [D, 4096])
    w2_d = P.dram(pfx + "w2T", [D, 4096])
    h1_d = P.dram(pfx + "h1T", [4096, 4096])
    h2_d = P.dram(pfx + "h2T", [4096, 4096])
    w3_d = P.dram(pfx + "w3T", [4096, D])
    vecs_d = P.dram(pfx + "vecs", [128, 512])
    cbf_d = P.dram(pfx + "cbf", [128, 192], BF16)
    cid_d = P.dram(pfx + "cid32", [128, 128])

    vec = l3_pack().load(P, vecs_d)
    c_ones_col = vec["ones_col"]
    c_ones_row = vec["ones_row"]
    cbf = P.alloc(192, BF16, "cbf")
    P.dma("y", cbf.ap, cbf_d, writes=[cbf])
    cid = P.alloc(128, F32, "cid32")
    P.dma("y", cid.ap, cid_d, writes=[cid])
    ident = cbf.ap[:, 0:128]
    E2 = cbf.ap[:, 128:130]
    obf = P.alloc(128, BF16, "ones_row_bf")
    P.emit("v", lambda E: E.memset(obf.ap[0:1, :], 1.0), writes=[obf])

    def dump_fm(sb3, nch=NCH, guard=0):
        outb = P.alloc(T, F32, "dumpout")
        for c in range(nch):
            sl = sb3.ap[:, c, guard:T + guard]
            P.emit("v", lambda E, sl=sl: E.tensor_copy(outb.ap, sl), reads=[sb3], writes=[outb])
            P.dma("y", y_d[(c % NCH) * 128:((c % NCH) + 1) * 128, :], outb.ap, reads=[outb])
        return P, [("y", "d", len(P.ops["y"]) - 1)]

    xn = alloc_xn(P)
    x = P.alloc(NCH * T, F32, "x")
    x.ap = x.ap.rearrange("p (c t) -> p c t", c=NCH)
    P.dma("y", x.ap, x_d.rearrange("(c p) t -> p c t", p=128), writes=[x])
    global_ln(P, x, xn, c_ones_col, c_ones_row)
    P.free(x)

    qkv = P.alloc(24 * T, BF16, "qkv")
    qkv.ap = qkv.ap.rearrange("p (c t) -> p c t", c=24)
    wslot = P.alloc(NCH * D, BF16, "wslot")
    for j in range(3):
        wb = load_weight(P, wqkv_d, NCH, D, "wqkv", slot=wslot, col0=j * D)

        def evac_qkv(dc, blk, bank, j=j):
            gdc = j * NCH + dc
            sl = slice(blk * 512, (blk + 1) * 512)
            P.emit("v", lambda E, gdc=gdc, sl=sl, bank=bank: E.tensor_scalar(
                qkv.ap[:, gdc, sl], bank.ap, vec["b_qkv"].ap[:, gdc:gdc + 1], None,
                ALU.add, ALU.bypass), reads=[bank, vec["b_qkv"]], writes=[qkv])
        proj(P, wb, lambda c, blk: xn.ap[:, c, 1 + blk * 512:1 + (blk + 1) * 512], [xn],
             NCH, NCH, evac_qkv)
    P.free(wslot)
    if dump == "qkv":
        return dump_fm(qkv, 24)

    sqb = P.alloc(T, BF16, "rms_sq")
    B2 = vec["B2"]
    rows2h = P.alloc(T, F32, "rms_rows")
    rbc = P.alloc(512, F32, "rms_rbc")
    tmul = P.alloc(512, F32, "rms_tmul")

    for kind in ("q", "k"):
        wcol = vec["qn" if kind == "q" else "kn"]
        for c in range(NCH):
            ch = c if kind == "q" else NCH + c
            for blk in range(4):
                sl = slice(blk * 512, (blk + 1) * 512)
                act(P, sqb, sqb.ap[:, sl], [qkv], qkv.ap[:, ch, sl], AF.Square)
                P.emit("t", lambda E, sl=sl, blk=blk: E.matmul(
                    P.banks[blk].ap[0:2, :], E2, sqb.ap[:, sl], start=True, stop=True),
                    reads=[sqb, cbf], writes=[P.banks[blk]])
            for blk in range(4):
                sl = slice(blk * 512, (blk + 1) * 512)
                P.emit("v", lambda E, sl=sl, blk=blk: E.tensor_scalar(
                    rows2h.ap[0:2, sl], P.banks[blk].ap[0:2, :], 1.0 / 64, 1e-6,
                    ALU.mult, ALU.add), reads=[P.banks[blk]], writes=[rows2h])
            P.emit("s", lambda E: E.activation(rows2h.ap[0:2, :], rows2h.ap[0:2, :],
                   AF.Sqrt), reads=[rows2h], writes=[rows2h])
            P.emit("v", lambda E: E.reciprocal(rows2h.ap[0:2, :], rows2h.ap[0:2, :]),
                   reads=[rows2h], writes=[rows2h])
            for blk in range(4):
                sl = slice(blk * 512, (blk + 1) * 512)
                bank = P.banks[4 + blk % 2]
                P.emit("t", lambda E, sl=sl, bank=bank: E.matmul(
                    bank.ap, B2.ap[0:2, :], rows2h.ap[0:2, sl], start=True, stop=True),
                    reads=[rows2h, B2], writes=[bank])
                P.emit("v", lambda E, ch=ch, sl=sl, bank=bank: E.tensor_tensor(
                    tmul.ap, qkv.ap[:, ch, sl], bank.ap, ALU.mult),
                    reads=[qkv, bank], writes=[tmul])
                if kind == "q":
                    P.emit("v", lambda E, ch=ch, sl=sl: E.tensor_scalar(
                        qkv.ap[:, ch, sl], tmul.ap, wcol.ap, 0.125, ALU.mult, ALU.mult),
                        reads=[tmul, wcol], writes=[qkv])
                else:
                    P.emit("v", lambda E, ch=ch, sl=sl: E.tensor_scalar(
                        qkv.ap[:, ch, sl], tmul.ap, wcol.ap, None, ALU.mult, ALU.bypass),
                        reads=[tmul, wcol], writes=[qkv])
    P.free(sqb, rows2h, rbc, tmul)
    if dump == "qkvn":
        return dump_fm(qkv, 24)

    vtm = P.alloc(16 * 16 * 64, BF16, "vtm")
    vtm.ap = vtm.ap.rearrange("p (h k e) -> p h k e", h=16, k=16)
    for h in range(16):
        base = (h % 2) * 64
        vch = 16 + h // 2
        bank = P.banks[h % 2]
        bankb = bank.ap.bitcast(BF16)
        for kt in range(16):
            vsl = qkv.ap[base:base + 64, vch, kt * 128:(kt + 1) * 128]
            idn = ident[base:base + 64, base:base + 64]
            P.emit("t", lambda E, vsl=vsl, idn=idn, kt=kt, bankb=bankb: E.transpose(
                bankb[:, kt * 64:(kt + 1) * 64], vsl, idn), reads=[qkv, cbf], writes=[bank])
        P.emit("v", lambda E, h=h, bankb=bankb: E.tensor_copy(
            vtm.ap[:, h, :, :], bankb.rearrange("p (k e) -> p k e", k=16)),
            reads=[bank], writes=[vtm])

    negmb = P.alloc(8, F32, "negm_cols")
    nrow = P.alloc(2 * 2048, BF16, "negm_row")
    rrow = P.alloc(2 * T, F32, "recip_rows")
    dcol = P.alloc(8, F32, "den_cols")
    scrap = P.alloc(512, F32, "exp_scrap")
    ptb = P.alloc(T, BF16, "pt_sb")
    rbc2 = P.alloc(1024, F32, "o_rbc")

    for hp in range(8):
        for par in range(2):
            h = 2 * hp + par
            base = par * 64
            qch = h // 2
            kch = NCH + h // 2
            Q = qkv.ap[base:base + 64, qch, :]
            K = qkv.ap[base:base + 64, kch, :]
            nrow_h = nrow.ap[0:1, par * 2048:par * 2048 + 2048]
            rrow_h = rrow.ap[0:1, par * T:par * T + T]
            for qt in range(16):
                bank = P.banks[qt % 2]
                for kblk in range(4):
                    ksl = slice(kblk * 512, (kblk + 1) * 512)
                    P.emit("t", lambda E, qt=qt, ksl=ksl, bank=bank, Q=Q, K=K: E.matmul(
                        bank.ap, Q[:, qt * 128:(qt + 1) * 128], K[:, ksl],
                        start=True, stop=True), reads=[qkv], writes=[bank])
                    P.emit("v", lambda E, kblk=kblk, bank=bank: E.tensor_reduce(
                        negmb.ap[:, kblk:kblk + 1], bank.ap, axis=AX.X, op=ALU.max,
                        negate=True), reads=[bank], writes=[negmb])
                P.emit("v", lambda E: E.tensor_reduce(negmb.ap[:, 4:5],
                       negmb.ap[:, 0:4], axis=AX.X, op=ALU.min), reads=[negmb], writes=[negmb])
                P.emit("v", lambda E: E.tensor_copy(negmb.ap[:, 5:6].bitcast(BF16)[:, 0:1],
                       negmb.ap[:, 4:5]), reads=[negmb], writes=[negmb])
                P.emit("v", lambda E: E.tensor_copy(negmb.ap[:, 6:7],
                       negmb.ap[:, 5:6].bitcast(BF16)[:, 0:1]), reads=[negmb], writes=[negmb])
                for kblk in range(4):
                    ksl = slice(kblk * 512, (kblk + 1) * 512)
                    bank = P.banks[2 + kblk % 2]
                    P.emit("t", lambda E, qt=qt, ksl=ksl, bank=bank, Q=Q, K=K: E.matmul(
                        bank.ap, Q[:, qt * 128:(qt + 1) * 128], K[:, ksl],
                        start=True, stop=True), reads=[qkv], writes=[bank])
                    act(P, scrap, scrap.ap, [bank, negmb], bank.ap, AF.Exp,
                        bias=negmb.ap[:, 6:7], accum_ap=dcol.ap[:, kblk:kblk + 1],
                        accum_buf=dcol)
                P.emit("v", lambda E: E.tensor_reduce(dcol.ap[:, 4:5], dcol.ap[:, 0:4],
                       axis=AX.X, op=ALU.add), reads=[dcol], writes=[dcol])
                P.emit("v", lambda E: E.reciprocal(dcol.ap[:, 5:6], dcol.ap[:, 4:5]),
                       reads=[dcol], writes=[dcol])
                P.emit("t", lambda E: E.transpose(P.banks[6].ap[0:1, 0:128],
                       negmb.ap[:, 6:7], cid.ap), reads=[negmb, cid], writes=[P.banks[6]])
                P.emit("v", lambda E, qt=qt, nrow_h=nrow_h: E.tensor_copy(
                    nrow_h[0:1, qt * 128:(qt + 1) * 128],
                    P.banks[6].ap[0:1, 0:128]), reads=[P.banks[6]], writes=[nrow])
                P.emit("t", lambda E: E.transpose(P.banks[7].ap[0:1, 0:128],
                       dcol.ap[:, 5:6], cid.ap), reads=[dcol, cid], writes=[P.banks[7]])
                P.emit("v", lambda E, qt=qt, rrow_h=rrow_h: E.tensor_copy(
                    rrow_h[0:1, qt * 128:(qt + 1) * 128], P.banks[7].ap[0:1, 0:128]),
                    reads=[P.banks[7]], writes=[rrow])
            for kt in range(16):
                for qblk in range(4):
                    qsl = slice(qblk * 512, (qblk + 1) * 512)
                    bank = P.banks[qblk % 2]
                    P.emit("t", lambda E, kt=kt, qsl=qsl, bank=bank, Q=Q, K=K: E.matmul(
                        bank.ap, K[:, kt * 128:(kt + 1) * 128], Q[:, qsl],
                        start=True, stop=False), reads=[qkv], writes=[bank])
                    P.emit("t", lambda E, qsl=qsl, bank=bank, nrow_h=nrow_h: E.matmul(
                        bank.ap, obf.ap[0:1, :], nrow_h[0:1, qsl], start=False, stop=True),
                        reads=[obf, nrow], writes=[bank])
                    act(P, ptb, ptb.ap[:, qsl], [bank], bank.ap, AF.Exp)
                for qblk in range(4):
                    qsl = slice(qblk * 512, (qblk + 1) * 512)
                    obank = P.banks[4 + qblk]
                    P.emit("t", lambda E, kt=kt, qsl=qsl, obank=obank, h=h, par=par:
                           E.matmul(obank.ap[par * 64:par * 64 + 64, :],
                                    vtm.ap[:, h, kt, :], ptb.ap[:, qsl],
                                    start=(kt == 0), stop=(kt == 15),
                                    tile_position=(0, par * 64)),
                           reads=[vtm, ptb], writes=[obank])
        for qblk in range(4):
            qsl = slice(qblk * 512, (qblk + 1) * 512)
            obank = P.banks[4 + qblk]
            for par in range(2):
                rrow_h = rrow.ap[0:1, par * T + qblk * 512:par * T + (qblk + 1) * 512]
                bank = P.banks[par]
                P.emit("t", lambda E, rrow_h=rrow_h, bank=bank: E.matmul(
                    bank.ap, c_ones_row.ap, rrow_h, start=True, stop=True),
                    reads=[rrow, c_ones_row], writes=[bank])
                P.emit("v", lambda E, bank=bank: E.tensor_copy(rbc2.ap[:, 0:512], bank.ap),
                       reads=[bank], writes=[rbc2])
                P.emit("v", lambda E, hp=hp, par=par, qsl=qsl, obank=obank: E.tensor_tensor(
                    qkv.ap[par * 64:par * 64 + 64, 16 + hp, qsl],
                    obank.ap[par * 64:par * 64 + 64, :], rbc2.ap[par * 64:par * 64 + 64, 0:512],
                    ALU.mult), reads=[obank, rbc2], writes=[qkv])
    P.free(vtm, negmb, nrow, rrow, dcol, scrap, ptb, rbc2)
    if dump == "ofm":
        return dump_fm(qkv, 24)

    wslot = P.alloc(NCH * D, BF16, "wslot")
    wb = load_weight(P, wproj_d, NCH, D, "wproj", slot=wslot)
    sst = P.alloc(T, F32, "s_stage")

    def evac_s(dc, blk, bank):
        sl = slice(blk * 512, (blk + 1) * 512)
        P.emit("v", lambda E, dc=dc, sl=sl, bank=bank: E.tensor_scalar(
            sst.ap[:, sl], bank.ap, vec["b_proj"].ap[:, dc:dc + 1], None,
            ALU.add, ALU.bypass), reads=[bank, vec["b_proj"]], writes=[sst])
        if blk == 3:
            P.dma("y", s_d[dc * 128:(dc + 1) * 128, :], sst.ap, reads=[sst])
    proj(P, wb, lambda c, blk: qkv.ap[:, 16 + c, blk * 512:(blk + 1) * 512], [qkv],
         NCH, NCH, evac_s)
    P.free(qkv, wslot, sst)
    if dump == "sdone":
        tmpb = P.alloc(T, F32, "tmpb")
        for c in range(NCH):
            P.dma("y", tmpb.ap, s_d[c * 128:(c + 1) * 128, :], writes=[tmpb])
            P.dma("y", y_d[c * 128:(c + 1) * 128, :], tmpb.ap, reads=[tmpb])
        return P, [("y", "d", len(P.ops["y"]) - 1)]

    xn_d = P.dram(pfx + "xn_spill", [D, T], BF16, kind="Internal")
    for c in range(NCH):
        P.dma("y", xn_d[c * 128:(c + 1) * 128, :], xn.ap[:, c, 1:T + 1], reads=[xn])
    P.free(xn)

    HB = T // 2
    u1 = P.alloc(32 * HB, BF16, "u1")
    u1.ap = u1.ap.rearrange("p (c t) -> p c t", c=32)
    u2 = P.alloc(32 * HB, BF16, "u2")
    u2.ap = u2.ap.rearrange("p (c t) -> p c t", c=32)
    xnh = P.alloc(NCH * HB, BF16, "xnh")
    xnh.ap = xnh.ap.rearrange("p (c t) -> p c t", c=NCH)
    t1 = P.alloc(512, F32, "ffn_t1")
    t2 = P.alloc(512, F32, "ffn_t2")
    wslH = WSlots(P, 32 * 256, "wslH")
    wslA = wslH
    stg = [P.alloc(512, F32, "ffn_stg0"), P.alloc(512, F32, "ffn_stg1")]
    gst = P.alloc(512, BF16, "g_stage")
    xst = P.alloc(HB, F32, "xst")
    sst2 = P.alloc(HB, F32, "sst2")
    yst = P.alloc(HB, F32, "yst")
    NBH = HB // 512

    for half in range(2):
        t0 = half * HB
        for c in range(NCH):
            P.dma("y", xnh.ap[:, c, :], xn_d[c * 128:(c + 1) * 128, t0:t0 + HB],
                  writes=[xnh])
        for mi, (mat_d, scol) in enumerate(((w1_d, "w1s"), (w2_d, "w2s"))):
            for ds in range(8):
                wA = load_weight(P, mat_d, NCH, 512, "wA", slot=wslA(), stage=stg,
                                 col0=ds * 512)
                for dsub in range(4):
                    dc = ds * 4 + dsub
                    for blk in range(NBH):
                        usl = slice(blk * 512, (blk + 1) * 512)
                        bank = P.banks[(dc * NBH + blk) % 6]
                        for c in range(NCH):
                            P.emit("t", lambda E, c=c, dsub=dsub, usl=usl, bank=bank,
                                   wA=wA: E.matmul(
                                bank.ap, wA.ap[:, c, dsub * 128:(dsub + 1) * 128],
                                xnh.ap[:, c, usl], start=(c == 0), stop=(c == NCH - 1)),
                                reads=[wA, xnh], writes=[bank])
                        if mi == 0:
                            act(P, u1, u1.ap[:, dc, usl], [bank], bank.ap, AF.Silu,
                                scale=vec["w1s"].ap[:, dc:dc + 1])
                        else:
                            P.emit("v", lambda E, dc=dc, bank=bank: E.tensor_scalar(
                                t2.ap, bank.ap, vec["w2s"].ap[:, dc:dc + 1], None,
                                ALU.mult, ALU.bypass), reads=[bank, vec["w2s"]], writes=[t2])
                            P.emit("v", lambda E, dc=dc, usl=usl: E.tensor_tensor(
                                u1.ap[:, dc, usl], u1.ap[:, dc, usl], t2.ap, ALU.mult),
                                reads=[u1, t2], writes=[u1])
        for (mat_d, scol, src3, dst3) in ((h1_d, "h1s", u1, u2), (h2_d, "h2s", u2, u1)):
            for ds in range(16):
                wH = load_weight(P, mat_d, 32, 256, "wH", slot=wslH(), stage=stg,
                                 col0=ds * 256)
                for dsub in range(2):
                    dc = ds * 2 + dsub
                    for blk in range(NBH):
                        usl = slice(blk * 512, (blk + 1) * 512)
                        bank = P.banks[(dc * NBH + blk) % 6]
                        for c in range(32):
                            P.emit("t", lambda E, c=c, dsub=dsub, usl=usl, bank=bank,
                                   wH=wH, src3=src3: E.matmul(
                                bank.ap, wH.ap[:, c, dsub * 128:(dsub + 1) * 128],
                                src3.ap[:, c, usl], start=(c == 0), stop=(c == 31)),
                                reads=[wH, src3], writes=[bank])
                        act(P, dst3, dst3.ap[:, dc, usl], [bank], bank.ap, AF.Gelu,
                            scale=vec[scol].ap[:, dc:dc + 1])
        for ds in range(4):
            wC = load_weight(P, w3_d, 32, 256, "w3", slot=wslH(), stage=stg, col0=ds * 256)
            for dsub in range(2):
                dc = ds * 2 + dsub
                P.dma("y", xst.ap, x_d[dc * 128:(dc + 1) * 128, t0:t0 + HB], writes=[xst])
                P.dma("y", sst2.ap, s_d[dc * 128:(dc + 1) * 128, t0:t0 + HB], writes=[sst2])
                for blk in range(NBH):
                    usl = slice(blk * 512, (blk + 1) * 512)
                    bank = P.banks[(dc * NBH + blk) % 6]
                    for c in range(32):
                        P.emit("t", lambda E, c=c, dsub=dsub, usl=usl, bank=bank, wC=wC:
                               E.matmul(bank.ap, wC.ap[:, c, dsub * 128:(dsub + 1) * 128],
                                        u1.ap[:, c, usl], start=(c == 0), stop=(c == 31)),
                               reads=[wC, u1], writes=[bank])
                    act(P, gst, gst.ap[:, :], [bank], bank.ap, AF.Silu,
                        scale=vec["w3s"].ap[:, dc:dc + 1])
                    P.emit("v", lambda E, usl=usl: E.tensor_tensor(
                        t1.ap, gst.ap[:, :], sst2.ap[:, usl], ALU.mult),
                        reads=[gst, sst2], writes=[t1])
                    P.emit("v", lambda E, usl=usl: E.tensor_tensor(
                        yst.ap[:, usl], t1.ap, xst.ap[:, usl], ALU.add),
                        reads=[t1, xst], writes=[yst])
                P.dma("y", y_d[dc * 128:(dc + 1) * 128, t0:t0 + HB], yst.ap, reads=[yst])
    last_y = ("y", "d", len(P.ops["y"]) - 1)
    P.pop_scope()
    return P, [last_y]


def l3_inputs(core_x, ins, pfx=""):
    rd = {
        "x_in": _f32(core_x),
        "wqkvT": _f32(ins["L3_attn_w_qkv"].T),
        "wprojT": _f32(ins["L3_attn_w_proj"].T),
        "w1T": _f32(ins["L3_ffn_w1_w"].T),
        "w2T": _f32(ins["L3_ffn_w2_w"].T),
        "h1T": _f32(ins["L3_ffn_h1_w"].T),
        "h2T": _f32(ins["L3_ffn_h2_w"].T),
        "w3T": _f32(ins["L3_ffn_w3_w"].T),
        "vecs": l3_pack(ins).host,
        "cbf": _bf_consts(),
        "cid32": np.eye(128, dtype=np.float32),
    }
    return {("x_in" if k == "x_in" else pfx + k): v for k, v in rd.items()}




def l2_pack(ins=None):
    vp = VecPack(512)
    if ins is not None:
        vp.add_cols("gate_w", _f32(np.asarray(ins["L2_ffn_gate_w"]).reshape(NCH, 128, 8)
                                   .transpose(1, 0, 2).reshape(128, NCH * 8)))
        vp.add_cols("b_proj", _col8(ins["L2_attn_b_proj"]))
        vp.add_cols("snw", np.tile(np.asarray(ins["L2_attn_sub_norm_w"], np.float32), 2)[:, None])
        vp.add_row("lq1", ins["L2_attn_lambda_q1"])
        vp.add_row("lk1", ins["L2_attn_lambda_k1"])
        vp.add_row("lq2", ins["L2_attn_lambda_q2"])
        vp.add_row("lk2", ins["L2_attn_lambda_k2"])
    else:
        vp.add_cols("gate_w", np.zeros((128, NCH * 8), np.float32))
        vp.add_cols("b_proj", np.zeros((128, 8), np.float32))
        vp.add_cols("snw", np.zeros((128, 1), np.float32))
        for n in ("lq1", "lk1", "lq2", "lk2"):
            vp.add_row(n, np.zeros(32, np.float32))
    vp.add_cols("ones_col", np.ones((128, 1), np.float32))
    vp.add_row("ones_row", np.ones(128, np.float32))
    b2 = np.zeros((2, 128), np.float32)
    b2[0, 0:64] = 1
    b2[1, 64:128] = 1
    vp.add_block("B2", b2)
    return vp


def build_l2(P=None, x_d=None, y_d=None, pfx="", dump=None):
    if P is None:
        P = Prog("l2").make_arena()
        x_d = P.dram("x_in", [D, T])
        y_d = P.dram("y_out", [D, T], kind="ExternalOutput")
    P.push_scope()
    g_d = P.dram(pfx + "g_spill", [D, T], BF16, kind="Internal")
    wqkv_d = P.dram(pfx + "wqkvT4", [D, 4 * D])
    wproj_d = P.dram(pfx + "wprojT", [D, D])
    ew1_d = P.dram(pfx + "ew1", [8, D, 2048])
    ew2_d = P.dram(pfx + "ew2", [8, D, 2048])
    ew3_d = P.dram(pfx + "ew3", [8, 2048, D])
    sw1_d = P.dram(pfx + "sw1", [D, 2048])
    sw2_d = P.dram(pfx + "sw2", [D, 2048])
    sw3_d = P.dram(pfx + "sw3", [2048, D])
    vecs_d = P.dram(pfx + "vecs", [128, 512])
    cbf_d = P.dram(pfx + "cbf", [128, 192], BF16)
    cid_d = P.dram(pfx + "cid32", [128, 128])

    vec = l2_pack().load(P, vecs_d)
    c_ones_col = vec["ones_col"]
    c_ones_row = vec["ones_row"]
    B2 = vec["B2"]
    cbf = P.alloc(192, BF16, "cbf")
    P.dma("y", cbf.ap, cbf_d, writes=[cbf])
    cid = P.alloc(128, F32, "cid32")
    P.dma("y", cid.ap, cid_d, writes=[cid])
    ident = cbf.ap[:, 0:128]
    E2 = cbf.ap[:, 128:130]
    obf = P.alloc(128, BF16, "ones_row_bf")
    P.emit("v", lambda E: E.memset(obf.ap[0:1, :], 1.0), writes=[obf])

    def dump_fm(sb3, nch=NCH, guard=0):
        outb = P.alloc(T, F32, "dumpout")
        for c in range(nch):
            sl = sb3.ap[:, c, guard:T + guard]
            P.emit("v", lambda E, sl=sl: E.tensor_copy(outb.ap, sl), reads=[sb3], writes=[outb])
            P.dma("y", y_d[(c % NCH) * 128:((c % NCH) + 1) * 128, :], outb.ap, reads=[outb])
        return P, [("y", "d", len(P.ops["y"]) - 1)]

    xn = alloc_xn(P)
    x = P.alloc(NCH * T, F32, "x")
    x.ap = x.ap.rearrange("p (c t) -> p c t", c=NCH)
    P.dma("y", x.ap, x_d.rearrange("(c p) t -> p c t", p=128), writes=[x])
    global_ln(P, x, xn, c_ones_col, c_ones_row)
    P.free(x)

    gwb = P.alloc(NCH * 8, BF16, "gate_w_bf")
    gwb.ap = gwb.ap.rearrange("p (c e) -> p c e", c=NCH)
    P.emit("v", lambda E: E.tensor_copy(
        gwb.ap, vec["gate_w"].ap.rearrange("p (c e) -> p c e", c=NCH)),
        reads=[vec["gate_w"]], writes=[gwb])
    logits = P.alloc(16 * 8, F32, "logits")
    logits.ap = logits.ap.rearrange("p (n e) -> p n e", n=16)
    for tt in range(16):
        bank = P.banks[tt % 2]
        for c in range(NCH):
            P.emit("t", lambda E, tt=tt, c=c, bank=bank: E.matmul(
                bank.ap[:, 0:8], xn.ap[:, c, 1 + tt * 128:1 + (tt + 1) * 128], gwb.ap[:, c, :],
                start=(c == 0), stop=(c == NCH - 1)), reads=[xn, gwb], writes=[bank])
        P.emit("v", lambda E, tt=tt, bank=bank: E.tensor_copy(logits.ap[:, tt, :],
               bank.ap[:, 0:8]), reads=[bank], writes=[logits])
    r_m1 = P.alloc(16, F32, "r_m1")
    r_m2 = P.alloc(16, F32, "r_m2")
    r_g1 = P.alloc(16, F32, "r_g1")
    r_g2 = P.alloc(16, F32, "r_g2")
    msk1 = P.alloc(16 * 8, F32, "msk1")
    msk1.ap = msk1.ap.rearrange("p (n e) -> p n e", n=16)
    msk2 = P.alloc(16 * 8, F32, "msk2")
    msk2.ap = msk2.ap.rearrange("p (n e) -> p n e", n=16)
    lg2 = P.alloc(16 * 8, F32, "lg2")
    lg2.ap = lg2.ap.rearrange("p (n e) -> p n e", n=16)
    gates = P.alloc(16 * 8, F32, "gates")
    gates.ap = gates.ap.rearrange("p (n e) -> p n e", n=16)
    P.emit("v", lambda E: E.tensor_reduce(r_m1.ap, logits.ap, axis=AX.X, op=ALU.max),
           reads=[logits], writes=[r_m1])
    P.emit("v", lambda E: E.tensor_tensor(msk1.ap, logits.ap,
           r_m1.ap.unsqueeze(2).broadcast_to([128, 16, 8]), ALU.is_equal),
           reads=[logits, r_m1], writes=[msk1])
    P.emit("v", lambda E: E.scalar_tensor_tensor(lg2.ap, msk1.ap, -1e30, logits.ap,
           ALU.mult, ALU.add), reads=[msk1, logits], writes=[lg2])
    P.emit("v", lambda E: E.tensor_reduce(r_m2.ap, lg2.ap, axis=AX.X, op=ALU.max),
           reads=[lg2], writes=[r_m2])
    P.emit("v", lambda E: E.tensor_tensor(msk2.ap, lg2.ap,
           r_m2.ap.unsqueeze(2).broadcast_to([128, 16, 8]), ALU.is_equal),
           reads=[lg2, r_m2], writes=[msk2])
    P.emit("v", lambda E: E.tensor_tensor(r_g1.ap, r_m1.ap, r_m2.ap, ALU.subtract),
           reads=[r_m1, r_m2], writes=[r_g1])
    act(P, r_g1, r_g1.ap, [r_g1], r_g1.ap, AF.Sigmoid)
    P.emit("v", lambda E: E.tensor_scalar(r_g2.ap, r_g1.ap, -1.0, 1.0, ALU.mult, ALU.add),
           reads=[r_g1], writes=[r_g2])
    P.emit("v", lambda E: E.tensor_tensor(gates.ap, msk1.ap,
           r_g1.ap.unsqueeze(2).broadcast_to([128, 16, 8]), ALU.mult),
           reads=[msk1, r_g1], writes=[gates])
    P.emit("v", lambda E: E.tensor_tensor(msk2.ap, msk2.ap,
           r_g2.ap.unsqueeze(2).broadcast_to([128, 16, 8]), ALU.mult),
           reads=[msk2, r_g2], writes=[msk2])
    P.emit("v", lambda E: E.tensor_tensor(gates.ap, gates.ap, msk2.ap, ALU.add),
           reads=[gates, msk2], writes=[gates])
    grow = P.alloc(T, F32, "gate_row")
    if dump == "gates":
        outb = P.alloc(T, F32, "dumpout")
        for tt in range(16):
            bank = P.banks[2 + tt % 2]
            P.emit("t", lambda E, tt=tt, bank=bank: E.transpose(
                bank.ap[0:1, 0:128], gates.ap[:, tt, 0:1], cid.ap),
                reads=[gates, cid], writes=[bank])
            P.emit("v", lambda E, tt=tt, bank=bank: E.tensor_copy(
                grow.ap[0:1, tt * 128:(tt + 1) * 128], bank.ap[0:1, 0:128]),
                reads=[bank], writes=[grow])
        P.emit("v", lambda E: E.memset(outb.ap, 0.0), writes=[outb])
        P.emit("v", lambda E: E.tensor_copy(outb.ap[0:1, :], grow.ap[0:1, :]),
               reads=[grow], writes=[outb])
        P.dma("y", y_d[0:128, :], outb.ap, reads=[outb])
        return P, [("y", "d", len(P.ops["y"]) - 1)]
    P.free(gwb, logits, r_m1, r_m2, r_g1, r_g2, msk1, msk2, lg2)

    yacc = P.alloc(NCH * T, F32, "yacc")
    yacc.ap = yacc.ap.rearrange("p (c t) -> p c t", c=NCH)
    abuf = P.alloc(16 * T, BF16, "abuf")
    abuf.ap = abuf.ap.rearrange("p (c t) -> p c t", c=16)
    wsl_a = WSlots(P, NCH * 256, "wsl_a")
    wsl_b = WSlots(P, NCH * 256, "wsl_b")
    wsl_3 = wsl_a
    stg = [P.alloc(512, F32, "l2_stg0"), P.alloc(512, F32, "l2_stg1")]
    t1 = P.alloc(512, F32, "l2_t1")
    t1b = P.alloc(512, F32, "l2_t1b")
    gbc = P.alloc(T, F32, "l2_gbc")

    for e in range(9):
        shared = (e == 8)
        w1d = sw1_d if shared else ew1_d[e]
        w2d = sw2_d if shared else ew2_d[e]
        w3d = sw3_d if shared else ew3_d[e]
        for ds in range(8):
            wA = load_weight(P, w1d, NCH, 256, "ew1", slot=wsl_a(), stage=stg, col0=ds * 256)
            wB = load_weight(P, w2d, NCH, 256, "ew2", slot=wsl_b(), stage=stg, col0=ds * 256)
            for dsub in range(2):
                hc = ds * 2 + dsub
                for blk in range(4):
                    sl = slice(blk * 512, (blk + 1) * 512)
                    bankA = P.banks[(hc * 4 + blk) % 3]
                    bankB = P.banks[3 + (hc * 4 + blk) % 3]
                    for c in range(NCH):
                        P.emit("t", lambda E, c=c, dsub=dsub, sl=sl, bankA=bankA, wA=wA:
                               E.matmul(bankA.ap, wA.ap[:, c, dsub * 128:(dsub + 1) * 128],
                                        xn.ap[:, c, 1 + sl.start:1 + sl.stop],
                                        start=(c == 0), stop=(c == NCH - 1)),
                               reads=[wA, xn], writes=[bankA])
                    for c in range(NCH):
                        P.emit("t", lambda E, c=c, dsub=dsub, sl=sl, bankB=bankB, wB=wB:
                               E.matmul(bankB.ap, wB.ap[:, c, dsub * 128:(dsub + 1) * 128],
                                        xn.ap[:, c, 1 + sl.start:1 + sl.stop],
                                        start=(c == 0), stop=(c == NCH - 1)),
                               reads=[wB, xn], writes=[bankB])
                    tx = t1 if (hc * 4 + blk) % 2 == 0 else t1b
                    act(P, tx, tx.ap, [bankA], bankA.ap, AF.Silu)
                    P.emit("v", lambda E, hc=hc, sl=sl, bankB=bankB, tx=tx: E.tensor_tensor(
                        abuf.ap[:, hc, sl], tx.ap, bankB.ap, ALU.mult),
                        reads=[tx, bankB], writes=[abuf])
        if not shared:
            for tt in range(16):
                bank = P.banks[4 + tt % 2]
                P.emit("t", lambda E, tt=tt, e=e, bank=bank: E.transpose(
                    bank.ap[0:1, 0:128], gates.ap[:, tt, e:e + 1], cid.ap),
                    reads=[gates, cid], writes=[bank])
                P.emit("v", lambda E, tt=tt, bank=bank: E.tensor_copy(
                    grow.ap[0:1, tt * 128:(tt + 1) * 128], bank.ap[0:1, 0:128]),
                    reads=[bank], writes=[grow])
            for blk in range(4):
                sl = slice(blk * 512, (blk + 1) * 512)
                bank = P.banks[4 + blk % 2]
                P.emit("t", lambda E, sl=sl, bank=bank: E.matmul(
                    bank.ap, c_ones_row.ap, grow.ap[0:1, sl],
                    start=True, stop=True), reads=[grow, c_ones_row], writes=[bank])
                P.emit("v", lambda E, sl=sl, bank=bank: E.tensor_copy(gbc.ap[:, sl], bank.ap),
                       reads=[bank], writes=[gbc])
        for ds3 in range(8):
            wC = load_weight(P, w3d, 16, 128, "ew3", slot=wsl_3(), stage=stg, col0=ds3 * 128)
            for dsub in range(1):
                dc = ds3
                for blk in range(4):
                    sl = slice(blk * 512, (blk + 1) * 512)
                    bank = P.banks[6 + (dc * 4 + blk) % 2]
                    for c in range(16):
                        P.emit("t", lambda E, c=c, dsub=dsub, sl=sl, bank=bank, wC=wC:
                               E.matmul(bank.ap, wC.ap[:, c, dsub * 128:(dsub + 1) * 128],
                                        abuf.ap[:, c, sl], start=(c == 0), stop=(c == 15)),
                               reads=[wC, abuf], writes=[bank])
                    if shared:
                        P.emit("v", lambda E, dc=dc, sl=sl, bank=bank: E.tensor_tensor(
                            yacc.ap[:, dc, sl], yacc.ap[:, dc, sl], bank.ap, ALU.add),
                            reads=[yacc, bank], writes=[yacc])
                    elif e == 0:
                        P.emit("v", lambda E, dc=dc, sl=sl, bank=bank: E.tensor_tensor(
                            yacc.ap[:, dc, sl], bank.ap, gbc.ap[:, sl], ALU.mult),
                            reads=[bank, gbc], writes=[yacc])
                    else:
                        P.emit("v", lambda E, sl=sl, bank=bank: E.tensor_tensor(
                            t1.ap, bank.ap, gbc.ap[:, sl], ALU.mult),
                            reads=[bank, gbc], writes=[t1])
                        P.emit("v", lambda E, dc=dc, sl=sl: E.tensor_tensor(
                            yacc.ap[:, dc, sl], yacc.ap[:, dc, sl], t1.ap, ALU.add),
                            reads=[yacc, t1], writes=[yacc])
    P.free(grow, t1, t1b, gbc)
    gstage = P.alloc(T, BF16, "gstage")
    for c in range(NCH):
        act(P, gstage, gstage.ap, [yacc], yacc.ap[:, c, :], AF.Silu)
        P.dma("y", g_d[c * 128:(c + 1) * 128, :], gstage.ap, reads=[gstage])
    P.free(yacc, abuf, *wsl_a.bufs, *wsl_b.bufs, stg[0], stg[1],
           gstage, gates)
    if dump == "gdone":
        tmpb = P.alloc(T, F32, "tmpb")
        gtmp = P.alloc(T, BF16, "gtmp")
        for c in range(NCH):
            P.dma("y", gtmp.ap, g_d[c * 128:(c + 1) * 128, :], writes=[gtmp])
            P.emit("v", lambda E: E.tensor_copy(tmpb.ap, gtmp.ap), reads=[gtmp], writes=[tmpb])
            P.dma("y", y_d[c * 128:(c + 1) * 128, :], tmpb.ap, reads=[tmpb])
        return P, [("y", "d", len(P.ops["y"]) - 1)]

    lamc = P.alloc(64, F32, "lam")
    P.emit("v", lambda E: E.tensor_tensor(lamc.ap[0:1, 0:32], vec["lq1"].ap,
           vec["lk1"].ap, ALU.mult), reads=[vec["lq1"], vec["lk1"]], writes=[lamc])
    P.emit("v", lambda E: E.tensor_tensor(lamc.ap[0:1, 32:64], vec["lq2"].ap,
           vec["lk2"].ap, ALU.mult), reads=[vec["lq2"], vec["lk2"]], writes=[lamc])
    P.emit("v", lambda E: E.tensor_reduce(lamc.ap[0:1, 0:1], lamc.ap[0:1, 0:32],
           axis=AX.X, op=ALU.add), reads=[lamc], writes=[lamc])
    P.emit("v", lambda E: E.tensor_reduce(lamc.ap[0:1, 1:2], lamc.ap[0:1, 32:64],
           axis=AX.X, op=ALU.add), reads=[lamc], writes=[lamc])
    act(P, lamc, lamc.ap[0:1, 2:4], [lamc], lamc.ap[0:1, 0:2], AF.Exp)
    P.emit("v", lambda E: E.tensor_tensor(lamc.ap[0:1, 4:5], lamc.ap[0:1, 3:4],
           lamc.ap[0:1, 2:3], ALU.subtract), reads=[lamc], writes=[lamc])
    P.emit("v", lambda E: E.tensor_scalar(lamc.ap[0:1, 5:6], lamc.ap[0:1, 4:5], -0.2,
           None, ALU.add, ALU.bypass), reads=[lamc], writes=[lamc])
    neglam_col = P.alloc(4, F32, "neglam_col")
    P.emit("t", lambda E: E.matmul(P.banks[0].ap[:, 0:1], c_ones_row.ap,
           lamc.ap[0:1, 5:6], start=True, stop=True), reads=[lamc, c_ones_row],
           writes=[P.banks[0]])
    P.emit("v", lambda E: E.tensor_copy(neglam_col.ap[:, 0:1], P.banks[0].ap[:, 0:1]),
           reads=[P.banks[0]], writes=[neglam_col])

    qkv = P.alloc(32 * T, BF16, "qkv")
    qkv.ap = qkv.ap.rearrange("p (c t) -> p c t", c=32)
    wslot = P.alloc(NCH * D, BF16, "wslot")
    for j in range(4):
        wb = load_weight(P, wqkv_d, NCH, D, "wqkv", slot=wslot, col0=j * D)

        def evac_qkv(dc, blk, bank, j=j):
            gdc = j * NCH + dc
            sl = slice(blk * 512, (blk + 1) * 512)
            if j == 0:
                P.emit("v", lambda E, gdc=gdc, sl=sl, bank=bank: E.tensor_scalar(
                    qkv.ap[:, gdc, sl], bank.ap, 32.0 ** -0.5, None, ALU.mult,
                    ALU.bypass), reads=[bank], writes=[qkv])
            else:
                P.emit("v", lambda E, gdc=gdc, sl=sl, bank=bank: E.tensor_copy(
                    qkv.ap[:, gdc, sl], bank.ap), reads=[bank], writes=[qkv])
        proj(P, wb, lambda c, blk: xn.ap[:, c, 1 + blk * 512:1 + (blk + 1) * 512], [xn],
             NCH, NCH, evac_qkv)
    P.free(wslot, xn)
    if dump == "qkv":
        return dump_fm(qkv, 32)

    vtm = P.alloc(16 * 16 * 64, BF16, "vtm")
    vtm.ap = vtm.ap.rearrange("p (h k e) -> p h k e", h=16, k=16)
    for h in range(16):
        base = (h % 2) * 64
        vch = 24 + h // 2
        bank = P.banks[h % 2]
        bankb = bank.ap.bitcast(BF16)
        for kt in range(16):
            vsl = qkv.ap[base:base + 64, vch, kt * 128:(kt + 1) * 128]
            idn = ident[base:base + 64, base:base + 64]
            P.emit("t", lambda E, vsl=vsl, idn=idn, kt=kt, bankb=bankb: E.transpose(
                bankb[:, kt * 64:(kt + 1) * 64], vsl, idn), reads=[qkv, cbf], writes=[bank])
        P.emit("v", lambda E, h=h, bankb=bankb: E.tensor_copy(
            vtm.ap[:, h, :, :], bankb.rearrange("p (k e) -> p k e", k=16)),
            reads=[bank], writes=[vtm])

    negmb = P.alloc(8, F32, "negm_cols")
    nrow = P.alloc(2 * 2048, BF16, "negm_rows")
    rrow = P.alloc(2 * T, F32, "recip_rows")
    dcol = P.alloc(8, F32, "den_cols")
    scrap = P.alloc(512, F32, "exp_scrap")
    pt1 = P.alloc(512, BF16, "pt1")
    pt2 = P.alloc(512, BF16, "pt2")
    rbc2 = P.alloc(1024, F32, "o_rbc")
    ot1 = P.alloc(512, F32, "o_t1")
    ot2 = P.alloc(512, F32, "o_t2")

    for hp in range(8):
        for par in range(2):
            h = 2 * hp + par
            base = par * 64
            qch = h // 2
            Q = qkv.ap[base:base + 64, qch, :]
            for comp in range(2):
                kch = (8 if comp == 0 else 16) + h // 2
                K = qkv.ap[base:base + 64, kch, :]
                nrow_h = nrow.ap[0:1, comp * 2048:comp * 2048 + 2048]
                rrow_h = rrow.ap[0:1, comp * T:comp * T + T]
                for qt in range(16):
                    bank = P.banks[qt % 2]
                    for kblk in range(4):
                        ksl = slice(kblk * 512, (kblk + 1) * 512)
                        P.emit("t", lambda E, qt=qt, ksl=ksl, bank=bank, Q=Q, K=K: E.matmul(
                            bank.ap, Q[:, qt * 128:(qt + 1) * 128], K[:, ksl],
                            start=True, stop=True), reads=[qkv], writes=[bank])
                        P.emit("v", lambda E, kblk=kblk, bank=bank: E.tensor_reduce(
                            negmb.ap[:, kblk:kblk + 1], bank.ap, axis=AX.X, op=ALU.max,
                            negate=True), reads=[bank], writes=[negmb])
                    P.emit("v", lambda E: E.tensor_reduce(negmb.ap[:, 4:5],
                           negmb.ap[:, 0:4], axis=AX.X, op=ALU.min), reads=[negmb],
                           writes=[negmb])
                    P.emit("v", lambda E: E.tensor_copy(
                        negmb.ap[:, 5:6].bitcast(BF16)[:, 0:1], negmb.ap[:, 4:5]),
                        reads=[negmb], writes=[negmb])
                    P.emit("v", lambda E: E.tensor_copy(negmb.ap[:, 6:7],
                           negmb.ap[:, 5:6].bitcast(BF16)[:, 0:1]), reads=[negmb],
                           writes=[negmb])
                    for kblk in range(4):
                        ksl = slice(kblk * 512, (kblk + 1) * 512)
                        bank = P.banks[2 + kblk % 2]
                        P.emit("t", lambda E, qt=qt, ksl=ksl, bank=bank, Q=Q, K=K: E.matmul(
                            bank.ap, Q[:, qt * 128:(qt + 1) * 128], K[:, ksl],
                            start=True, stop=True), reads=[qkv], writes=[bank])
                        act(P, scrap, scrap.ap, [bank, negmb], bank.ap, AF.Exp,
                            bias=negmb.ap[:, 6:7], accum_ap=dcol.ap[:, kblk:kblk + 1],
                            accum_buf=dcol)
                    P.emit("v", lambda E: E.tensor_reduce(dcol.ap[:, 4:5], dcol.ap[:, 0:4],
                           axis=AX.X, op=ALU.add), reads=[dcol], writes=[dcol])
                    P.emit("v", lambda E: E.reciprocal(dcol.ap[:, 5:6], dcol.ap[:, 4:5]),
                           reads=[dcol], writes=[dcol])
                    P.emit("t", lambda E: E.transpose(P.banks[6].ap[0:1, 0:128],
                           negmb.ap[:, 6:7], cid.ap), reads=[negmb, cid], writes=[P.banks[6]])
                    P.emit("v", lambda E, qt=qt, nrow_h=nrow_h: E.tensor_copy(
                        nrow_h[0:1, qt * 128:(qt + 1) * 128], P.banks[6].ap[0:1, 0:128]),
                        reads=[P.banks[6]], writes=[nrow])
                    P.emit("t", lambda E: E.transpose(P.banks[7].ap[0:1, 0:128],
                           dcol.ap[:, 5:6], cid.ap), reads=[dcol, cid], writes=[P.banks[7]])
                    P.emit("v", lambda E, qt=qt, rrow_h=rrow_h: E.tensor_copy(
                        rrow_h[0:1, qt * 128:(qt + 1) * 128], P.banks[7].ap[0:1, 0:128]),
                        reads=[P.banks[7]], writes=[rrow])
            for qblk in range(4):
                qsl = slice(qblk * 512, (qblk + 1) * 512)
                o1bank = P.banks[4]
                o2bank = P.banks[5]
                for kt in range(16):
                    for comp in range(2):
                        kch = (8 if comp == 0 else 16) + h // 2
                        K = qkv.ap[base:base + 64, kch, :]
                        nrow_h = nrow.ap[0:1, comp * 2048:comp * 2048 + 2048]
                        bank = P.banks[comp]
                        ptx = pt1 if comp == 0 else pt2
                        P.emit("t", lambda E, kt=kt, qsl=qsl, bank=bank, Q=Q, K=K: E.matmul(
                            bank.ap, K[:, kt * 128:(kt + 1) * 128], Q[:, qsl],
                            start=True, stop=False), reads=[qkv], writes=[bank])
                        P.emit("t", lambda E, qsl=qsl, bank=bank, nrow_h=nrow_h: E.matmul(
                            bank.ap, obf.ap[0:1, :], nrow_h[0:1, qsl], start=False,
                            stop=True), reads=[obf, nrow], writes=[bank])
                        act(P, ptx, ptx.ap, [bank], bank.ap, AF.Exp)
                        obank = o1bank if comp == 0 else o2bank
                        P.emit("t", lambda E, kt=kt, obank=obank, h=h, par=par, ptx=ptx:
                               E.matmul(obank.ap[par * 64:par * 64 + 64, :],
                                        vtm.ap[:, h, kt, :], ptx.ap,
                                        start=(kt == 0), stop=(kt == 15),
                                        tile_position=(0, par * 64)),
                               reads=[vtm, ptx], writes=[obank])
                for (comp, obank, ot) in ((0, o1bank, ot1), (1, o2bank, ot2)):
                    rrow_h = rrow.ap[0:1, comp * T + qblk * 512:comp * T + (qblk + 1) * 512]
                    bank = P.banks[2 + comp]
                    P.emit("t", lambda E, rrow_h=rrow_h, bank=bank: E.matmul(
                        bank.ap, c_ones_row.ap, rrow_h, start=True, stop=True),
                        reads=[rrow, c_ones_row], writes=[bank])
                    P.emit("v", lambda E, bank=bank: E.tensor_copy(rbc2.ap[:, 0:512],
                           bank.ap), reads=[bank], writes=[rbc2])
                    P.emit("v", lambda E, obank=obank, ot=ot, par=par: E.tensor_tensor(
                        ot.ap[par * 64:par * 64 + 64, :],
                        obank.ap[par * 64:par * 64 + 64, :],
                        rbc2.ap[par * 64:par * 64 + 64, 0:512], ALU.mult),
                        reads=[obank, rbc2], writes=[ot])
                P.emit("v", lambda E, hp=hp, par=par, qsl=qsl: E.scalar_tensor_tensor(
                    qkv.ap[par * 64:par * 64 + 64, 24 + hp, qsl],
                    ot2.ap[par * 64:par * 64 + 64, :], neglam_col.ap[par * 64:par * 64 + 64, 0:1],
                    ot1.ap[par * 64:par * 64 + 64, :], ALU.mult, ALU.add),
                    reads=[ot1, ot2, neglam_col], writes=[qkv])
    P.free(vtm, negmb, nrow, rrow, dcol, scrap, pt1, pt2, rbc2, ot1, ot2)
    if dump == "ofm":
        return dump_fm(qkv, 32)

    sqb = P.alloc(T, BF16, "sn_sq")
    rows2h = P.alloc(T, F32, "sn_rows")
    tmul = P.alloc(512, F32, "sn_tmul")
    for c in range(NCH):
        ch = 24 + c
        for blk in range(4):
            sl = slice(blk * 512, (blk + 1) * 512)
            act(P, sqb, sqb.ap[:, sl], [qkv], qkv.ap[:, ch, sl], AF.Square)
            P.emit("t", lambda E, sl=sl, blk=blk: E.matmul(
                P.banks[blk].ap[0:2, :], E2, sqb.ap[:, sl], start=True, stop=True),
                reads=[sqb, cbf], writes=[P.banks[blk]])
        for blk in range(4):
            sl = slice(blk * 512, (blk + 1) * 512)
            P.emit("v", lambda E, sl=sl, blk=blk: E.tensor_scalar(
                rows2h.ap[0:2, sl], P.banks[blk].ap[0:2, :], 1.0 / 64, 1e-5,
                ALU.mult, ALU.add), reads=[P.banks[blk]], writes=[rows2h])
        P.emit("s", lambda E: E.activation(rows2h.ap[0:2, :], rows2h.ap[0:2, :], AF.Sqrt),
               reads=[rows2h], writes=[rows2h])
        P.emit("v", lambda E: E.reciprocal(rows2h.ap[0:2, :], rows2h.ap[0:2, :]),
               reads=[rows2h], writes=[rows2h])
        for blk in range(4):
            sl = slice(blk * 512, (blk + 1) * 512)
            bank = P.banks[4 + blk % 2]
            P.emit("t", lambda E, sl=sl, bank=bank: E.matmul(
                bank.ap, B2.ap, rows2h.ap[0:2, sl], start=True, stop=True),
                reads=[rows2h, B2], writes=[bank])
            P.emit("v", lambda E, ch=ch, sl=sl, bank=bank: E.tensor_tensor(
                tmul.ap, qkv.ap[:, ch, sl], bank.ap, ALU.mult),
                reads=[qkv, bank], writes=[tmul])
            P.emit("v", lambda E, ch=ch, sl=sl: E.tensor_scalar(
                qkv.ap[:, ch, sl], tmul.ap, vec["snw"].ap, 0.8, ALU.mult, ALU.mult),
                reads=[tmul, vec["snw"]], writes=[qkv])
    P.free(sqb, rows2h, tmul)
    if dump == "osn":
        return dump_fm(qkv, 32)

    wslot = P.alloc(NCH * D, BF16, "wslot2")
    wb = load_weight(P, wproj_d, NCH, D, "wproj", slot=wslot)
    xst = P.alloc(T, F32, "xst")
    gtmp = P.alloc(T, BF16, "gtmp")
    yst = P.alloc(T, F32, "yst")
    st1 = P.alloc(512, F32, "st1")
    state = {"dc": -1}

    def evac_upd(dc, blk, bank):
        if state["dc"] != dc:
            state["dc"] = dc
            P.dma("y", xst.ap, x_d[dc * 128:(dc + 1) * 128, :], writes=[xst])
            P.dma("y", gtmp.ap, g_d[dc * 128:(dc + 1) * 128, :], writes=[gtmp])
        sl = slice(blk * 512, (blk + 1) * 512)
        P.emit("v", lambda E, dc=dc, sl=sl, bank=bank: E.tensor_scalar(
            st1.ap, bank.ap, vec["b_proj"].ap[:, dc:dc + 1], None, ALU.add, ALU.bypass),
            reads=[bank, vec["b_proj"]], writes=[st1])
        P.emit("v", lambda E, sl=sl: E.tensor_tensor(st1.ap, st1.ap, gtmp.ap[:, sl],
               ALU.mult), reads=[st1, gtmp], writes=[st1])
        P.emit("v", lambda E, sl=sl: E.tensor_tensor(yst.ap[:, sl], st1.ap, xst.ap[:, sl],
               ALU.add), reads=[st1, xst], writes=[yst])
        if blk == 3:
            P.dma("y", y_d[dc * 128:(dc + 1) * 128, :], yst.ap, reads=[yst])
    proj(P, wb, lambda c, blk: qkv.ap[:, 24 + c, blk * 512:(blk + 1) * 512], [qkv],
         NCH, NCH, evac_upd)
    last_y = ("y", "d", len(P.ops["y"]) - 1)
    P.pop_scope()
    return P, [last_y]


def l2_inputs(core_x, ins, pfx=""):
    wq = np.asarray(ins["L2_attn_w_qkv"], np.float32)
    q_w = wq[0:1024]
    k_w = wq[1024:2048]
    v_w = wq[2048:3072]
    kz1 = np.zeros_like(k_w)
    kz2 = np.zeros_like(k_w)
    for h in range(16):
        kz1[64 * h:64 * h + 32] = k_w[64 * h:64 * h + 32]
        kz2[64 * h + 32:64 * h + 64] = k_w[64 * h + 32:64 * h + 64]
    wpack = np.concatenate([q_w, kz1, kz2, v_w], axis=0)
    rd = {
        "x_in": _f32(core_x),
        "wqkvT4": _f32(wpack.T),
        "wprojT": _f32(ins["L2_attn_w_proj"].T),
        "ew1": _f32(ins["L2_ffn_ew1"]),
        "ew2": _f32(ins["L2_ffn_ew2"]),
        "ew3": _f32(ins["L2_ffn_ew3"]),
        "sw1": _f32(ins["L2_ffn_sw1"]),
        "sw2": _f32(ins["L2_ffn_sw2"]),
        "sw3": _f32(ins["L2_ffn_sw3"]),
        "vecs": l2_pack(ins).host,
        "cbf": _bf_consts(),
        "cid32": np.eye(128, dtype=np.float32),
    }
    return {("x_in" if k == "x_in" else pfx + k): v for k, v in rd.items()}





def l0_pack(ins=None):
    vp = VecPack(4096)
    if ins is not None:
        vp.add_cols("lng_c", _col8(ins["L0_attn_ln_g"]))
        vp.add_cols("lnb_c", _col8(ins["L0_attn_ln_b"]))
        vp.add_cols("h1b1", _f32(np.asarray(ins["L0_attn_h1_b1"]).reshape(32, 128).T))
        vp.add_cols("h1b2", _col8(ins["L0_attn_h1_b2"]))
        vp.add_cols("h2b1", _f32(np.asarray(ins["L0_attn_h2_b1"]).reshape(32, 128).T))
        vp.add_cols("h2b2", _col8(ins["L0_attn_h2_b2"]))
        vp.add_cols("fmb1", _f32(np.asarray(ins["L0_attn_fm_b1"]).reshape(32, 128).T))
        vp.add_cols("fmb2", _col8(ins["L0_attn_fm_b2"]))
        vp.add_cols("moeb1", _f32(np.asarray(ins["L0_ffn_b1"]).reshape(8, 32, 128)
                                  .transpose(2, 0, 1).reshape(128, 256)))
        vp.add_row("lng_r", ins["L0_attn_ln_g"])
        vp.add_row("lnb_r", ins["L0_attn_ln_b"])
        vp.add_row("router_b", ins["L0_ffn_router_b"])
        vp.add_cols("router_w", _f32(np.asarray(ins["L0_ffn_router_w"]).reshape(NCH, 128, 8)
                                     .transpose(1, 0, 2).reshape(128, NCH * 8)))
    else:
        vp.add_cols("lng_c", np.zeros((128, 8), np.float32))
        vp.add_cols("lnb_c", np.zeros((128, 8), np.float32))
        for n in ("h1b1", "h1b2", "h2b1", "h2b2", "fmb1", "fmb2"):
            vp.add_cols(n, np.zeros((128, 32 if n.endswith("b1") else 8), np.float32))
        vp.add_cols("moeb1", np.zeros((128, 256), np.float32))
        vp.add_row("lng_r", np.zeros(1024, np.float32))
        vp.add_row("lnb_r", np.zeros(1024, np.float32))
        vp.add_row("router_b", np.zeros(8, np.float32))
        vp.add_cols("router_w", np.zeros((128, NCH * 8), np.float32))
    vp.add_cols("ones_col", np.ones((128, 1), np.float32))
    vp.add_row("ones_row", np.ones(128, np.float32))
    return vp


def _bf_consts_l0():
    c = np.zeros((128, 2304), BF)
    c[:, 0:128] = np.eye(128, dtype=BF)
    c[:, 130] = 1
    zz = np.arange(512)[None, :]
    pp = np.arange(128)[:, None]
    for k in range(4):
        c[:, 256 + 512 * k:256 + 512 * (k + 1)] = (zz >= 128 * k + pp).astype(BF)
    return c


def internal_ln(P, src3, dst3, gcol, bcol, onescol_bf, c_ones_row, gofs=0, eps=1e-5):
    sqb = P.alloc(T, BF16, "iln_sq")
    rows = P.alloc(2 * T, F32, "iln_rows")
    A_bc = P.alloc(T, F32, "iln_Abc")
    B_bc = P.alloc(T, F32, "iln_Bbc")
    t3 = P.alloc(T, F32, "iln_t3")
    t1 = P.alloc(T, F32, "iln_t1")
    bsum = P.banks[:4]
    bsq = P.banks[4:8]
    mean = rows.ap[0:1, 0:T]
    msq = rows.ap[0:1, T:2 * T]
    for c in range(NCH):
        act(P, sqb, sqb.ap, [src3], src3.ap[:, c, gofs:gofs + T], AF.Square)
        for blk in range(4):
            sl = slice(blk * 512, (blk + 1) * 512)
            P.emit("t", lambda E, c=c, sl=sl, blk=blk: E.matmul(
                bsum[blk].ap[0:1, :], onescol_bf, src3.ap[:, c, gofs + sl.start:gofs + sl.stop],
                start=(c == 0), stop=(c == NCH - 1)), reads=[src3], writes=[bsum[blk]])
            P.emit("t", lambda E, c=c, sl=sl, blk=blk: E.matmul(
                bsq[blk].ap[0:1, :], onescol_bf, sqb.ap[:, sl],
                start=(c == 0), stop=(c == NCH - 1)), reads=[sqb], writes=[bsq[blk]])
    for blk in range(4):
        sl = slice(blk * 512, (blk + 1) * 512)
        P.emit("v", lambda E, blk=blk, sl=sl: E.tensor_scalar(
            mean[:, sl], bsum[blk].ap[0:1, :], 1.0 / D, None, ALU.mult, ALU.bypass),
            reads=[bsum[blk]], writes=[rows])
        P.emit("v", lambda E, blk=blk, sl=sl: E.tensor_scalar(
            msq[:, sl], bsq[blk].ap[0:1, :], 1.0 / D, eps, ALU.mult, ALU.add),
            reads=[bsq[blk]], writes=[rows])
    m2row = t3.ap[0:1, 0:T]
    P.emit("v", lambda E: E.tensor_tensor(m2row, mean, mean, ALU.mult), reads=[rows], writes=[t3])
    P.emit("v", lambda E: E.tensor_tensor(msq, msq, m2row, ALU.subtract),
           reads=[rows, t3], writes=[rows])
    P.emit("s", lambda E: E.activation(msq, msq, AF.Sqrt), reads=[rows], writes=[rows])
    P.emit("v", lambda E: E.reciprocal(msq, msq), reads=[rows], writes=[rows])
    P.emit("v", lambda E: E.scalar_tensor_tensor(mean, mean, -1.0, msq, ALU.mult, ALU.mult),
           reads=[rows], writes=[rows])
    for (srow, dst) in ((msq, A_bc), (mean, B_bc)):
        for blk in range(4):
            sl = slice(blk * 512, (blk + 1) * 512)
            bank = P.banks[blk % 2]
            P.emit("t", lambda E, srow=srow, sl=sl, bank=bank: E.matmul(
                bank.ap, c_ones_row.ap, srow[:, sl], start=True, stop=True),
                reads=[rows, c_ones_row], writes=[bank])
            P.emit("v", lambda E, dst=dst, sl=sl, bank=bank: E.tensor_copy(
                dst.ap[:, sl], bank.ap), reads=[bank], writes=[dst])
    for c in range(NCH):
        P.emit("v", lambda E, c=c: E.tensor_scalar(
            t3.ap, B_bc.ap, gcol.ap[:, c:c + 1], bcol.ap[:, c:c + 1], ALU.mult, ALU.add),
            reads=[B_bc, gcol, bcol], writes=[t3])
        P.emit("v", lambda E, c=c: E.tensor_tensor(
            t1.ap, src3.ap[:, c, gofs:gofs + T], A_bc.ap, ALU.mult),
            reads=[src3, A_bc], writes=[t1])
        P.emit("v", lambda E, c=c: E.scalar_tensor_tensor(
            dst3.ap[:, c, :], t1.ap, gcol.ap[:, c:c + 1], t3.ap, ALU.mult, ALU.add),
            reads=[t1, gcol, t3], writes=[dst3])
    P.free(sqb, rows, A_bc, B_bc, t3, t1)


def build_l0(P=None, x_d=None, y_d=None, pfx="", dump=None):
    if P is None:
        P = Prog("l0").make_arena()
        x_d = P.dram("x_in", [D, T])
        y_d = P.dram("y_out", [D, T], kind="ExternalOutput")
    P.push_scope()
    g_d = P.dram(pfx + "g_spill", [D, T], BF16, kind="Internal")
    h1w1_d = P.dram(pfx + "h1w1", [D, 4096])
    h1w2_d = P.dram(pfx + "h1w2", [4096, D])
    h2w1_d = P.dram(pfx + "h2w1", [D, 4096])
    h2w2_d = P.dram(pfx + "h2w2", [4096, D])
    fmw1_d = P.dram(pfx + "fmw1", [D, 4096])
    fmw2_d = P.dram(pfx + "fmw2", [4096, D])
    mw1_d = P.dram(pfx + "mw1", [8, D, 4096])
    mw2_d = P.dram(pfx + "mw2", [8, 4096, D])
    mb2_d = P.dram(pfx + "mb2", [8, D])
    vecs_d = P.dram(pfx + "vecs", [128, 4096])
    cbf_d = P.dram(pfx + "cbf", [128, 2304], BF16)

    vec = l0_pack().load(P, vecs_d)
    c_ones_col = vec["ones_col"]
    c_ones_row = vec["ones_row"]
    cbf = P.alloc(2304, BF16, "cbf")
    P.dma("y", cbf.ap, cbf_d, writes=[cbf])
    ident = cbf.ap[:, 0:128]
    onescol_bf = cbf.ap[:, 130:131]
    masks = [cbf.ap[:, 256 + 512 * k:256 + 512 * (k + 1)] for k in range(4)]
    b2row = P.alloc(D, BF16, "moe_b2row")
    b2f = P.alloc(D, F32, "moe_b2f")

    def dump_fm(sb3, nch=NCH, guard=0):
        outb = P.alloc(T, F32, "dumpout")
        for c in range(nch):
            sl = sb3.ap[:, c, guard:T + guard]
            P.emit("v", lambda E, sl=sl: E.tensor_copy(outb.ap, sl), reads=[sb3], writes=[outb])
            P.dma("y", y_d[(c % NCH) * 128:((c % NCH) + 1) * 128, :], outb.ap, reads=[outb])
        return P, [("y", "d", len(P.ops["y"]) - 1)]

    xn = alloc_xn(P)
    x = P.alloc(NCH * T, F32, "x")
    x.ap = x.ap.rearrange("p (c t) -> p c t", c=NCH)
    P.dma("y", x.ap, x_d.rearrange("(c p) t -> p c t", p=128), writes=[x])
    global_ln(P, x, xn, c_ones_col, c_ones_row)
    P.free(x)

    rwb = P.alloc(NCH * 8, BF16, "router_w_bf")
    rwb.ap = rwb.ap.rearrange("p (c e) -> p c e", c=NCH)
    P.emit("v", lambda E: E.tensor_copy(
        rwb.ap, vec["router_w"].ap.rearrange("p (c e) -> p c e", c=NCH)),
        reads=[vec["router_w"]], writes=[rwb])
    rbrow_bf = P.alloc(8, BF16, "router_b_bf")
    P.emit("v", lambda E: E.tensor_copy(rbrow_bf.ap[0:1, :], vec["router_b"].ap),
           reads=[vec["router_b"]], writes=[rbrow_bf])
    obf = P.alloc(512, BF16, "ones_row_bf")
    P.emit("v", lambda E: E.memset(obf.ap[0:1, :], 1.0), writes=[obf])
    logits = P.alloc(16 * 8, F32, "logits")
    logits.ap = logits.ap.rearrange("p (n e) -> p n e", n=16)
    for tt in range(16):
        bank = P.banks[tt % 2]
        for c in range(NCH):
            P.emit("t", lambda E, tt=tt, c=c, bank=bank: E.matmul(
                bank.ap[:, 0:8], xn.ap[:, c, 1 + tt * 128:1 + (tt + 1) * 128], rwb.ap[:, c, :],
                start=(c == 0), stop=False), reads=[xn, rwb], writes=[bank])
        P.emit("t", lambda E, tt=tt, bank=bank: E.matmul(
            bank.ap[:, 0:8], obf.ap[0:1, 0:128], rbrow_bf.ap[0:1, :], start=False, stop=True),
            reads=[obf, rbrow_bf], writes=[bank])
        P.emit("v", lambda E, tt=tt, bank=bank: E.tensor_copy(logits.ap[:, tt, :],
               bank.ap[:, 0:8]), reads=[bank], writes=[logits])
    r_m1 = P.alloc(16, F32, "r_m1")
    r_m2 = P.alloc(16, F32, "r_m2")
    r_g1 = P.alloc(16, F32, "r_g1")
    r_g2 = P.alloc(16, F32, "r_g2")
    msk1 = P.alloc(16 * 8, F32, "msk1")
    msk1.ap = msk1.ap.rearrange("p (n e) -> p n e", n=16)
    msk2 = P.alloc(16 * 8, F32, "msk2")
    msk2.ap = msk2.ap.rearrange("p (n e) -> p n e", n=16)
    lg2 = P.alloc(16 * 8, F32, "lg2")
    lg2.ap = lg2.ap.rearrange("p (n e) -> p n e", n=16)
    gates = P.alloc(16 * 8, F32, "gates")
    gates.ap = gates.ap.rearrange("p (n e) -> p n e", n=16)
    P.emit("v", lambda E: E.tensor_reduce(r_m1.ap, logits.ap, axis=AX.X, op=ALU.max),
           reads=[logits], writes=[r_m1])
    P.emit("v", lambda E: E.tensor_tensor(msk1.ap, logits.ap,
           r_m1.ap.unsqueeze(2).broadcast_to([128, 16, 8]), ALU.is_equal),
           reads=[logits, r_m1], writes=[msk1])
    P.emit("v", lambda E: E.scalar_tensor_tensor(lg2.ap, msk1.ap, -1e30, logits.ap,
           ALU.mult, ALU.add), reads=[msk1, logits], writes=[lg2])
    P.emit("v", lambda E: E.tensor_reduce(r_m2.ap, lg2.ap, axis=AX.X, op=ALU.max),
           reads=[lg2], writes=[r_m2])
    P.emit("v", lambda E: E.tensor_tensor(msk2.ap, lg2.ap,
           r_m2.ap.unsqueeze(2).broadcast_to([128, 16, 8]), ALU.is_equal),
           reads=[lg2, r_m2], writes=[msk2])
    P.emit("v", lambda E: E.tensor_tensor(r_g1.ap, r_m1.ap, r_m2.ap, ALU.subtract),
           reads=[r_m1, r_m2], writes=[r_g1])
    act(P, r_g1, r_g1.ap, [r_g1], r_g1.ap, AF.Sigmoid)
    P.emit("v", lambda E: E.tensor_scalar(r_g2.ap, r_g1.ap, -1.0, 1.0, ALU.mult, ALU.add),
           reads=[r_g1], writes=[r_g2])
    P.emit("v", lambda E: E.tensor_tensor(gates.ap, msk1.ap,
           r_g1.ap.unsqueeze(2).broadcast_to([128, 16, 8]), ALU.mult),
           reads=[msk1, r_g1], writes=[gates])
    P.emit("v", lambda E: E.tensor_tensor(msk2.ap, msk2.ap,
           r_g2.ap.unsqueeze(2).broadcast_to([128, 16, 8]), ALU.mult),
           reads=[msk2, r_g2], writes=[msk2])
    P.emit("v", lambda E: E.tensor_tensor(gates.ap, gates.ap, msk2.ap, ALU.add),
           reads=[gates, msk2], writes=[gates])
    P.free(rwb, rbrow_bf, logits, r_m1, r_m2, r_g1, r_g2, msk1, msk2, lg2)
    cid = P.alloc(128, F32, "cid32_dyn")
    P.emit("v", lambda E: E.tensor_copy(cid.ap, ident), reads=[cbf], writes=[cid])

    yacc = P.alloc(NCH * T, BF16, "yacc")
    yacc.ap = yacc.ap.rearrange("p (c t) -> p c t", c=NCH)
    HB = T // 2
    hbuf = P.alloc(32 * HB, BF16, "hbuf")
    hbuf.ap = hbuf.ap.rearrange("p (c t) -> p c t", c=32)
    stg = [P.alloc(512, F32, "l0_stg0"), P.alloc(512, F32, "l0_stg1")]
    grow = P.alloc(T, F32, "gate_row")
    gbc = P.alloc(T, F32, "l0_gbc")
    t1 = P.alloc(512, F32, "l0_t1")
    wslA = WSlots(P, NCH * 512, "wslA")
    wslB = wslA

    for e in range(8):
        for tt in range(16):
            bank = P.banks[4 + tt % 2]
            P.emit("t", lambda E, tt=tt, e=e, bank=bank: E.transpose(
                bank.ap[0:1, 0:128], gates.ap[:, tt, e:e + 1], cid.ap),
                reads=[gates, cid], writes=[bank])
            P.emit("v", lambda E, tt=tt, bank=bank: E.tensor_copy(
                grow.ap[0:1, tt * 128:(tt + 1) * 128], bank.ap[0:1, 0:128]),
                reads=[bank], writes=[grow])
        P.dma("y", b2f.ap[0:1, :], mb2_d[e:e + 1, :], writes=[b2f])
        P.emit("v", lambda E: E.tensor_copy(b2row.ap[0:1, :], b2f.ap[0:1, :]),
               reads=[b2f], writes=[b2row])
        for blk in range(4):
            sl = slice(blk * 512, (blk + 1) * 512)
            bank = P.banks[4 + blk % 2]
            P.emit("t", lambda E, sl=sl, bank=bank: E.matmul(
                bank.ap, c_ones_row.ap, grow.ap[0:1, sl], start=True, stop=True),
                reads=[grow, c_ones_row], writes=[bank])
            P.emit("v", lambda E, sl=sl, bank=bank: E.tensor_copy(gbc.ap[:, sl], bank.ap),
                   reads=[bank], writes=[gbc])
        for half in range(2):
            t0 = half * HB
            for ds in range(8):
                wA = load_weight(P, mw1_d[e], NCH, 512, "mw1", slot=wslA(), stage=stg,
                                 col0=ds * 512)
                for dsub in range(4):
                    hc = ds * 4 + dsub
                    for blk in range(HB // 512):
                        sl = slice(t0 + blk * 512, t0 + (blk + 1) * 512)
                        bank = P.banks[(hc * 2 + blk) % 6]
                        for c in range(NCH):
                            P.emit("t", lambda E, c=c, dsub=dsub, sl=sl, bank=bank, wA=wA:
                                   E.matmul(bank.ap, wA.ap[:, c, dsub * 128:(dsub + 1) * 128],
                                            xn.ap[:, c, 1 + sl.start:1 + sl.stop],
                                            start=(c == 0), stop=(c == NCH - 1)),
                                   reads=[wA, xn], writes=[bank])
                        act(P, hbuf, hbuf.ap[:, hc, blk * 512:(blk + 1) * 512], [bank],
                            bank.ap, AF.Gelu, bias=vec["moeb1"].ap[:, e * 32 + hc:e * 32 + hc + 1])
            for ds3 in range(8):
                wC = load_weight(P, mw2_d[e], 32, 128, "mw2", slot=wslB(), stage=stg,
                                 col0=ds3 * 128)
                dc = ds3
                for blk in range(HB // 512):
                    sl = slice(t0 + blk * 512, t0 + (blk + 1) * 512)
                    bank = P.banks[6 + blk % 2]
                    for c in range(32):
                        P.emit("t", lambda E, c=c, sl=sl, t0=t0, bank=bank, wC=wC,
                               hbuf=hbuf: E.matmul(
                            bank.ap, wC.ap[:, c, 0:128],
                            hbuf.ap[:, c, sl.start - t0:sl.stop - t0],
                            start=(c == 0), stop=False), reads=[wC, hbuf], writes=[bank])
                    P.emit("t", lambda E, dc=dc, bank=bank: E.matmul(
                        bank.ap, b2row.ap[0:1, dc * 128:(dc + 1) * 128],
                        obf.ap[0:1, 0:512], start=False, stop=True),
                        reads=[b2row, obf], writes=[bank])
                    if e == 0:
                        P.emit("v", lambda E, dc=dc, sl=sl, bank=bank: E.tensor_tensor(
                            yacc.ap[:, dc, sl], bank.ap, gbc.ap[:, sl], ALU.mult),
                            reads=[bank, gbc], writes=[yacc])
                    else:
                        P.emit("v", lambda E, sl=sl, bank=bank: E.tensor_tensor(
                            t1.ap, bank.ap, gbc.ap[:, sl], ALU.mult),
                            reads=[bank, gbc], writes=[t1])
                        P.emit("v", lambda E, dc=dc, sl=sl: E.tensor_tensor(
                            yacc.ap[:, dc, sl], yacc.ap[:, dc, sl], t1.ap, ALU.add),
                            reads=[yacc, t1], writes=[yacc])
    P.free(*wslA.bufs, stg[0], stg[1], grow, gbc, t1)
    gstage = P.alloc(T, BF16, "gstage")
    for c in range(NCH):
        act(P, gstage, gstage.ap, [yacc], yacc.ap[:, c, :], AF.Silu)
        P.dma("y", g_d[c * 128:(c + 1) * 128, :], gstage.ap, reads=[gstage])
    P.free(yacc, hbuf, gstage, gates, cid, b2row, b2f)
    if dump == "gdone":
        tmpb = P.alloc(T, F32, "tmpb")
        gtmp = P.alloc(T, BF16, "gtmp")
        for c in range(NCH):
            P.dma("y", gtmp.ap, g_d[c * 128:(c + 1) * 128, :], writes=[gtmp])
            P.emit("v", lambda E: E.tensor_copy(tmpb.ap, gtmp.ap), reads=[gtmp], writes=[tmpb])
            P.dma("y", y_d[c * 128:(c + 1) * 128, :], tmpb.ap, reads=[tmpb])
        return P, [("y", "d", len(P.ops["y"]) - 1)]

    n3 = P.alloc(NCH * T, BF16, "n3")
    n3.ap = n3.ap.rearrange("p (c t) -> p c t", c=NCH)
    internal_ln(P, xn, n3, vec["lng_c"], vec["lnb_c"], onescol_bf, c_ones_row, gofs=1)
    if dump == "n3":
        return dump_fm(n3)

    HB = T // 2
    W12 = []
    for outname in ("W1", "W2"):
        Wx = P.alloc(NCH * T, BF16, outname)
        Wx.ap = Wx.ap.rearrange("p (c t) -> p c t", c=NCH)
        W12.append(Wx)
    stg = [P.alloc(512, F32, "l0c_stg0")]
    hbuf = P.alloc(32 * (T // 4), BF16, "hbuf2")
    hbuf.ap = hbuf.ap.rearrange("p (c t) -> p c t", c=32)
    wslA = WSlots(P, NCH * 512, "wslA2")
    wslB = wslA
    for mi, (w1d, w2d, b1n, b2n, outname, HBx) in enumerate((
            (h1w1_d, h1w2_d, "h1b1", "h1b2", "W1", T // 4),
            (h2w1_d, h2w2_d, "h2b1", "h2b2", "W2", T // 4))):
        Wx = W12[mi]
        for half in range(T // HBx):
            t0 = half * HBx
            for ds in range(8):
                wA = load_weight(P, w1d, NCH, 512, "w1", slot=wslA(), stage=stg, col0=ds * 512)
                for dsub in range(4):
                    hc = ds * 4 + dsub
                    for blk in range(HBx // 512):
                        sl = slice(t0 + blk * 512, t0 + (blk + 1) * 512)
                        bank = P.banks[(hc * 2 + blk) % 6]
                        for c in range(NCH):
                            P.emit("t", lambda E, c=c, dsub=dsub, sl=sl, bank=bank, wA=wA:
                                   E.matmul(bank.ap, wA.ap[:, c, dsub * 128:(dsub + 1) * 128],
                                            n3.ap[:, c, sl], start=(c == 0),
                                            stop=(c == NCH - 1)), reads=[wA, n3], writes=[bank])
                        act(P, hbuf, hbuf.ap[:, hc, blk * 512:(blk + 1) * 512], [bank],
                            bank.ap, AF.Gelu, bias=vec[b1n].ap[:, hc:hc + 1])
            for ds3 in range(8):
                wC = load_weight(P, w2d, 32, 128, "w2", slot=wslB(), stage=stg, col0=ds3 * 128)
                dc = ds3
                for blk in range(HBx // 512):
                    sl = slice(t0 + blk * 512, t0 + (blk + 1) * 512)
                    bank = P.banks[6 + blk % 2]
                    for c in range(32):
                        P.emit("t", lambda E, c=c, sl=sl, t0=t0, bank=bank, wC=wC,
                               hbuf=hbuf: E.matmul(
                            bank.ap, wC.ap[:, c, 0:128],
                            hbuf.ap[:, c, sl.start - t0:sl.stop - t0],
                            start=(c == 0), stop=(c == 31)), reads=[wC, hbuf], writes=[bank])
                    P.emit("v", lambda E, dc=dc, sl=sl, bank=bank, b2n=b2n, Wx=Wx:
                           E.tensor_scalar(
                        Wx.ap[:, dc, sl], bank.ap, vec[b2n].ap[:, dc:dc + 1], None,
                        ALU.add, ALU.bypass), reads=[bank, vec[b2n]], writes=[Wx])
        if dump == "n3b" and outname == "W1":
            return dump_fm(n3)
    W1b, W2b = W12
    P.free(hbuf, *wslA.bufs, stg[0])
    if dump == "n3c":
        return dump_fm(n3)
    if dump == "w12":
        return dump_fm(W1b)

    offs = []
    off = 0
    for i in range(16):
        offs.append(off)
        off += T - 512 * (i // 4)
    w3t = P.alloc(off, BF16, "w3t")
    for i in range(16):
        z0 = 512 * (i // 4)
        for zb in range(z0, T, 512):
            bank = P.banks[(zb // 512) % 4]
            for c in range(NCH):
                P.emit("t", lambda E, i=i, zb=zb, c=c, bank=bank: E.matmul(
                    bank.ap, W2b.ap[:, c, i * 128:(i + 1) * 128],
                    W1b.ap[:, c, zb:zb + 512], start=(c == 0), stop=(c == NCH - 1)),
                    reads=[W1b, W2b], writes=[bank])
            dsl = slice(offs[i] + (zb - z0), offs[i] + (zb - z0) + 512)
            if zb == z0:
                P.emit("v", lambda E, i=i, dsl=dsl, bank=bank: E.tensor_tensor(
                    w3t.ap[:, dsl], bank.ap, masks[i % 4], ALU.mult),
                    reads=[bank, cbf], writes=[w3t])
            else:
                P.emit("v", lambda E, dsl=dsl, bank=bank: E.tensor_copy(
                    w3t.ap[:, dsl], bank.ap), reads=[bank], writes=[w3t])
    P.free(W1b, W2b)

    ntm = P.alloc(16 * D, BF16, "ntm")
    ntm.ap = ntm.ap.rearrange("p (k m) -> p k m", k=16)
    for tt in range(16):
        bank = P.banks[tt % 2]
        bankb = bank.ap.bitcast(BF16)
        for c in range(NCH):
            P.emit("t", lambda E, tt=tt, c=c, bankb=bankb: E.transpose(
                bankb[:, c * 128:(c + 1) * 128], n3.ap[:, c, tt * 128:(tt + 1) * 128],
                ident), reads=[n3, cbf], writes=[bank])
        P.emit("v", lambda E, tt=tt, bankb=bankb: E.tensor_copy(
            ntm.ap[:, tt, :], bankb[:, 0:1024]), reads=[bank], writes=[ntm])
    P.free(n3)

    x1 = P.alloc(NCH * T, BF16, "x1")
    x1.ap = x1.ap.rearrange("p (c t) -> p c t", c=NCH)
    ttile = P.alloc(D, F32, "ttile")
    tst = P.alloc(8, F32, "t_stats")
    gb_bc = P.alloc(D, F32, "t_gbc")
    bb_bc = P.alloc(D, F32, "t_bbc")
    t2b = P.alloc(D, BF16, "t2b")
    for blk in range(2):
        sl = slice(blk * 512, (blk + 1) * 512)
        bank = P.banks[6]
        P.emit("t", lambda E, sl=sl, bank=bank: E.matmul(
            bank.ap, c_ones_row.ap, vec["lng_r"].ap[:, sl], start=True, stop=True),
            reads=[vec["lng_r"], c_ones_row], writes=[bank])
        P.emit("v", lambda E, sl=sl, bank=bank: E.tensor_copy(gb_bc.ap[:, sl], bank.ap),
               reads=[bank], writes=[gb_bc])
        P.emit("t", lambda E, sl=sl, bank=bank: E.matmul(
            bank.ap, c_ones_row.ap, vec["lnb_r"].ap[:, sl], start=True, stop=True),
            reads=[vec["lnb_r"], c_ones_row], writes=[bank])
        P.emit("v", lambda E, sl=sl, bank=bank: E.tensor_copy(bb_bc.ap[:, sl], bank.ap),
               reads=[bank], writes=[bb_bc])
    for zt in range(16):
        for mb in range(2):
            msl = slice(mb * 512, (mb + 1) * 512)
            bank = P.banks[mb]
            first = True
            for i in range(zt + 1):
                z0 = 512 * (i // 4)
                lsl = slice(offs[i] + zt * 128 - z0, offs[i] + zt * 128 - z0 + 128)
                P.emit("t", lambda E, i=i, zt=zt, lsl=lsl, msl=msl, bank=bank,
                       first=first: E.matmul(bank.ap, w3t.ap[:, lsl], ntm.ap[:, i, msl],
                                             start=first, stop=(i == zt)),
                       reads=[w3t, ntm], writes=[bank])
                first = False
            act(P, ttile, ttile.ap[:, msl], [bank], bank.ap, AF.Gelu)
        P.emit("v", lambda E: E.tensor_reduce(tst.ap[:, 0:1], ttile.ap, axis=AX.X,
               op=ALU.add), reads=[ttile], writes=[tst])
        act(P, t2b, t2b.ap, [ttile], ttile.ap, AF.Square,
            accum_ap=tst.ap[:, 1:2], accum_buf=tst)
        P.emit("v", lambda E: E.tensor_scalar(tst.ap[:, 0:1], tst.ap[:, 0:1], 1.0 / D,
               None, ALU.mult, ALU.bypass), reads=[tst], writes=[tst])
        P.emit("v", lambda E: E.tensor_scalar(tst.ap[:, 1:2], tst.ap[:, 1:2], 1.0 / D,
               1e-5, ALU.mult, ALU.add), reads=[tst], writes=[tst])
        P.emit("v", lambda E: E.tensor_tensor(tst.ap[:, 2:3], tst.ap[:, 0:1],
               tst.ap[:, 0:1], ALU.mult), reads=[tst], writes=[tst])
        P.emit("v", lambda E: E.tensor_tensor(tst.ap[:, 1:2], tst.ap[:, 1:2],
               tst.ap[:, 2:3], ALU.subtract), reads=[tst], writes=[tst])
        P.emit("s", lambda E: E.activation(tst.ap[:, 1:2], tst.ap[:, 1:2], AF.Sqrt),
               reads=[tst], writes=[tst])
        P.emit("v", lambda E: E.reciprocal(tst.ap[:, 1:2], tst.ap[:, 1:2]),
               reads=[tst], writes=[tst])
        P.emit("v", lambda E: E.tensor_scalar(ttile.ap, ttile.ap, tst.ap[:, 0:1],
               tst.ap[:, 1:2], ALU.subtract, ALU.mult), reads=[ttile, tst], writes=[ttile])
        P.emit("v", lambda E: E.tensor_tensor(ttile.ap, ttile.ap, gb_bc.ap, ALU.mult),
               reads=[ttile, gb_bc], writes=[ttile])
        P.emit("v", lambda E: E.tensor_tensor(t2b.ap, ttile.ap, bb_bc.ap, ALU.add),
               reads=[ttile, bb_bc], writes=[t2b])
        for c in range(NCH):
            bank = P.banks[2 + c % 2]
            bankb = bank.ap.bitcast(BF16)
            P.emit("t", lambda E, c=c, bankb=bankb: E.transpose(
                bankb[:, 0:128], t2b.ap[:, c * 128:(c + 1) * 128], ident),
                reads=[t2b, cbf], writes=[bank])
            P.emit("v", lambda E, c=c, zt=zt, bankb=bankb: E.tensor_tensor(
                x1.ap[:, c, zt * 128:(zt + 1) * 128], xn.ap[:, c, 1 + zt * 128:1 + (zt + 1) * 128],
                bankb[:, 0:128], ALU.add), reads=[xn, bank], writes=[x1])
    P.free(w3t, ntm, ttile, tst, gb_bc, bb_bc, t2b)
    if dump == "x1":
        return dump_fm(x1)

    internal_ln(P, x1, x1, vec["lng_c"], vec["lnb_c"], onescol_bf, c_ones_row, gofs=0)
    hbuf = P.alloc(32 * HB, BF16, "hbuf3")
    hbuf.ap = hbuf.ap.rearrange("p (c t) -> p c t", c=32)
    wslA = WSlots(P, NCH * 512, "wslA3")
    wslB = wslA
    stg = [P.alloc(512, F32, "l0f_stg0"), P.alloc(512, F32, "l0f_stg1")]
    xst = P.alloc(HB, F32, "xst")
    gst = P.alloc(HB, BF16, "gst")
    yst = P.alloc(HB, F32, "yst")
    u1 = P.alloc(512, F32, "l0f_t1")
    for half in range(2):
        t0 = half * HB
        for ds in range(8):
            wA = load_weight(P, fmw1_d, NCH, 512, "fmw1", slot=wslA(), stage=stg, col0=ds * 512)
            for dsub in range(4):
                hc = ds * 4 + dsub
                for blk in range(HB // 512):
                    sl = slice(t0 + blk * 512, t0 + (blk + 1) * 512)
                    bank = P.banks[(hc * 2 + blk) % 6]
                    for c in range(NCH):
                        P.emit("t", lambda E, c=c, dsub=dsub, sl=sl, bank=bank, wA=wA:
                               E.matmul(bank.ap, wA.ap[:, c, dsub * 128:(dsub + 1) * 128],
                                        x1.ap[:, c, sl], start=(c == 0),
                                        stop=(c == NCH - 1)), reads=[wA, x1], writes=[bank])
                    act(P, hbuf, hbuf.ap[:, hc, blk * 512:(blk + 1) * 512], [bank],
                        bank.ap, AF.Gelu, bias=vec["fmb1"].ap[:, hc:hc + 1])
        for ds3 in range(8):
            wC = load_weight(P, fmw2_d, 32, 128, "fmw2", slot=wslB(), stage=stg, col0=ds3 * 128)
            dc = ds3
            P.dma("y", xst.ap, x_d[dc * 128:(dc + 1) * 128, t0:t0 + HB], writes=[xst])
            P.dma("y", gst.ap, g_d[dc * 128:(dc + 1) * 128, t0:t0 + HB], writes=[gst])
            for blk in range(HB // 512):
                sl = slice(t0 + blk * 512, t0 + (blk + 1) * 512)
                usl = slice(blk * 512, (blk + 1) * 512)
                bank = P.banks[4 + blk % 2]
                for c in range(32):
                    P.emit("t", lambda E, c=c, usl=usl, bank=bank, wC=wC, hbuf=hbuf:
                           E.matmul(
                        bank.ap, wC.ap[:, c, 0:128], hbuf.ap[:, c, usl.start:usl.stop],
                        start=(c == 0), stop=(c == 31)), reads=[wC, hbuf], writes=[bank])
                P.emit("v", lambda E, dc=dc, bank=bank: E.tensor_scalar(
                    u1.ap, bank.ap, vec["fmb2"].ap[:, dc:dc + 1], None, ALU.add,
                    ALU.bypass), reads=[bank, vec["fmb2"]], writes=[u1])
                P.emit("v", lambda E, dc=dc, sl=sl: E.tensor_tensor(
                    u1.ap, u1.ap, xn.ap[:, dc, 1 + sl.start:1 + sl.stop], ALU.add),
                    reads=[u1, xn], writes=[u1])
                P.emit("v", lambda E, usl=usl: E.tensor_tensor(
                    u1.ap, u1.ap, gst.ap[:, usl], ALU.mult), reads=[u1, gst], writes=[u1])
                P.emit("v", lambda E, usl=usl: E.tensor_tensor(
                    yst.ap[:, usl], u1.ap, xst.ap[:, usl], ALU.add),
                    reads=[u1, xst], writes=[yst])
            P.dma("y", y_d[dc * 128:(dc + 1) * 128, t0:t0 + HB], yst.ap, reads=[yst])
    last_y = ("y", "d", len(P.ops["y"]) - 1)
    P.pop_scope()
    return P, [last_y]


def l0_inputs(core_x, ins, pfx=""):
    rd = {
        "x_in": _f32(core_x),
        "h1w1": _f32(ins["L0_attn_h1_w1"]),
        "h1w2": _f32(ins["L0_attn_h1_w2"]),
        "h2w1": _f32(ins["L0_attn_h2_w1"]),
        "h2w2": _f32(ins["L0_attn_h2_w2"]),
        "fmw1": _f32(ins["L0_attn_fm_w1"]),
        "fmw2": _f32(ins["L0_attn_fm_w2"]),
        "mw1": _f32(ins["L0_ffn_w1"]),
        "mw2": _f32(ins["L0_ffn_w2"]),
        "mb2": _f32(ins["L0_ffn_b2"]),
        "vecs": l0_pack(ins).host,
        "cbf": _bf_consts_l0(),
    }
    return {("x_in" if k == "x_in" else pfx + k): v for k, v in rd.items()}





def _f32(a):
    return np.ascontiguousarray(a, dtype=np.float32)


def _col8(v):
    return _f32(np.asarray(v).reshape(NCH, 128).T)


def l1_inputs(core_x, ins, pfx=""):
    rd = {
        "x_in": _f32(core_x),
        "wkT": _f32(ins["L1_attn_Wk"].T),
        "wvT": _f32(ins["L1_attn_Wv"].T),
        "wrT": _f32(ins["L1_attn_Wr"].T),
        "woT": _f32(ins["L1_attn_Wo"].T),
        "frT": _f32(ins["L1_ffn_W_r"].T),
        "fkT": _f32(ins["L1_ffn_W_k"].T),
        "fvT": _f32(ins["L1_ffn_W_v"].T),
        "vecs": l1_pack(ins).host,
    }
    return {("x_in" if k == "x_in" else pfx + k): v for k, v in rd.items()}



def run_layer(build_fn, input_fn, x_bct, ins):
    P, finals = build_fn()
    nc = P.build(finals)
    in_maps = [input_fn(x_bct[i], ins) for i in range(NCORES)]
    res = run_bass_kernel_spmd(nc, in_maps, core_ids=list(range(NCORES)))
    return np.stack([r["y_out"] for r in res.results])


def build_fused():
    P = Prog("mk").make_arena()
    x_d = P.dram("x_in", [D, T])
    xb1 = P.dram("xb1", [D, T], kind="Internal")
    xb2 = P.dram("xb2", [D, T], kind="Internal")
    xb3 = P.dram("xb3", [D, T], kind="Internal")
    y_d = P.dram("y_out", [D, T], kind="ExternalOutput")
    build_l0(P, x_d, xb1, pfx="l0_")
    build_l1(P, xb1, xb2, pfx="l1_")
    build_l2(P, xb2, xb3, pfx="l2_")
    build_l3(P, xb3, y_d, pfx="l3_")
    last_y = ("y", "d", len(P.ops["y"]) - 1)
    return P, [last_y]


def fused_inputs(core_x, ins):
    m = {"x_in": _f32(core_x)}
    m.update(l0_inputs(core_x, ins, "l0_"))
    m.update(l1_inputs(core_x, ins, "l1_"))
    m.update(l2_inputs(core_x, ins, "l2_"))
    m.update(l3_inputs(core_x, ins, "l3_"))
    return m


def kernel(**ins):
    x = ins["L0_attn_x"]
    x_bct = np.ascontiguousarray(x.transpose(0, 2, 1)).astype(np.float32, copy=False)
    P, finals = build_fused()
    nc = P.build(finals)
    in_maps = [fused_inputs(x_bct[i], ins) for i in range(NCORES)]
    res = run_bass_kernel_spmd(nc, in_maps, core_ids=list(range(NCORES)))
    y = np.stack([r["y_out"] for r in res.results])
    return np.ascontiguousarray(y.transpose(0, 2, 1))
